# version 05
# baseline (speedup 1.0000x reference)
import numpy as np
from contextlib import ExitStack
import concourse.bass as bass
import concourse.mybir as mybir
from concourse.bass_utils import run_bass_kernel_spmd
from concourse.bass import IndirectOffsetOnAxis

F32 = mybir.dt.float32
BF16 = mybir.dt.bfloat16
I32 = mybir.dt.int32
AF = mybir.ActivationFunctionType
ALU = mybir.AluOpType
AX = mybir.AxisListType

T = 4096
D = 1024
C = 2048
NS = 16
NTT = T // 128
NDC = D // 128
NCC = C // 128
NEGF = float(np.float32(-3.4028235e38))
PI = float(np.float32(np.pi))

C_INVD, C_EPS, C_NEGHALF, C_NEG1, C_CLIP, C_INV003, C_SQD, C_HALF, \
    C_ONE, C_INV4096, C_K1024, C_BIG, C_LOGK, C_INV2PI, C_N2PI, C_HPI, \
    C_MAGIC, C_NMAGIC, C_ZERO, C_QSC, C_BIGN, C_E12 = range(22)
CONST_COLS = [1.0 / D, 1e-5, -0.5, -1.0, 1e-5, float(np.float32(1.0 / 0.03)), 32.0, 0.5,
              1.0, 1.0 / 4096.0, 1024.0, 8192.0, float(np.log(1152.0)), float(np.float32(1.0 / (2 * np.pi))),
              float(np.float32(-2 * np.pi)), float(np.float32(np.pi / 2)),
              8388608.0, -8388608.0, 0.0, 0.125, -8192.0, 1e-12]
CURS = []
_c = 4.0
for _ in range(20):
    CURS.append(_c)
    _c = max(_c * 0.7, 0.03)

SCRATCH = {
    "xnf":    ([128, NDC, T], BF16),
    "x1":     ([T, D], F32),
    "x1a":    ([T, D], F32),
    "xn1tm":  ([T, D], F32),
    "xn1f32": ([128, NDC, T], F32),
    "xn1f":   ([128, NDC, T], BF16),
    "xnlf":   ([128, NDC, T], BF16),
    "qkvsc":  ([128, 12, T], BF16),
    "vtm":    ([T, 512], BF16),
    "lighto": ([T, D], F32),
    "hscat":  ([T, D], F32),
    "scoreq": ([128, 64], F32),
    "selq":   ([1024, 1], I32),
    "selkv":  ([1024, 1], I32),
    "bwrow":  ([1, 1536], F32),
    "scorerow": ([2, 4096], F32),
    "cfrow":  ([8, 4096], BF16),
    "rankrow": ([2, 4096], I32),
    "shsc":   ([128, 36, T], BF16),
    "out":    ([T, D], F32),
    **{f"xs_{d}":  ([128, NCC, T], BF16) for d in "fb"},
    **{f"zg_{d}":  ([128, NCC, T], BF16) for d in "fb"},
    **{f"u_{d}":   ([128, NCC, T], BF16) for d in "fb"},
    **{f"dl_{d}":  ([128, NCC, T], F32)  for d in "fb"},
    **{f"du_{d}":  ([128, NCC, T], BF16) for d in "fb"},
    **{f"bc_{d}":  ([32, T], BF16)       for d in "fb"},
    **{f"y_{d}":   ([128, NCC, T], F32)  for d in "fb"},
    **{f"winT_{d}": ([128, NDC, 2 * C], BF16) for d in "fb"},
    **{f"wxpT_{d}": ([128, NCC, 96], BF16) for d in "fb"},
    **{f"wdtT_{d}": ([64, 1, C], BF16) for d in "fb"},
    **{f"wtlT_{d}": ([128, NCC, D], BF16) for d in "fb"},
    "weff1":  ([4096, D], BF16),
    "weff2":  ([D, 4096], BF16),
    "w1T":    ([128, NDC, 4096], BF16),
    "w2T":    ([128, 32, D], BF16),
    "fusT":   ([128, 16, D], BF16),
    "vTT":    ([128, NDC, 16], BF16),
    "uTT":    ([16, 1, 4096], BF16),
    "fbT":    ([16, 1, 16], BF16),
    "r2l0":   ([128, 32, T], BF16),
    "rWTf":   ([128, NDC, 8], F32),
    "kvT":    ([128, NDC, 1024], BF16),
}


class Ctx:
    PHYS_MAX = 100

    def __init__(self, nc, es, kinds=None):
        self.nc = nc
        self.es = es
        self.sb = {}
        self.aps = {}
        self.barrier_target = 0
        self.kinds = kinds
        self.kinds_out = {}
        self.phys = {}
        self.physv = {}
        self.name2slot = {}
        self.base = {}
        self.cur_names = []
        self.hold1 = []
        self.hold2 = []
        self.free = {"sw": [], "hw": [], "eng": []}
        self.slotkind = {}
        self.nslots = 0
        self.maxw = {}
        self.dangling = []

    def _adopt(self, name, pinned=False):
        kind = "eng" if self.kinds is None else self.kinds.get(name, "eng")
        if self.free[kind]:
            slot = self.free[kind].pop(0)
        else:
            slot = self.nslots
            self.nslots += 1
            assert self.nslots <= self.PHYS_MAX, "semaphore pool exhausted"
            self.phys[slot] = self.es.enter_context(self.nc.semaphore(f"S{slot}"))
            self.physv[slot] = 0
            self.slotkind[slot] = kind
        self.name2slot[name] = slot
        self.base[name] = self.physv[slot]
        if not pinned:
            self.cur_names.append(name)

    def sem(self, name):
        if name not in self.name2slot:
            self._adopt(name, pinned=(name == "SB"))
        return self.phys[self.name2slot[name]]

    def inc(self, instr, name, n=1):
        instr.then_inc(self.sem(name), n)
        self.physv[self.name2slot[name]] += n
        if n >= 16:
            q = str(getattr(instr, "ins", ""))
            kind = "sw" if "qPool" in q else "hw"
            if self.kinds_out.get(name) != "sw":
                self.kinds_out[name] = kind
        else:
            self.kinds_out.setdefault(name, "eng")
        return instr

    def eng_inc(self, eng, name, n=1):
        eng.sem_inc(self.sem(name), n)
        self.physv[self.name2slot[name]] += n

    def wge(self, eng, name, v):
        eng.wait_ge(self.sem(name), self.base[name] + v)
        if v > self.maxw.get(name, -1):
            self.maxw[name] = v

    def val(self, name):
        self.sem(name)
        return self.physv[self.name2slot[name]] - self.base[name]

    def sbuf(self, name, shape=None, dtype=None):
        if name not in self.sb:
            self.sb[name] = self.es.enter_context(self.nc.sbuf_tensor(name, shape, dtype))
        return self.sb[name]

    def psum(self, name, shape=None, dtype=F32):
        if name not in self.sb:
            self.sb[name] = self.es.enter_context(self.nc.psum_tensor(name, shape, dtype))
        return self.sb[name]

    def carve(self, off, shape, dtype):
        import math
        per = int(np.prod(shape[1:])) * mybir.dt.size(dtype)
        v = self.sb["ext"][0:shape[0], off:off + per].bitcast(dtype)
        if len(shape) == 2:
            return v
        return v.rearrange("p (a b) -> p a b", a=shape[1])

    def wait_sb(self, e):
        e.wait_ge(self.sem("SB"), self.barrier_target)

    def bar(self, instr, n=1):
        self.inc(instr, "SB", n)

    def bar_inc(self, eng):
        self.eng_inc(eng, "SB", 1)

    def stage_done(self):
        self.barrier_target = self.physv[self.name2slot["SB"]]
        recyclable = []
        for nm in self.cur_names:
            tot = self.physv[self.name2slot[nm]] - self.base[nm]
            if self.maxw.get(nm, 0) < tot:
                self.dangling.append((nm, self.maxw.get(nm, 0), tot))
            else:
                recyclable.append(nm)
        for nm in self.hold2:
            slot = self.name2slot[nm]
            self.free[self.slotkind[slot]].append(slot)
            del self.name2slot[nm]
            del self.base[nm]
        self.hold2 = self.hold1
        self.hold1 = [] if DISABLE_RECYCLE else recyclable
        self.cur_names = []


def alloc_pools(ctx):
    ctx.sbuf("pb0", [128, 2, 16384], BF16)
    ctx.sbuf("pb1", [128, 2, 4096], BF16)
    ctx.sbuf("pf0", [128, 2, 4096], F32)
    ctx.sbuf("pf1", [128, 4096], F32)
    ctx.sbuf("ysc", [128, 2, 4096], F32)
    ctx.sbuf("bcs", [128, 2, 4096], BF16)
    ctx.sbuf("st0", [128, 64], F32)
    ctx.sbuf("st1", [128, 64], F32)
    ctx.sbuf("stb", [128, 64], BF16)
    ctx.sbuf("ext", [128, 16384], mybir.dt.uint8)
    ctx.psum("pp0", [128, 4, 512], F32)
    ctx.psum("pp1", [128, 2, 512], F32)
    ctx.psum("ppt", [128, 2, 1024], BF16)
    ctx.sbuf("idt", [128, 128], BF16)
    ctx.sbuf("idtf", [128, 128], F32)
    ctx.sbuf("cst", [128, 24], F32)
    ctx.sbuf("cur1", [128, 20], F32)
    ctx.sbuf("cur2", [128, 20], F32)
    ctx.sbuf("cur3", [128, 20], F32)
    ctx.sbuf("sti2", [128, 64], I32)
    ctx.sbuf("sti3", [128, 64], I32)
    ctx.sbuf("nkvrow", [1, 1024], BF16)
    ctx.sbuf("nqrow", [1, 1024], F32)
    ctx.sbuf("ones1", [1, 1024], BF16)
    ctx.sbuf("ones1f", [1, 1024], F32)


def emit_init(ctx):
    nc = ctx.nc
    idt, idtf, cst = ctx.sb["idt"], ctx.sb["idtf"], ctx.sb["cst"]
    with nc.Block("init") as blk:
        @blk.gpsimd
        def _(g):
            g.memset(idt[:], 1.0)
            g.affine_select(out=idt[:], in_=idt[:], pattern=[[-1, 128]],
                            compare_op=ALU.is_equal, fill=0.0, base=0, channel_multiplier=1)
            g.memset(idtf[:], 1.0)
            g.affine_select(out=idtf[:], in_=idtf[:], pattern=[[-1, 128]],
                            compare_op=ALU.is_equal, fill=0.0, base=0, channel_multiplier=1)
            g.memset(ctx.sb["ones1"][:], 1.0)
            g.memset(ctx.sb["ones1f"][:], 1.0)
            for i, v in enumerate(CONST_COLS):
                g.memset(cst[:, i:i + 1], float(v))
            for i, cu in enumerate(CURS):
                g.memset(ctx.sb["cur1"][:, i:i + 1], float(np.float32(1.0 / cu)))
                g.memset(ctx.sb["cur3"][:, i:i + 1], float(np.float32(cu)))
                ins = g.memset(ctx.sb["cur2"][:, i:i + 1], float(np.float32(-cu)))
            ctx.bar(ins, 1)
    ctx.stage_done()


def emit_transpose_mat(ctx, tag, src_name, rows, cols, dst_name, dt=BF16):
    src_ap = ctx.aps[src_name]
    dst_ap = ctx.aps[dst_name]
    """dst = src^T stored [pcols(<=128), cols//128(>=1), rows] with
    dst[p, c, j] = src[j, c*128+p]. rows%128 may be partial on the last tile;
    cols may be < 128 (then single chunk of width cols)."""
    nc = ctx.nc
    nrt = (rows + 127) // 128
    nct = max(1, cols // 128)
    cw = min(128, cols)
    tin = ctx.sb["pb1"] if dt == BF16 else ctx.sb["pf0"]
    pst = ctx.sb["ppt"] if dt == BF16 else ctx.sb["pp0"]
    dst = ctx.sb["pb0"] if dt == BF16 else ctx.sb["pf1"]
    dstf = dst[:].rearrange("p a b -> p (a b)") if dt == BF16 else dst[:]
    idt = ctx.sb["idt"] if dt == BF16 else ctx.sb["idtf"]
    s_ld, s_mm, s_ev, s_st = tag + "l", tag + "m", tag + "e", tag + "s"
    npb = 2 if dt == BF16 else 4
    with nc.Block("tp" + tag) as blk:
        @blk.gpsimd
        def _(g):
            ctx.wait_sb(g)
            for r in range(nrt):
                rs = min(128, rows - r * 128)
                if r >= 2:
                    ctx.wge(g, s_mm, (r - 1) * nct)
                ctx.inc(g.dma_start(out=tin[0:rs, r % 2, 0:cols],
                                    in_=src_ap[r * 128:r * 128 + rs, :]), s_ld, 16)
        @blk.tensor
        def _(t):
            ctx.wait_sb(t)
            k = 0
            for r in range(nrt):
                rs = min(128, rows - r * 128)
                ctx.wge(t, s_ld, (r + 1) * 16)
                for c in range(nct):
                    if k >= npb:
                        ctx.wge(t, s_ev, k - npb + 1)
                    pv = pst[:, k % 2, 0:128] if dt == BF16 else pst[:, k % 4, 0:128]
                    ins = t.transpose(pv[0:cw, 0:rs], tin[0:rs, r % 2, c * 128:c * 128 + cw], idt[0:rs, 0:rs])
                    ctx.inc(ins, s_mm, 1)
                    k += 1
        @blk.vector
        def _(v):
            ctx.wait_sb(v)
            k = 0
            for r in range(nrt):
                rs = min(128, rows - r * 128)
                for c in range(nct):
                    ctx.wge(v, s_mm, k + 1)
                    pv = pst[:, k % 2, 0:128] if dt == BF16 else pst[:, k % 4, 0:128]
                    ins = v.tensor_copy(dstf[0:cw, c * rows + r * 128: c * rows + r * 128 + rs],
                                        pv[0:cw, 0:rs])
                    ctx.inc(ins, s_ev, 1)
                    k += 1
        @blk.sync
        def _(sy):
            ctx.wge(sy, s_ev, nrt * nct)
            ins = sy.dma_start(out=dst_ap.rearrange("p c r -> p (c r)"),
                               in_=dstf[0:cw, 0:nct * rows])
            ctx.inc(ins, s_st, 16)
            ctx.wge(sy, s_st, 16)
            ctx.bar_inc(sy)
    ctx.stage_done()


def emit_ternarize(ctx, tag, src_name, rows, cols, dst_name, invn_col):
    src_ap = ctx.aps[src_name]
    dst_ap = ctx.aps[dst_name]
    """BitLinear effective weight: per row scale=max(mean|w|,1e-5), w_eff =
    clip(round(w/scale),-1,1)*scale -> bf16 DRAM [rows, cols]."""
    nc = ctx.nc
    nrt = rows // 128
    tin = ctx.sb["pf0"]
    tw = ctx.sb["pf1"]
    ti = ctx.sb["ysc"][:, 0, :].bitcast(I32)
    tb = ctx.sb["pb1"]
    st0 = ctx.sb["st0"]
    cst = ctx.sb["cst"]
    s_ld, s_sc, s_ve, s_st = tag + "l", tag + "c", tag + "v", tag + "s"
    with nc.Block("tern" + tag) as blk:
        @blk.sync
        def _(sy):
            ctx.wait_sb(sy)
            for r in range(nrt):
                if r >= 2:
                    ctx.wge(sy, s_ve, r - 1)
                ctx.inc(sy.dma_start(out=tin[:, r % 2, 0:cols],
                                     in_=src_ap[r * 128:(r + 1) * 128, :]), s_ld, 16)
        @blk.scalar
        def _(s):
            ctx.wait_sb(s)
            for r in range(nrt):
                ctx.wge(s, s_ld, (r + 1) * 16)
                if r >= 1:
                    ctx.wge(s, s_ve, r)
                ins = s.activation(out=tb[:, 0, 0:cols], in_=tin[:, r % 2, 0:cols],
                                   func=AF.Abs, accum_out=st0[:, 0:1])
                ctx.inc(ins, s_sc, 1)
        @blk.vector
        def _(v):
            ctx.wait_sb(v)
            for r in range(nrt):
                ctx.wge(v, s_sc, r + 1)
                if r > 0:
                    ctx.wge(v, s_st, r * 16)
                v.tensor_scalar(out=st0[:, 1:2], in0=st0[:, 0:1], scalar1=cst[:, invn_col:invn_col + 1],
                                scalar2=cst[:, C_CLIP:C_CLIP + 1], op0=ALU.mult, op1=ALU.max)
                v.drain()
                v.reciprocal(st0[:, 2:3], st0[:, 1:2])
                v.drain()
                v.tensor_scalar(out=tw[:, 0:cols], in0=tin[:, r % 2, 0:cols], scalar1=st0[:, 2:3],
                                scalar2=None, op0=ALU.mult)
                v.tensor_copy(ti[:, 0:cols], tw[:, 0:cols])
                v.tensor_copy(tw[:, 0:cols], ti[:, 0:cols])
                v.tensor_scalar(out=tw[:, 0:cols], in0=tw[:, 0:cols], scalar1=cst[:, C_NEG1:C_NEG1 + 1],
                                scalar2=cst[:, C_ONE:C_ONE + 1], op0=ALU.max, op1=ALU.min)
                ins = v.tensor_scalar(out=tb[:, 1, 0:cols], in0=tw[:, 0:cols], scalar1=st0[:, 1:2],
                                      scalar2=None, op0=ALU.mult)
                ctx.inc(ins, s_ve, 1)
        @blk.gpsimd
        def _(g):
            ctx.wait_sb(g)
            for r in range(nrt):
                ctx.wge(g, s_ve, r + 1)
                ins = g.dma_start(out=dst_ap[r * 128:(r + 1) * 128, :], in_=tb[:, 1, 0:cols])
                ctx.inc(ins, s_st, 16)
            ctx.wge(g, s_st, nrt * 16)
            ctx.bar_inc(g)
    ctx.stage_done()


def emit_wtl(ctx, dir_, outw_name, dst_name):
    nc = ctx.nc
    wout = ctx.sb["pb0"][:].rearrange("p a b -> p (a b)")
    fusT = ctx.sb["pb0"][:].rearrange("p a b -> p (a b)")[:, 16384:32768].rearrange("p (a b) -> p a b", a=16)
    stg = ctx.sb["ysc"][:].rearrange("p a b -> p (a b)").bitcast(BF16).rearrange("p (a b) -> p a b", a=16)
    pp = ctx.sb["pp1"]
    goff = 0 if dir_ == "f" else 8
    s_ld, s_mm, s_ev, s_st = [f"wt{dir_}{x}" for x in "lmes"]
    with nc.Block("wtl" + dir_) as blk:
        @blk.gpsimd
        def _(g):
            ctx.wait_sb(g)
            ctx.inc(g.dma_start(out=wout[:, 0:16384].rearrange("p (a b) -> p a b", a=8),
                                in_=ctx.aps[outw_name].rearrange("(a p) c -> p a c", p=128)), s_ld, 16)
            ctx.inc(g.dma_start(out=fusT[:], in_=ctx.aps["fusT"]), s_ld, 16)
        @blk.tensor
        def _(t):
            ctx.wait_sb(t)
            ctx.wge(t, s_ld, 32)
            k = 0
            for m in range(16):
                for nh in range(2):
                    if k >= 2:
                        ctx.wge(t, s_ev, k - 1)
                    for g in range(8):
                        ins = t.matmul(pp[:, k % 2, :],
                                       wout[:, 0:16384].rearrange("p (a b) -> p a b", a=8)[:, g, m * 128:(m + 1) * 128],
                                       fusT[:, goff + g, nh * 512:(nh + 1) * 512],
                                       start=(g == 0), stop=(g == 7))
                    ctx.inc(ins, s_mm, 1)
                    k += 1
        @blk.vector
        def _(v):
            ctx.wait_sb(v)
            for k in range(32):
                m, nh = divmod(k, 2)
                ctx.wge(v, s_mm, k + 1)
                ctx.inc(v.tensor_copy(stg[:, m, nh * 512:(nh + 1) * 512], pp[:, k % 2, :]), s_ev, 1)
        @blk.sync
        def _(sy):
            ctx.wge(sy, s_ev, 32)
            ctx.inc(sy.dma_start(out=ctx.aps[dst_name], in_=stg[:]), s_st, 16)
            ctx.wge(sy, s_st, 16)
            ctx.bar_inc(sy)
    ctx.stage_done()


def emit_ln(ctx, tag, src_name, dst_bf=None, dst_f32=None, dst_tm=None):
    nc = ctx.nc
    G = 4
    src_ap = ctx.aps[src_name]
    xin = ctx.sb["pf0"]
    xnb = ctx.sb["pb1"]
    ysc = ctx.sb["ysc"]
    st0 = ctx.sb["st0"]
    cst = ctx.sb["cst"]
    junk = ctx.sb["bcs"]
    pst = ctx.sb["ppt"]
    pstf = ctx.sb["pp0"]
    stgb = ctx.sb["pb0"][:].rearrange("p a b -> p (a b)")
    stgf = ctx.sb["pf1"]
    idt, idtf = ctx.sb["idt"], ctx.sb["idtf"]
    s_ld, s_r, s_sq, s_v2, s_rs, s_xn, s_mm, s_ev, s_st, s_tm = [tag + x for x in
        ("l", "r", "q", "v", "s", "n", "m", "e", "t", "o")]
    need_f32_tile = (dst_f32 is not None) or (dst_tm is not None)
    with nc.Block("ln" + tag) as blk:
        @blk.sync
        def _(sy):
            ctx.wait_sb(sy)
            for g in range(8):
                if g >= 2:
                    ctx.wge(sy, s_xn, (g - 1) * G)
                ctx.inc(sy.dma_start(out=xin[:, g % 2, :].rearrange("p (f d) -> p f d", f=G),
                                     in_=src_ap.rearrange("(g f p) d -> g p f d", g=8, f=G)[g]), s_ld, 16)
        @blk.vector
        def _(v):
            ctx.wait_sb(v)
            kev = 0
            for g in range(8):
                ctx.wge(v, s_ld, (g + 1) * 16)
                for i in range(G):
                    v.reduce_sum(st0[:, i:i + 1], xin[:, g % 2, i * 1024:(i + 1) * 1024],
                                 axis=AX.X, negate=True)
                v.drain()
                for i in range(G):
                    ins = v.tensor_scalar(out=st0[:, 8 + i:9 + i], in0=st0[:, i:i + 1],
                                          scalar1=cst[:, C_INVD:C_INVD + 1], scalar2=None, op0=ALU.mult)
                ctx.inc(ins, s_r, 1)
                ctx.wge(v, s_sq, g + 1)
                for i in range(G):
                    ins = v.tensor_scalar(out=st0[:, 16 + i:17 + i], in0=st0[:, 24 + i:25 + i],
                                          scalar1=cst[:, C_INVD:C_INVD + 1], scalar2=cst[:, C_EPS:C_EPS + 1],
                                          op0=ALU.mult, op1=ALU.add)
                ctx.inc(ins, s_v2, 1)
                ctx.wge(v, s_rs, g + 1)
                if g >= 2:
                    ctx.wge(v, s_mm, (g - 1) * G * NDC)
                if dst_tm is not None and g >= 2:
                    ctx.wge(v, s_tm, (g - 1) * G * 16)
                for i in range(G):
                    if dst_bf is not None:
                        ins = v.tensor_scalar(out=xnb[:, g % 2, i * 1024:(i + 1) * 1024],
                                              in0=xin[:, g % 2, i * 1024:(i + 1) * 1024],
                                              scalar1=st0[:, 8 + i:9 + i], scalar2=st0[:, 32 + i:33 + i],
                                              op0=ALU.add, op1=ALU.mult)
                    if need_f32_tile:
                        ins = v.tensor_scalar(out=ysc[:, g % 2, i * 1024:(i + 1) * 1024],
                                              in0=xin[:, g % 2, i * 1024:(i + 1) * 1024],
                                              scalar1=st0[:, 8 + i:9 + i], scalar2=st0[:, 32 + i:33 + i],
                                              op0=ALU.add, op1=ALU.mult)
                ctx.inc(ins, s_xn, G)
                for ge in ([g - 1] if g >= 1 else []) + ([7] if g == 7 else []):
                    for i in range(G):
                        ti = ge * G + i
                        for c in range(NDC):
                            ctx.wge(v, s_mm, kev + 1)
                            if dst_bf is not None:
                                ins = v.tensor_copy(stgb[:, c * 4096 + ti * 128: c * 4096 + (ti + 1) * 128],
                                                    pst[:, kev % 2, 0:128])
                            else:
                                if kev >= 32 and c == 0 and i == 0:
                                    ctx.wge(v, s_st, ge * NDC * 16)
                                ins = v.tensor_copy(stgf[:, c * 512 + i * 128: c * 512 + (i + 1) * 128],
                                                    pstf[:, kev % 4, 0:128])
                            ctx.inc(ins, s_ev, 1)
                            kev += 1
        @blk.scalar
        def _(s):
            ctx.wait_sb(s)
            for g in range(8):
                ctx.wge(s, s_r, g + 1)
                for i in range(G):
                    ins = s.activation(out=junk[:, 0, i * 1024:(i + 1) * 1024],
                                       in_=xin[:, g % 2, i * 1024:(i + 1) * 1024],
                                       func=AF.Square, bias=st0[:, 8 + i:9 + i], scale=1.0,
                                       accum_out=st0[:, 24 + i:25 + i])
                ctx.inc(ins, s_sq, 1)
                ctx.wge(s, s_v2, g + 1)
                for i in range(G):
                    s.activation(out=st0[:, 40 + i:41 + i], in_=st0[:, 16 + i:17 + i], func=AF.Ln)
                s.drain()
                for i in range(G):
                    ins = s.activation(out=st0[:, 32 + i:33 + i], in_=st0[:, 40 + i:41 + i],
                                       func=AF.Exp, scale=cst[:, C_NEGHALF:C_NEGHALF + 1])
                ctx.inc(ins, s_rs, 1)
        @blk.tensor
        def _(t):
            ctx.wait_sb(t)
            k = 0
            for g in range(8):
                for i in range(G):
                    ti = g * G + i
                    ctx.wge(t, s_xn, g * G + i + 1)
                    for c in range(NDC):
                        if dst_bf is not None:
                            if k >= 2:
                                ctx.wge(t, s_ev, k - 1)
                            src_t = xnb[:, g % 2, i * 1024 + c * 128:i * 1024 + (c + 1) * 128]
                            ins = t.transpose(pst[:, k % 2, 0:128], src_t, idt[:])
                        else:
                            if k >= 4:
                                ctx.wge(t, s_ev, k - 3)
                            src_t = ysc[:, g % 2, i * 1024 + c * 128:i * 1024 + (c + 1) * 128]
                            ins = t.transpose(pstf[:, k % 4, 0:128], src_t, idtf[:])
                        ctx.inc(ins, s_mm, 1)
                        k += 1
        @blk.gpsimd
        def _(g_):
            ctx.wait_sb(g_)
            for g in range(9):
                if g < 8 and dst_tm is not None:
                    for i in range(G):
                        ti = g * G + i
                        ctx.wge(g_, s_xn, g * G + i + 1)
                        ctx.inc(g_.dma_start(out=ctx.aps[dst_tm][ti * 128:(ti + 1) * 128, :],
                                             in_=ysc[:, g % 2, i * 1024:(i + 1) * 1024]), s_tm, 16)
                if dst_f32 is not None and g >= 1:
                    ge = g - 1
                    for c in range(NDC):
                        ctx.wge(g_, s_ev, ge * 32 + 24 + c + 1)
                        ins = g_.dma_start(out=ctx.aps[dst_f32][:, c, ge * 512:(ge + 1) * 512],
                                           in_=stgf[:, c * 512:(c + 1) * 512])
                        ctx.inc(ins, s_st, 16)

        @blk.scalar
        def _(s2):
            pass
    with nc.Block("ln_st" + tag) as blk:
        @blk.sync
        def _(sy):
            if dst_bf is not None:
                ctx.wge(sy, s_ev, NTT * NDC)
                ins = sy.dma_start(out=ctx.aps[dst_bf].rearrange("p c t -> p (c t)"), in_=stgb[:, 0:NDC * T])
                ctx.inc(ins, s_st, 16)
                ctx.wge(sy, s_st, ctx.val(s_st))
            else:
                ctx.wge(sy, s_st, ctx.val(s_st))
            if dst_tm is not None:
                ctx.wge(sy, s_tm, NTT * 16)
            ctx.bar_inc(sy)
    ctx.stage_done()


def emit_inproj(ctx, dir_):
    nc = ctx.nc
    w = ctx.sb["pb0"][:].rearrange("p a b -> p (a b)").rearrange("p (c m) -> p c m", c=NDC)
    xch = ctx.sb["pb1"]
    evb = ctx.sb["bcs"][:].rearrange("p a b -> p (a b)")
    pp = ctx.sb["pp0"]
    s_w, s_ld, s_mm, s_ev, s_st = [f"ip{dir_}{x}" for x in "wlmes"]
    s_ev2 = f"ip{dir_}z"
    xs_ap = ctx.aps[f"xs_{dir_}"]
    zg_ap = ctx.aps[f"zg_{dir_}"]
    with nc.Block("inproj" + dir_) as blk:
        @blk.gpsimd
        def _(g):
            ctx.wait_sb(g)
            ctx.inc(g.dma_start(out=w[:, :, :], in_=ctx.aps[f"winT_{dir_}"]), s_w, 16)
        @blk.sync
        def _(sy):
            ctx.wait_sb(sy)
            for ch in range(8):
                if ch >= 2:
                    ctx.wge(sy, s_mm, (ch - 1) * 32)
                ctx.inc(sy.dma_start(out=xch[:, ch % 2, :].rearrange("p (c t) -> p c t", c=NDC),
                                     in_=ctx.aps["xnf"][:, :, ch * 512:(ch + 1) * 512]), s_ld, 16)
        @blk.tensor
        def _(t):
            ctx.wait_sb(t)
            ctx.wge(t, s_w, 16)
            k = 0
            for ch in range(8):
                ctx.wge(t, s_ld, (ch + 1) * 16)
                for m in range(32):
                    if k >= 4:
                        kk = k - 4
                        nxs = sum(1 for j in range(kk + 1) if (j % 32) < 16)
                        nz = (kk + 1) - nxs
                        if nxs:
                            ctx.wge(t, s_ev, nxs)
                        if nz:
                            ctx.wge(t, s_ev2, nz)
                    for kc in range(NDC):
                        ins = t.matmul(pp[:, k % 4, :], w[:, kc, m * 128:(m + 1) * 128],
                                       xch[:, ch % 2, :].rearrange("p (c t) -> p c t", c=NDC)[:, kc, :],
                                       start=(kc == 0), stop=(kc == NDC - 1))
                    ctx.inc(ins, s_mm, 1)
                    k += 1
        @blk.vector
        def _(v):
            ctx.wait_sb(v)
            for k in range(256):
                ch, m = divmod(k, 32)
                if m >= 16:
                    continue
                ctx.wge(v, s_mm, k + 1)
                if k >= 16:
                    ctx.wge(v, s_st, (k - 15) * 16)
                ins = v.tensor_copy(evb[:, (k % 16) * 512:(k % 16 + 1) * 512], pp[:, k % 4, :])
                ctx.inc(ins, s_ev, 1)
        @blk.scalar
        def _(s):
            ctx.wait_sb(s)
            for k in range(256):
                ch, m = divmod(k, 32)
                if m < 16:
                    continue
                ctx.wge(s, s_mm, k + 1)
                if k >= 16:
                    ctx.wge(s, s_st, (k - 15) * 16)
                ins = s.activation(out=evb[:, (k % 16) * 512:(k % 16 + 1) * 512], in_=pp[:, k % 4, :],
                                   func=AF.Silu)
                ctx.inc(ins, s_ev2, 1)
        @blk.gpsimd
        def _(g):
            for k in range(256):
                ch, m = divmod(k, 32)
                ctx.wge(g, s_ev2 if m >= 16 else s_ev, (ch * 16 + (m - 16 if m >= 16 else m)) + 1)
                dst = xs_ap[:, m, ch * 512:(ch + 1) * 512] if m < 16 else zg_ap[:, m - 16, ch * 512:(ch + 1) * 512]
                ins = g.dma_start(out=dst, in_=evb[:, (k % 16) * 512:(k % 16 + 1) * 512])
                ctx.inc(ins, s_st, 16)
            ctx.wge(g, s_st, 256 * 16)
            ctx.bar_inc(g)
    ctx.stage_done()


def emit_conv(ctx, dir_):
    nc = ctx.nc
    st0, st1 = ctx.sb["st0"], ctx.sb["st1"]
    win = ctx.sb["pb1"]
    acc = ctx.sb["pb0"][:].rearrange("p a b -> p (a b)")
    s_c, s_ld, s_ve, s_si, s_st = [f"cv{dir_}{x}" for x in "clvis"]
    xs_ap, u_ap = ctx.aps[f"xs_{dir_}"], ctx.aps[f"u_{dir_}"]
    fwd = dir_ == "f"
    with nc.Block("conv" + dir_) as blk:
        @blk.gpsimd
        def _(g):
            ctx.wait_sb(g)
            ctx.inc(g.dma_start(out=st1[:, 0:64].rearrange("p (c k) -> p c k", k=4),
                    in_=ctx.aps["conv_w"].rearrange("(c p) one k -> p c (one k)", p=128)), s_c, 16)
            ins = g.dma_start(out=st0[:, 0:16], in_=ctx.aps["conv_b"].rearrange("(c p) -> p c", p=128), allow_slow_non_contiguous=True)
            ctx.inc(ins, s_c, 16)
            for idx in range(32):
                ct, hf = divmod(idx, 2)
                c0 = hf * 2048
                sl = idx % 2
                if idx >= 2:
                    ctx.wge(g, s_ve, (idx - 1) * 4)
                if fwd:
                    if ct == 0 or True:
                        pass
                    if hf == 0:
                        g.memset(win[:, sl, 0:3], 0.0)
                        ins = g.dma_start(out=win[:, sl, 3:2051], in_=xs_ap[:, ct, 0:2048])
                    else:
                        ins = g.dma_start(out=win[:, sl, 0:2051], in_=xs_ap[:, ct, 2045:4096])
                else:
                    if hf == 1:
                        g.memset(win[:, sl, 2048:2051], 0.0)
                        ins = g.dma_start(out=win[:, sl, 0:2048], in_=xs_ap[:, ct, 2048:4096])
                    else:
                        ins = g.dma_start(out=win[:, sl, 0:2051], in_=xs_ap[:, ct, 0:2051])
                ctx.inc(ins, s_ld, 16)
        @blk.vector
        def _(v):
            ctx.wait_sb(v)
            ctx.wge(v, s_c, 32)
            for idx in range(32):
                ct, hf = divmod(idx, 2)
                sl = idx % 2
                a = acc[:, sl * 2048:(sl + 1) * 2048]
                ctx.wge(v, s_ld, (idx + 1) * 16)
                if idx >= 2:
                    ctx.wge(v, s_si, idx - 1)
                taps = range(4)
                for j, tap in enumerate(taps):
                    wcol = st1[:, ct * 4 + (tap if fwd else 3 - tap): ct * 4 + (tap if fwd else 3 - tap) + 1]
                    if j == 0:
                        ins = v.tensor_scalar(out=a, in0=win[:, sl, tap:tap + 2048], scalar1=wcol,
                                              scalar2=None, op0=ALU.mult)
                    else:
                        ins = v.scalar_tensor_tensor(out=a, in0=win[:, sl, tap:tap + 2048], scalar=wcol,
                                                     in1=a, op0=ALU.mult, op1=ALU.add)
                    ctx.inc(ins, s_ve, 1)
        @blk.scalar
        def _(s):
            ctx.wait_sb(s)
            for idx in range(32):
                ct, hf = divmod(idx, 2)
                sl = idx % 2
                ctx.wge(s, s_ve, (idx + 1) * 4)
                if idx >= 2:
                    ctx.wge(s, s_st, (idx - 1) * 16)
                ins = s.activation(out=acc[:, 4096 + sl * 2048: 4096 + (sl + 1) * 2048],
                                   in_=acc[:, sl * 2048:(sl + 1) * 2048],
                                   func=AF.Silu, bias=st0[:, ct:ct + 1], scale=1.0)
                ctx.inc(ins, s_si, 1)
        @blk.sync
        def _(sy):
            for idx in range(32):
                ct, hf = divmod(idx, 2)
                sl = idx % 2
                ctx.wge(sy, s_si, idx + 1)
                ins = sy.dma_start(out=u_ap[:, ct, hf * 2048:(hf + 1) * 2048],
                                   in_=acc[:, 4096 + sl * 2048: 4096 + (sl + 1) * 2048])
                ctx.inc(ins, s_st, 16)
            ctx.wge(sy, s_st, 32 * 16)
            ctx.bar_inc(sy)
    ctx.stage_done()


def emit_xdbl(ctx, dir_):
    nc = ctx.nc
    wxp = ctx.carve(0, [128, NCC, 96], BF16)
    wdt = ctx.carve(3072, [64, C], BF16)
    st0 = ctx.sb["st0"]
    uch = ctx.sb["pb0"][:].rearrange("p a b -> p (a b)")
    xd = ctx.carve(7168, [96, 2, 512], BF16)
    ebuf = ctx.sb["pf1"]
    dlo = ctx.sb["pf0"]
    dub = ctx.sb["bcs"][:].rearrange("p a b -> p (a b)")
    pp1 = ctx.sb["pp1"]
    pp = ctx.sb["pp0"]
    s_w, s_ld, s_m1, s_e1, s_m2, s_sp, s_du, s_st, s_bc = [f"xd{dir_}{x}" for x in "wlaebpdsc"]
    u_ap, dl_ap, du_ap, bc_ap = [ctx.aps[f"{n}_{dir_}"] for n in ("u", "dl", "du", "bc")]
    with nc.Block("xdbl" + dir_) as blk:
        @blk.gpsimd
        def _(g):
            ctx.wait_sb(g)
            ctx.inc(g.dma_start(out=wxp[:], in_=ctx.aps[f"wxpT_{dir_}"]), s_w, 16)
            ctx.inc(g.dma_start(out=wdt[:], in_=ctx.aps[f"wdtT_{dir_}"].rearrange("a one b -> a (one b)")), s_w, 16)
            ctx.inc(g.dma_start(out=st0[:, 0:16], in_=ctx.aps["dt_proj_b"].rearrange("(c p) -> p c", p=128), allow_slow_non_contiguous=True), s_w, 16)
            for ch in range(8):
                if ch >= 2:
                    ctx.wge(g, s_du, (ch - 1) * 16)
                ins = g.dma_start(out=uch[:, (ch % 2) * 8192:(ch % 2) * 8192 + 8192].rearrange("p (c t) -> p c t", c=NCC),
                                  in_=u_ap[:, :, ch * 512:(ch + 1) * 512])
                ctx.inc(ins, s_ld, 16)
        @blk.tensor
        def _(t):
            ctx.wait_sb(t)
            ctx.wge(t, s_w, 48)
            for ch in range(8):
                ctx.wge(t, s_ld, (ch + 1) * 16)
                if ch >= 2:
                    ctx.wge(t, s_bc, (ch - 1) * 16)
                uv = uch[:, (ch % 2) * 8192:(ch % 2) * 8192 + 8192].rearrange("p (c t) -> p c t", c=NCC)
                for kc in range(NCC):
                    ins = t.matmul(pp1[0:96, 0, :], wxp[:, kc, :], uv[:, kc, :],
                                   start=(kc == 0), stop=(kc == NCC - 1))
                ctx.inc(ins, s_m1, 1)
                ctx.wge(t, s_e1, ch + 1)
                for mt in range(16):
                    k = ch * 16 + mt
                    if k >= 4:
                        ctx.wge(t, s_sp, k - 3)
                    ins = t.matmul(pp[:, mt % 4, :], wdt[0:64, mt * 128:(mt + 1) * 128], xd[0:64, ch % 2, 0:512],
                                   start=True, stop=True)
                    ctx.inc(ins, s_m2, 1)
        @blk.vector
        def _(v):
            ctx.wait_sb(v)
            for ch in range(8):
                ctx.wge(v, s_m1, ch + 1)
                ins = v.tensor_copy(xd[0:96, ch % 2, 0:512], pp1[0:96, 0, :])
                ctx.inc(ins, s_e1, 1)
                uv = uch[:, (ch % 2) * 8192:(ch % 2) * 8192 + 8192].rearrange("p (c t) -> p c t", c=NCC)
                for mt in range(16):
                    k = ch * 16 + mt
                    ctx.wge(v, s_sp, k + 1)
                    if k >= 8:
                        ctx.wge(v, s_st, (k - 7) * 32)
                    ins = v.tensor_mul(dub[:, (k % 8) * 512:(k % 8 + 1) * 512],
                                       dlo[:, mt % 2, (ch % 4) * 1024:(ch % 4) * 1024 + 512] if False else
                                       dlo[:, k % 2, 0:512], uv[:, mt, :])
                    ctx.inc(ins, s_du, 1)
        @blk.scalar
        def _(s):
            ctx.wait_sb(s)
            for ch in range(8):
                for mt in range(16):
                    k = ch * 16 + mt
                    ctx.wge(s, s_m2, k + 1)
                    if k >= 2:
                        ctx.wge(s, s_du, k - 1)
                        ctx.wge(s, s_st, (2 * (k - 2) + 1) * 16)
                    s.activation(out=ebuf[:, (k % 2) * 2048:(k % 2) * 2048 + 512], in_=pp[:, mt % 4, :],
                                 func=AF.Exp, bias=st0[:, ct_col(mt):ct_col(mt) + 1], scale=1.0)
                    ins = s.activation(out=dlo[:, k % 2, 0:512], in_=ebuf[:, (k % 2) * 2048:(k % 2) * 2048 + 512],
                                       func=AF.Ln, bias=1.0, scale=1.0)
                    ctx.inc(ins, s_sp, 1)
        @blk.sync
        def _(sy):
            for ch in range(8):
                ctx.wge(sy, s_e1, ch + 1)
                ins = sy.dma_start(out=bc_ap[:, ch * 512:(ch + 1) * 512], in_=xd[64:96, ch % 2, 0:512])
                ctx.inc(ins, s_bc, 16)
                for mt in range(16):
                    k = ch * 16 + mt
                    ctx.wge(sy, s_sp, k + 1)
                    ins = sy.dma_start(out=dl_ap[:, mt, ch * 512:(ch + 1) * 512], in_=dlo[:, k % 2, 0:512])
                    ctx.inc(ins, s_st, 16)
                    ctx.wge(sy, s_du, k + 1)
                    ins = sy.dma_start(out=du_ap[:, mt, ch * 512:(ch + 1) * 512],
                                       in_=dub[:, (k % 8) * 512:(k % 8 + 1) * 512])
                    ctx.inc(ins, s_st, 16)
            ctx.wge(sy, s_st, 256 * 16)
            ctx.wge(sy, s_bc, 8 * 16)
            ctx.bar_inc(sy)
    ctx.stage_done()


def ct_col(mt):
    return mt


def emit_scan(ctx, dir_):
    nc = ctx.nc
    dl_ap, du_ap, bc_ap, y_ap = [ctx.aps[f"{n}_{dir_}"] for n in ("dl", "du", "bc", "y")]
    dlt = ctx.sb["pf0"]
    dut = ctx.sb["pb1"]
    bct = ctx.sb["bcs"]
    av = ctx.sb["pf1"]
    pb0f = ctx.sb["pb0"][:].rearrange("p a b -> p (a b)")
    yt = ctx.sb["ysc"]
    asb = ctx.carve(0, [128, 256], F32)
    s_A, s_ld, s_bc2, s_a, s_scan, s_y, s_st = [f"sc{dir_}{x}" for x in "Albzcys"]
    s_A2 = f"sc{dir_}B"
    s_pr = f"sc{dir_}p"
    s_y2 = f"sc{dir_}Y"
    HL = 2048
    PRS = (8192, 20480)

    def gcount(p):
        return (p // 32) * 30 + min(p % 32 + 1, 30)
    with nc.Block("scan" + dir_) as blk:
        @blk.gpsimd
        def _(g):
            ctx.wait_sb(g)
            ctx.inc(g.dma_start(out=asb[:].rearrange("p (c s) -> p c s", s=16), in_=ctx.aps["A_log"].rearrange("(c p) s -> p c s", p=128)), s_A, 16)
            for grp in range(8):
                if grp >= 1:
                    ctx.wge(g, s_st, grp * 2 * 16)
                for j in range(2):
                    ct = grp * 2 + j
                    ctx.inc(g.dma_start(out=dlt[:, j, :], in_=dl_ap[:, ct, :]), s_ld, 16)
                    ctx.inc(g.dma_start(out=dut[:, j, :], in_=du_ap[:, ct, :]), s_ld, 16)
                for s in range(17):
                    if s < 16:
                        gs = grp * 16 + s
                        if gs >= 2:
                            ctx.wge(g, s_pr, (gs - 1) * 2)
                        bdst = bct[:, 0, :] if s % 2 == 0 else pb0f[:, 12288:16384]
                        cdst = bct[:, 1, :] if s % 2 == 0 else pb0f[:, 16384:20480]
                        ctx.inc(g.dma_start(out=bdst, in_=bc_ap[s:s + 1, :].partition_broadcast(128)), s_bc2, 16)
                        ctx.inc(g.dma_start(out=cdst, in_=bc_ap[16 + s:17 + s, :].partition_broadcast(128)), s_bc2, 16)
                    if 1 <= s <= 15:
                        sa = s - 1
                        for j in range(2):
                            gk = grp * 32 + 2 * sa + j
                            ctx.wge(g, s_pr, gk + 1)
                            pr = pb0f[:, PRS[gk % 2]:PRS[gk % 2] + 4096]
                            if sa == 0:
                                ctx.wge(g, s_st, grp * 2 * 16)
                                src = pr if dir_ == "f" else pr[:, ::-1]
                                dst = yt[:, j, :] if dir_ == "f" else yt[:, j, :][:, ::-1]
                                ins = g.tensor_copy(dst, src)
                            elif dir_ == "f":
                                ins = g.tensor_tensor(out=yt[:, j, :], in0=yt[:, j, :], in1=pr, op=ALU.add)
                            else:
                                ins = g.tensor_tensor(out=yt[:, j, :][:, ::-1], in0=yt[:, j, :][:, ::-1],
                                                      in1=pr[:, ::-1], op=ALU.add)
                            ctx.inc(ins, s_y, 1)
        @blk.scalar
        def _(sc):
            ctx.wait_sb(sc)
            ctx.wge(sc, s_A, 16)
            ctx.inc(sc.activation(out=asb[:], in_=asb[:], func=AF.Exp), s_A2, 1)
            ctx.wge(sc, s_A2, 2)
            na = 0
            hfs = (0, 1) if dir_ == "f" else (1, 0)
            for grp in range(8):
                ctx.wge(sc, s_ld, (grp + 1) * 4 * 16)
                for s in range(16):
                    for j in range(2):
                        ct = grp * 2 + j
                        for hf in hfs:
                            if na >= 2:
                                ctx.wge(sc, s_scan, na - 1)
                            if dir_ == "f":
                                ins = sc.activation(out=av[:, hf * HL:(hf + 1) * HL],
                                                    in_=dlt[:, j, hf * HL:(hf + 1) * HL],
                                                    func=AF.Exp, scale=asb[:, ct * 16 + s:ct * 16 + s + 1])
                            else:
                                ins = sc.activation(out=av[:, hf * HL:(hf + 1) * HL][:, ::-1],
                                                    in_=dlt[:, j, hf * HL:(hf + 1) * HL][:, ::-1],
                                                    func=AF.Exp, scale=asb[:, ct * 16 + s:ct * 16 + s + 1])
                            ctx.inc(ins, s_a, 1)
                            na += 1
        @blk.vector
        def _(v):
            ctx.wait_sb(v)
            ctx.wge(v, s_A2, 1)
            ins = v.tensor_scalar(out=asb[:], in0=asb[:], scalar1=ctx.sb["cst"][:, C_NEG1:C_NEG1 + 1],
                                  scalar2=None, op0=ALU.mult)
            ctx.inc(ins, s_A2, 1)
            na = 0
            for grp in range(8):
                ctx.wge(v, s_ld, (grp + 1) * 4 * 16)
                for s in range(16):
                    gs = grp * 16 + s
                    ctx.wge(v, s_bc2, (gs + 1) * 32)
                    brep = bct[:, 0, :] if s % 2 == 0 else pb0f[:, 12288:16384]
                    crep = bct[:, 1, :] if s % 2 == 0 else pb0f[:, 16384:20480]
                    hfs = (0, 1) if dir_ == "f" else (1, 0)
                    for j in range(2):
                        for ihf, hf in enumerate(hfs):
                            if dir_ == "f":
                                ins = v.tensor_tensor(out=pb0f[:, hf * HL:(hf + 1) * HL],
                                                      in0=dut[:, j, hf * HL:(hf + 1) * HL],
                                                      in1=brep[:, hf * HL:(hf + 1) * HL], op=ALU.mult)
                            else:
                                ins = v.tensor_tensor(out=pb0f[:, hf * HL:(hf + 1) * HL][:, ::-1],
                                                      in0=dut[:, j, hf * HL:(hf + 1) * HL][:, ::-1],
                                                      in1=brep[:, hf * HL:(hf + 1) * HL][:, ::-1], op=ALU.mult)
                            ctx.wge(v, s_a, na + 1)
                            na += 1
                            if dir_ == "f":
                                init = 0.0 if ihf == 0 else pb0f[:, 4096 + HL - 1:4096 + HL]
                                ins = v.tensor_tensor_scan(out=pb0f[:, 4096 + hf * HL:4096 + (hf + 1) * HL],
                                                           data0=av[:, hf * HL:(hf + 1) * HL],
                                                           data1=pb0f[:, hf * HL:(hf + 1) * HL],
                                                           initial=init, op0=ALU.mult, op1=ALU.add)
                            else:
                                init = 0.0 if ihf == 0 else pb0f[:, 4096 + HL:4096 + HL + 1]
                                ins = v.tensor_tensor_scan(out=pb0f[:, 4096 + hf * HL:4096 + (hf + 1) * HL][:, ::-1],
                                                           data0=av[:, hf * HL:(hf + 1) * HL][:, ::-1],
                                                           data1=pb0f[:, hf * HL:(hf + 1) * HL][:, ::-1],
                                                           initial=init, op0=ALU.mult, op1=ALU.add)
                            ctx.inc(ins, s_scan, 1)
                        gk = grp * 32 + 2 * s + j
                        if gk >= 2 and (gk - 2) % 32 < 30:
                            ctx.wge(v, s_y, gcount(gk - 2))
                        pr = pb0f[:, PRS[gk % 2]:PRS[gk % 2] + 4096]
                        if dir_ == "f":
                            ins = v.tensor_tensor(out=pr, in0=pb0f[:, 4096:8192], in1=crep, op=ALU.mult)
                        else:
                            ins = v.tensor_tensor(out=pr[:, ::-1], in0=pb0f[:, 4096:8192][:, ::-1],
                                                  in1=crep[:, ::-1], op=ALU.mult)
                        ctx.inc(ins, s_pr, 1)
                        if s == 15:
                            ctx.wge(v, s_y, grp * 30 + 29 + j)
                            if dir_ == "f":
                                ins = v.tensor_tensor(out=yt[:, j, :], in0=yt[:, j, :], in1=pr, op=ALU.add)
                                ctx.inc(ins, s_y2, 1)
                            else:
                                v.tensor_tensor(out=yt[:, j, :][:, ::-1], in0=yt[:, j, :][:, ::-1],
                                                in1=pr[:, ::-1], op=ALU.add)
                                if j == 1:
                                    v.drain()
                                    ins = v.tensor_copy(ctx.sb["st0"][:, 62:63], ctx.sb["cst"][:, C_ZERO:C_ZERO + 1])
                                    ctx.inc(ins, s_y2, 2)
        @blk.sync
        def _(sy):
            for grp in range(8):
                for j in range(2):
                    ct = grp * 2 + j
                    ctx.wge(sy, s_y, (grp + 1) * 30)
                    ctx.wge(sy, s_y2, grp * 2 + j + 1)
                    ins = sy.dma_start(out=y_ap[:, ct, :], in_=yt[:, j, :])
                    ctx.inc(ins, s_st, 16)
            ctx.wge(sy, s_st, 16 * 16)
            ctx.bar_inc(sy)
    ctx.stage_done()


def dve_count_scan(si):
    return (si + 1) * 2


def emit_ffnh(ctx):
    nc = ctx.nc
    w1 = ctx.sb["pb0"][:].rearrange("p a b -> p (a b)").rearrange("p (c m) -> p c m", c=NDC)
    xch = ctx.sb["pb1"]
    vtt = ctx.carve(0, [128, NDC, 16], BF16)
    utt = ctx.carve(512, [16, 4096], BF16)
    fbt = ctx.carve(8704, [16, 16], BF16)
    g16 = ctx.carve(8768, [16, 1], F32)
    bcs = ctx.sb["bcs"]
    l1b = ctx.sb["bcs"][0:16, 0, 3584:4096]
    l2g = ctx.sb["bcs"][0:16, 1, 3584:4096]
    relb = ctx.sb["pf1"]
    pp = ctx.sb["pp0"]
    pp1 = ctx.sb["pp1"]
    s_w, s_ld, s_l1, s_l1e, s_l2, s_l2e, s_mm, s_re, s_sq, s_st = [f"fh{x}" for x in
        ("w", "l", "a", "b", "c", "d", "m", "r", "q", "s")]
    with nc.Block("ffnh") as blk:
        @blk.gpsimd
        def _(g):
            ctx.wait_sb(g)
            ctx.inc(g.dma_start(out=w1[:, :, :], in_=ctx.aps["w1T"]), s_w, 16)
            ctx.inc(g.dma_start(out=vtt[:], in_=ctx.aps["vTT"]), s_w, 16)
            ctx.inc(g.dma_start(out=utt[:], in_=ctx.aps["uTT"].rearrange("a one b -> a (one b)")), s_w, 16)
            ctx.inc(g.dma_start(out=fbt[:], in_=ctx.aps["fbT"].rearrange("a one b -> a (one b)")), s_w, 16)
            ctx.inc(g.dma_start(out=g16[:], in_=ctx.aps["L0_ffn_gate"].unsqueeze(1).partition_broadcast(16)), s_w, 16)
            for ch in range(8):
                if ch >= 2:
                    ctx.wge(g, s_mm, (ch - 1) * 32)
                ctx.inc(g.dma_start(out=xch[:, ch % 2, :].rearrange("p (c t) -> p c t", c=NDC),
                                    in_=ctx.aps["xnf"][:, :, ch * 512:(ch + 1) * 512]), s_ld, 16)
        @blk.tensor
        def _(t):
            ctx.wait_sb(t)
            ctx.wge(t, s_w, 80)
            for ch in range(8):
                ctx.wge(t, s_ld, (ch + 1) * 16)
                xv = xch[:, ch % 2, :].rearrange("p (c t) -> p c t", c=NDC)
                if ch >= 1:
                    ctx.wge(t, s_l2e, ch)
                for kc in range(NDC):
                    ins = t.matmul(pp1[0:16, 0, :], vtt[:, kc, :], xv[:, kc, :],
                                   start=(kc == 0), stop=(kc == NDC - 1))
                ctx.inc(ins, s_l1, 1)
                ctx.wge(t, s_l1e, ch + 1)
                ins = t.matmul(pp1[0:16, 1, :], fbt[:], l1b[:], start=True, stop=True)
                ctx.inc(ins, s_l2, 1)
                ctx.wge(t, s_l2e, ch + 1)
                for ht in range(32):
                    k = ch * 32 + ht
                    if k >= 4:
                        ctx.wge(t, s_re, k - 3)
                    for kc in range(NDC):
                        t.matmul(pp[:, k % 4, :], w1[:, kc, ht * 128:(ht + 1) * 128], xv[:, kc, :],
                                 start=(kc == 0), stop=False)
                    ins = t.matmul(pp[:, k % 4, :], utt[0:16, ht * 128:(ht + 1) * 128], l2g[:],
                                   start=False, stop=True)
                    ctx.inc(ins, s_mm, 1)
        @blk.vector
        def _(v):
            ctx.wait_sb(v)
            for ch in range(8):
                ctx.wge(v, s_l1, ch + 1)
                ctx.inc(v.tensor_copy(l1b[:], pp1[0:16, 0, :]), s_l1e, 1)
                ctx.wge(v, s_l2, ch + 1)
                ctx.inc(v.tensor_scalar(out=l2g[:], in0=pp1[0:16, 1, :], scalar1=g16[:, 0:1],
                                        scalar2=None, op0=ALU.mult), s_l2e, 1)
                for ht in range(32):
                    k = ch * 32 + ht
                    ctx.wge(v, s_mm, k + 1)
                    if k >= 2:
                        ctx.wge(v, s_sq, k - 1)
                    ins = v.tensor_relu(relb[:, (k % 2) * 2048:(k % 2) * 2048 + 512], pp[:, k % 4, :])
                    ctx.inc(ins, s_re, 1)
        @blk.scalar
        def _(s):
            ctx.wait_sb(s)
            for k in range(256):
                ctx.wge(s, s_re, k + 1)
                if k >= 4:
                    ctx.wge(s, s_st, (k - 3) * 16)
                ins = s.activation(out=bcs[:, 0, (k % 4) * 512:(k % 4 + 1) * 512],
                                   in_=relb[:, (k % 2) * 2048:(k % 2) * 2048 + 512], func=AF.Square)
                ctx.inc(ins, s_sq, 1)
        @blk.sync
        def _(sy):
            for k in range(256):
                ch, ht = divmod(k, 32)
                ctx.wge(sy, s_sq, k + 1)
                ins = sy.dma_start(out=ctx.aps["r2l0"][:, ht, ch * 512:(ch + 1) * 512],
                                   in_=bcs[:, 0, (k % 4) * 512:(k % 4 + 1) * 512])
                ctx.inc(ins, s_st, 16)
            ctx.wge(sy, s_st, 256 * 16)
            ctx.bar_inc(sy)
    ctx.stage_done()


def emit_l0out_mamba(ctx):
    nc = ctx.nc
    w = ctx.sb["pb0"][:].rearrange("p a b -> p (a b)")
    yb = ctx.sb["pb1"]
    aux = ctx.sb["pf0"][:].rearrange("p a b -> p (a b)").bitcast(BF16)
    dsb = ctx.carve(0, [128, 32], BF16)
    fbr = ctx.carve(64, [1, 1024], BF16)
    ev = ctx.sb["pf1"]
    pp1 = ctx.sb["pp1"]
    s_w, s_ld, s_yp, s_mm, s_ev, s_st = [f"lm{x}" for x in "wlpmes"]
    with nc.Block("l0outm") as blk:
        @blk.gpsimd
        def _(g):
            ctx.wait_sb(g)
            ctx.inc(g.dma_start(out=w[:, 0:16384].rearrange("p (c m) -> p c m", c=NCC), in_=ctx.aps["wtlT_f"]), s_w, 16)
            ctx.inc(g.dma_start(out=w[:, 16384:32768].rearrange("p (c m) -> p c m", c=NCC), in_=ctx.aps["wtlT_b"]), s_w, 16)
            ctx.inc(g.dma_start(out=dsb[:, 0:16], in_=ctx.aps["L0_attn_fwd_D"].rearrange("(c p) -> p c", p=128), allow_slow_non_contiguous=True), s_w, 16)
            ctx.inc(g.dma_start(out=dsb[:, 16:32], in_=ctx.aps["L0_attn_bwd_D"].rearrange("(c p) -> p c", p=128), allow_slow_non_contiguous=True), s_w, 16)
            ctx.inc(g.dma_start(out=fbr[:], in_=ctx.aps["L0_attn_fusion_b"].rearrange("(one e) -> one e", one=1)), s_w, 16)
            for pr in range(16):
                if pr >= 1:
                    ctx.wge(g, s_yp, pr * 4)
                    ctx.wge(g, s_mm, pr * 4)
                t0 = pr * 256
                for di, d in enumerate("fb"):
                    ctx.inc(g.dma_start(out=aux[:, di * 4096:(di + 1) * 4096].rearrange("p (c t) -> p c t", c=NCC),
                                        in_=ctx.aps[f"y_{d}"][:, :, t0:t0 + 256]), s_ld, 16)
                    ctx.inc(g.dma_start(out=yb[:, di, :].rearrange("p (c t) -> p c t", c=NCC),
                                        in_=ctx.aps[f"u_{d}"][:, :, t0:t0 + 256]), s_ld, 16)
                    ctx.inc(g.dma_start(out=aux[:, 8192 + di * 4096:8192 + (di + 1) * 4096].rearrange("p (c t) -> p c t", c=NCC),
                                        in_=ctx.aps[f"zg_{d}"][:, :, t0:t0 + 256]), s_ld, 16)
        @blk.vector
        def _(v):
            ctx.wait_sb(v)
            ctx.wge(v, s_w, 80)
            for pr in range(16):
                ctx.wge(v, s_ld, (pr + 1) * 96)
                for di in range(2):
                    uv = yb[:, di, :].rearrange("p (c t) -> p c t", c=NCC)
                    yv = aux[:, di * 4096:(di + 1) * 4096].rearrange("p (c t) -> p c t", c=NCC)
                    zv = aux[:, 8192 + di * 4096:8192 + (di + 1) * 4096].rearrange("p (c t) -> p c t", c=NCC)
                    dcol = dsb[:, di * 16:di * 16 + 16]
                    v.tensor_tensor(out=uv[:, :, :], in0=uv[:, :, :],
                                    in1=dcol.unsqueeze(2).broadcast_to([128, 16, 256]), op=ALU.mult)
                    v.tensor_tensor(out=uv[:, :, :], in0=uv[:, :, :], in1=yv[:, :, :], op=ALU.add)
                    ins = v.tensor_tensor(out=uv[:, :, :], in0=uv[:, :, :], in1=zv[:, :, :], op=ALU.mult)
                    ctx.inc(ins, s_yp, 2)
        @blk.tensor
        def _(t):
            ctx.wait_sb(t)
            for pr in range(16):
                ctx.wge(t, s_yp, (pr + 1) * 4)
                for sub in range(2):
                    k = pr * 2 + sub
                    if k >= 1:
                        ctx.wge(t, s_ev, k)
                    for nh in range(2):
                        first = True
                        for di in range(2):
                            yv = yb[:, di, :].rearrange("p (c t) -> p c t", c=NCC)
                            for kc in range(NCC):
                                t.matmul(pp1[:, nh, :], yv[:, kc, sub * 128:(sub + 1) * 128],
                                         w[:, di * 16384:(di + 1) * 16384].rearrange("p (c m) -> p c m", c=NCC)[:, kc, nh * 512:(nh + 1) * 512],
                                         start=first, stop=False)
                                first = False
                        ins = t.matmul(pp1[:, nh, :], ctx.sb["ones1"][0:1, 0:128],
                                       fbr[0:1, nh * 512:(nh + 1) * 512], start=False, stop=True)
                    ctx.inc(ins, s_mm, 2)
        @blk.scalar
        def _(s):
            ctx.wait_sb(s)
            for k in range(32):
                ctx.wge(s, s_mm, (k + 1) * 2)
                if k >= 2:
                    ctx.wge(s, s_st, (k - 1) * 16)
                ins = s.activation(out=ev[:, (k % 2) * 1024:(k % 2) * 1024 + 1024],
                                   in_=pp1[:, 0:2, :].rearrange("p a b -> p (a b)"), func=AF.Copy)
                ctx.inc(ins, s_ev, 1)
        @blk.sync
        def _(sy):
            for k in range(32):
                ctx.wge(sy, s_ev, k + 1)
                ins = sy.dma_start(out=ctx.aps["x1a"][k * 128:(k + 1) * 128, :],
                                   in_=ev[:, (k % 2) * 1024:(k % 2) * 1024 + 1024])
                ctx.inc(ins, s_st, 16)
            ctx.wge(sy, s_st, 32 * 16)
            ctx.bar_inc(sy)
    ctx.stage_done()


def emit_l0out_ffn(ctx):
    nc = ctx.nc
    w = ctx.sb["pb0"][:].rearrange("p a b -> p (a b)").rearrange("p (c m) -> p c m", c=32)
    r2 = ctx.sb["pb1"]
    xres = ctx.sb["pf0"]
    ev = ctx.sb["pf1"]
    pp1 = ctx.sb["pp1"]
    s_w, s_ld, s_mm, s_ev, s_st = [f"lf{x}" for x in "wlmes"]
    with nc.Block("l0outf") as blk:
        @blk.gpsimd
        def _(g):
            ctx.wait_sb(g)
            ctx.inc(g.dma_start(out=w[:, :, :], in_=ctx.aps["w2T"]), s_w, 16)
            for pr in range(16):
                t0 = pr * 256
                if pr >= 1:
                    ctx.wge(g, s_mm, pr * 4)
                ctx.inc(g.dma_start(out=r2[:, 0, :].rearrange("p (c t) -> p c t", c=32) if False else
                                    r2[:].rearrange("p a (c t) -> p (a c) t", c=16)[:, :, :],
                                    in_=ctx.aps["r2l0"][:, :, t0:t0 + 256]), s_ld, 16)
                if pr >= 2:
                    ctx.wge(g, s_ev, (pr - 1) * 2)
                ctx.inc(g.dma_start(out=xres[:, 0, :].rearrange("p (h s d) -> p h s d", h=2, s=2)[:, pr % 2],
                                    in_=ctx.aps["x_in"][t0:t0 + 256, :].rearrange("(s p) d -> p s d", p=128)), s_ld, 16)
                ctx.inc(g.dma_start(out=xres[:, 1, :].rearrange("p (h s d) -> p h s d", h=2, s=2)[:, pr % 2],
                                    in_=ctx.aps["x1a"][t0:t0 + 256, :].rearrange("(s p) d -> p s d", p=128)), s_ld, 16)
        @blk.tensor
        def _(t):
            ctx.wait_sb(t)
            ctx.wge(t, s_w, 16)
            for pr in range(16):
                ctx.wge(t, s_ld, (pr + 1) * 48)
                for sub in range(2):
                    k = pr * 2 + sub
                    if k >= 1:
                        ctx.wge(t, s_ev, k)
                    for nh in range(2):
                        for kc in range(32):
                            ins = t.matmul(pp1[:, nh, :],
                                           r2[:].rearrange("p a (c t) -> p (a c) t", c=16)[:, kc, sub * 128:(sub + 1) * 128],
                                           w[:, kc, nh * 512:(nh + 1) * 512],
                                           start=(kc == 0), stop=(kc == 31))
                    ctx.inc(ins, s_mm, 2)
        @blk.vector
        def _(v):
            ctx.wait_sb(v)
            for k in range(32):
                pr, sub = divmod(k, 2)
                ctx.wge(v, s_mm, (k + 1) * 2)
                if k >= 2:
                    ctx.wge(v, s_st, (k - 1) * 16)
                xv = xres[:, 0, :].rearrange("p (h s d) -> p h s d", h=2, s=2)[:, pr % 2, sub]
                av = xres[:, 1, :].rearrange("p (h s d) -> p h s d", h=2, s=2)[:, pr % 2, sub]
                v.tensor_add(ev[:, (k % 2) * 1024:(k % 2) * 1024 + 1024],
                             pp1[:, 0:2, :].rearrange("p a b -> p (a b)"), xv)
                ins = v.tensor_add(ev[:, (k % 2) * 1024:(k % 2) * 1024 + 1024],
                                   ev[:, (k % 2) * 1024:(k % 2) * 1024 + 1024], av)
                ctx.inc(ins, s_ev, 1)
        @blk.sync
        def _(sy):
            for k in range(32):
                ctx.wge(sy, s_ev, k + 1)
                ins = sy.dma_start(out=ctx.aps["x1"][k * 128:(k + 1) * 128, :],
                                   in_=ev[:, (k % 2) * 1024:(k % 2) * 1024 + 1024])
                ctx.inc(ins, s_st, 16)
            ctx.wge(sy, s_st, 32 * 16)
            ctx.bar_inc(sy)
    ctx.stage_done()


def L(fn, *args, **kw):
    return lambda ctx: fn(ctx, *args, **kw)


def ST(name, fn, reads, writes, aliases=None):
    return dict(name=name, fn=fn, reads=list(reads), writes=list(writes), aliases=aliases or {})


PREP_STAGES = [
    ST("tern1", L(emit_ternarize, "t1", "L0_ffn_w_fc", 4096, 1024, "weff1", C_INVD), ["L0_ffn_w_fc"], ["weff1"]),
    ST("tern2", L(emit_ternarize, "t2", "L0_ffn_w_proj", 1024, 4096, "weff2", C_INV4096), ["L0_ffn_w_proj"], ["weff2"]),
    ST("tpw1", L(emit_transpose_mat, "w1", "weff1", 4096, 1024, "w1T"), ["weff1"], ["w1T"]),
    ST("tpw2", L(emit_transpose_mat, "w2", "weff2", 1024, 4096, "w2T"), ["weff2"], ["w2T"]),
    ST("tpwinf", L(emit_transpose_mat, "wif", "L0_attn_fwd_in_proj_w", 4096, 1024, "winT_f"), ["L0_attn_fwd_in_proj_w"], ["winT_f"]),
    ST("tpwinb", L(emit_transpose_mat, "wib", "L0_attn_bwd_in_proj_w", 4096, 1024, "winT_b"), ["L0_attn_bwd_in_proj_w"], ["winT_b"]),
    ST("tpfus", L(emit_transpose_mat, "fu", "L0_attn_fusion_w", 1024, 2048, "fusT"), ["L0_attn_fusion_w"], ["fusT"]),
    ST("wtlf", L(emit_wtl, "f", "L0_attn_fwd_out_proj_w", "wtlT_f"),
       ["fusT", "L0_attn_fwd_out_proj_w"], ["wtlT_f"]),
    ST("wtlb", L(emit_wtl, "b", "L0_attn_bwd_out_proj_w", "wtlT_b"),
       ["fusT", "L0_attn_bwd_out_proj_w"], ["wtlT_b"]),
    ST("tpxpf", L(emit_transpose_mat, "xf", "L0_attn_fwd_x_proj_w", 96, 2048, "wxpT_f"), ["L0_attn_fwd_x_proj_w"], ["wxpT_f"]),
    ST("tpxpb", L(emit_transpose_mat, "xb", "L0_attn_bwd_x_proj_w", 96, 2048, "wxpT_b"), ["L0_attn_bwd_x_proj_w"], ["wxpT_b"]),
    ST("tpdtf", L(emit_transpose_mat, "df", "L0_attn_fwd_dt_proj_w", 2048, 64, "wdtT_f"), ["L0_attn_fwd_dt_proj_w"], ["wdtT_f"]),
    ST("tpdtb", L(emit_transpose_mat, "db", "L0_attn_bwd_dt_proj_w", 2048, 64, "wdtT_b"), ["L0_attn_bwd_dt_proj_w"], ["wdtT_b"]),
    ST("tpv", L(emit_transpose_mat, "vv", "L0_ffn_v", 16, 1024, "vTT"), ["L0_ffn_v"], ["vTT"]),
    ST("tpu", L(emit_transpose_mat, "uu", "L0_ffn_u", 4096, 16, "uTT"), ["L0_ffn_u"], ["uTT"]),
    ST("tpfb", L(emit_transpose_mat, "fb", "L0_ffn_fast_b", 16, 16, "fbT"), ["L0_ffn_fast_b"], ["fbT"]),
]

L0_STAGES = [
    ST("ln0", L(emit_ln, "l0", "x_in", dst_bf="xnf"), ["x_in"], ["xnf"]),
    ST("ipf", L(emit_inproj, "f"), ["winT_f", "xnf"], ["xs_f", "zg_f"]),
    ST("ipb", L(emit_inproj, "b"), ["winT_b", "xnf"], ["xs_b", "zg_b"]),
    ST("cvf", L(emit_conv, "f"), ["xs_f", "L0_attn_fwd_conv_w", "L0_attn_fwd_conv_b"], ["u_f"],
       {"conv_w": "L0_attn_fwd_conv_w", "conv_b": "L0_attn_fwd_conv_b"}),
    ST("cvb", L(emit_conv, "b"), ["xs_b", "L0_attn_bwd_conv_w", "L0_attn_bwd_conv_b"], ["u_b"],
       {"conv_w": "L0_attn_bwd_conv_w", "conv_b": "L0_attn_bwd_conv_b"}),
    ST("xdf", L(emit_xdbl, "f"), ["u_f", "wxpT_f", "wdtT_f", "L0_attn_fwd_dt_proj_b"],
       ["dl_f", "du_f", "bc_f"], {"dt_proj_b": "L0_attn_fwd_dt_proj_b"}),
    ST("xdb", L(emit_xdbl, "b"), ["u_b", "wxpT_b", "wdtT_b", "L0_attn_bwd_dt_proj_b"],
       ["dl_b", "du_b", "bc_b"], {"dt_proj_b": "L0_attn_bwd_dt_proj_b"}),
    ST("snf", L(emit_scan, "f"), ["dl_f", "du_f", "bc_f", "L0_attn_fwd_A_log"], ["y_f"],
       {"A_log": "L0_attn_fwd_A_log"}),
    ST("snb", L(emit_scan, "b"), ["dl_b", "du_b", "bc_b", "L0_attn_bwd_A_log"], ["y_b"],
       {"A_log": "L0_attn_bwd_A_log"}),
    ST("ffh", L(emit_ffnh), ["w1T", "xnf", "vTT", "uTT", "fbT", "L0_ffn_gate"], ["r2l0"]),
    ST("lom", L(emit_l0out_mamba),
       ["wtlT_f", "wtlT_b", "y_f", "y_b", "u_f", "u_b", "zg_f", "zg_b",
        "L0_attn_fwd_D", "L0_attn_bwd_D", "L0_attn_fusion_b"], ["x1a"]),
    ST("lof", L(emit_l0out_ffn), ["w2T", "r2l0", "x_in", "x1a"], ["x1"]),
]

INPUT_F32 = True


def _build_and_run(stages, state, n_cores=8):
    produced = set()
    reads, writes = [], []
    for st in stages:
        for r in st["reads"]:
            if r not in produced and r not in reads:
                reads.append(r)
        for w in st["writes"]:
            if w not in writes:
                writes.append(w)
            produced.add(w)
    ext_in = [r for r in reads if r not in writes]

    def build(es, kinds):
        nc = bass.Bass("TRN2", target_bir_lowering=False, detect_race_conditions=False)
        ctx = Ctx(nc, es, kinds=kinds)
        for name in ext_in:
            if name in SCRATCH:
                shape, dt = SCRATCH[name]
            else:
                shape, dt = list(state[0][name].shape), F32
            ctx.aps[name] = nc.dram_tensor(name, shape, dt, kind="ExternalInput").ap()
        for name in writes:
            shape, dt = SCRATCH[name]
            ctx.aps[name] = nc.dram_tensor(name, shape, dt, kind="ExternalOutput").ap()
        ctx.aps["__bar"] = nc.dram_tensor("__bar", [1, 128], F32, kind="Internal").ap()
        alloc_pools(ctx)
        emit_init(ctx)
        for st in stages:
            for a, real in st["aliases"].items():
                if not a.startswith("__"):
                    ctx.aps[a] = ctx.aps[real]
            st["fn"](ctx)
        return nc, ctx

    with ExitStack() as es0:
        _, ctx0 = build(es0, None)
        kinds = dict(ctx0.kinds_out)
    with ExitStack() as es:
        nc, ctx = build(es, kinds)
        in_maps = []
        for c in range(n_cores):
            m = {}
            for name in ext_in:
                a = state[c][name]
                if a.dtype != np.float32 and name not in SCRATCH:
                    a = a
                m[name] = a
            in_maps.append(m)
        res = run_bass_kernel_spmd(nc, in_maps, core_ids=list(range(n_cores)))
        for c in range(n_cores):
            for name in writes:
                state[c][name] = res.results[c][name]


GROUPS = None


def kernel(**inputs):
    state = []
    x = np.asarray(inputs["L0_attn_x"])
    for c in range(8):
        d = {"x_in": np.ascontiguousarray(x[c])}
        for k, v in inputs.items():
            if k == "L0_attn_x":
                continue
            a = np.asarray(v)
            if a.ndim == 0:
                a = a.reshape(1)
            d[k] = np.ascontiguousarray(a)
        d.update(_host_consts())
        state.append(d)
    for grp in GROUPS:
        _build_and_run(grp, state)
    return np.stack([state[c]["out"] for c in range(8)]).astype(np.float32)


def emit_cast_f32_bf16(ctx, tag, src_name, dst_name, n_elem_flat):
    nc = ctx.nc
    s_d = tag + "d"
    src = ctx.aps[src_name].rearrange("p a b -> p (a b)")
    dst = ctx.aps[dst_name].rearrange("p a b -> p (a b)")
    nch = 8
    w = n_elem_flat // nch
    with nc.Block("cast" + tag) as blk:
        @blk.gpsimd
        def _(g):
            ctx.wait_sb(g)
            for i in range(nch):
                ctx.inc(g.dma_start(out=dst[:, i * w:(i + 1) * w], in_=src[:, i * w:(i + 1) * w]), s_d, 16)
            ctx.wge(g, s_d, nch * 16)
            ctx.bar_inc(g)
    ctx.stage_done()


def emit_lqkv(ctx):
    nc = ctx.nc
    qkv = ctx.sb["pb0"][:].rearrange("p a b -> p (a b)")[:, 0:12288].rearrange("p (c j) -> p c j", c=NDC)
    xch = ctx.sb["pb1"]
    gsb = ctx.carve(0, [128, NDC], F32)
    bsb = ctx.carve(4096, [128, NDC], F32)
    bwT = ctx.carve(8192, [128, 12], F32)
    bbf = ctx.carve(12288, [128, NDC], BF16)
    evb = ctx.sb["bcs"][:].rearrange("p a b -> p (a b)")
    pp = ctx.sb["pp0"]
    ppt = ctx.sb["ppt"]
    pp1 = ctx.sb["pp1"]
    idt = ctx.sb["idt"]
    s_w, s_ld, s_bw, s_bwe, s_bb, s_bwc, s_g, s_mm, s_ev, s_tp, s_te, s_st, s_vs = [
        f"lq{x}" for x in ("w", "l", "b", "e", "B", "c", "g", "m", "v", "t", "u", "s", "z")]

    def evac_idx(k):
        return k + 1

    with nc.Block("lqkv") as blk:
        @blk.gpsimd
        def _(g):
            ctx.wait_sb(g)
            ctx.inc(g.dma_start(out=qkv[:, :, :], in_=ctx.aps["L1_attn_light_qkv_w"].rearrange("(c p) j -> p c j", p=128)), s_w, 16)
            ctx.inc(g.dma_start(out=gsb[:], in_=ctx.aps["L1_attn_ln_gamma"].rearrange("(c p) -> p c", p=128),
                                allow_slow_non_contiguous=True), s_w, 16)
            ctx.inc(g.dma_start(out=bsb[:], in_=ctx.aps["L1_attn_ln_beta"].rearrange("(c p) -> p c", p=128),
                                allow_slow_non_contiguous=True), s_w, 16)
            for ch in range(8):
                if ch >= 2:
                    ctx.wge(g, s_mm, (ch - 1) * 12)
                ctx.inc(g.dma_start(out=xch[:, ch % 2, :].rearrange("p (c t) -> p c t", c=NDC),
                                    in_=ctx.aps["xnlf"][:, :, ch * 512:(ch + 1) * 512]), s_ld, 16)
                for m in range(12):
                    k = ch * 12 + m
                    ctx.wge(g, s_ev, k + 1)
                    ins = g.dma_start(out=ctx.aps["qkvsc"][:, m, ch * 512:(ch + 1) * 512],
                                      in_=evb[:, (k % 12) * 512:(k % 12 + 1) * 512])
                    ctx.inc(ins, s_st, 16)
                for kt in range(ch * 16, ch * 16 + 16):
                    ctx.wge(g, s_te, kt + 1)
                    m8, b = divmod(kt % 16, 4)
                    ins = g.dma_start(out=ctx.aps["vtm"][ch * 512 + b * 128:ch * 512 + (b + 1) * 128,
                                                         m8 * 128:(m8 + 1) * 128],
                                      in_=evb[:, 6144 + (kt % 4) * 128:6144 + (kt % 4 + 1) * 128])
                    ctx.inc(ins, s_vs, 16)
            ctx.wge(g, s_st, ctx.val(s_st))
            ctx.wge(g, s_vs, ctx.val(s_vs))
            ctx.bar_inc(g)
        @blk.vector
        def _(v):
            ctx.wait_sb(v)
            ctx.wge(v, s_w, 48)
            ins = v.tensor_copy(bbf[:], bsb[:])
            ctx.inc(ins, s_g, 1)
            for nh in range(3):
                ctx.wge(v, s_bw, nh + 1)
                ins = v.tensor_copy(ctx.sb["pf1"][0:1, nh * 512:(nh + 1) * 512], pp1[0:1, 0, 0:512])
                ctx.inc(ins, s_bwe, 1)
            for kc in range(NDC):
                ins = v.tensor_scalar(out=qkv[:, kc, :], in0=qkv[:, kc, :], scalar1=gsb[:, kc:kc + 1],
                                      scalar2=None, op0=ALU.mult)
            ctx.inc(ins, s_g, 1)
            ctx.wge(v, s_bwc, 1)
            for ch in range(8):
                for m in range(12):
                    k = ch * 12 + m
                    ctx.wge(v, s_mm, k + 1)
                    if k >= 12:
                        kp = k - 12
                        ctx.wge(v, s_st, (kp + 1) * 16)
                        if kp % 12 >= 8:
                            ctx.wge(v, s_tp, (kp // 12) * 16 + (kp % 12 - 8 + 1) * 4)
                    ins = v.tensor_scalar(out=evb[:, (k % 12) * 512:(k % 12 + 1) * 512], in0=pp[:, k % 4, :],
                                          scalar1=bwT[:, m:m + 1], scalar2=None, op0=ALU.add)
                    ctx.inc(ins, s_ev, 1)
                for kt in range(ch * 16, ch * 16 + 16):
                    ctx.wge(v, s_tp, kt + 1)
                    if kt >= 4:
                        ctx.wge(v, s_vs, (kt - 3) * 16)
                    ins = v.tensor_copy(evb[:, 6144 + (kt % 4) * 128:6144 + (kt % 4 + 1) * 128],
                                        ppt[:, kt % 2, 0:128])
                    ctx.inc(ins, s_te, 1)
        @blk.tensor
        def _(t):
            ctx.wait_sb(t)
            ctx.wge(t, s_w, 48)
            ctx.wge(t, s_g, 1)
            for nh in range(3):
                if nh:
                    ctx.wge(t, s_bwe, nh)
                for kc in range(NDC):
                    ins = t.matmul(pp1[0:1, 0, 0:512], bbf[:, kc:kc + 1],
                                   qkv[:, kc, nh * 512:(nh + 1) * 512], start=(kc == 0), stop=(kc == NDC - 1))
                ctx.inc(ins, s_bw, 1)
            ctx.wge(t, s_g, 2)
            ctx.wge(t, s_bwc, 1)
            for ch in range(8):
                ctx.wge(t, s_ld, (ch + 1) * 16)
                for m in range(12):
                    k = ch * 12 + m
                    if k >= 4:
                        ctx.wge(t, s_ev, k - 3)
                    for kc in range(NDC):
                        ins = t.matmul(pp[:, k % 4, :], qkv[:, kc, m * 128:(m + 1) * 128],
                                       xch[:, ch % 2, :].rearrange("p (c t) -> p c t", c=NDC)[:, kc, :],
                                       start=(kc == 0), stop=(kc == NDC - 1))
                    ctx.inc(ins, s_mm, 1)
                for kt in range(ch * 16, ch * 16 + 16):
                    m8, b = divmod(kt % 16, 4)
                    ctx.wge(t, s_ev, ch * 12 + 8 + m8 + 1)
                    if kt >= 2:
                        ctx.wge(t, s_te, kt - 1)
                    ins = t.transpose(ppt[:, kt % 2, 0:128],
                                      evb[:, ((ch * 12 + 8 + m8) % 12) * 512 + b * 128:
                                          ((ch * 12 + 8 + m8) % 12) * 512 + (b + 1) * 128], idt[:])
                    ctx.inc(ins, s_tp, 1)
        @blk.scalar
        def _(s):
            pass
        @blk.sync
        def _(sy):
            ctx.wait_sb(sy)
            ctx.wge(sy, s_bwe, 3)
            ctx.inc(sy.dma_start(out=ctx.aps["bwrow"], in_=ctx.sb["pf1"][0:1, 0:1536]), s_bb, 16)
            ctx.wge(sy, s_bb, 16)
            ctx.inc(sy.dma_start(out=bwT[:], in_=ctx.aps["bwrow"].rearrange("one (m p) -> (one p) m", p=128),
                                 allow_slow_non_contiguous=True), s_bb, 16)
            ctx.wge(sy, s_bb, 32)
            ctx.eng_inc(sy, s_bwc, 1)
    ctx.stage_done()


def emit_lattn(ctx):
    nc = ctx.nc
    pb1f = ctx.sb["pb1"][:].rearrange("p a b -> p (a b)")
    bcsf = ctx.sb["bcs"][:].rearrange("p a b -> p (a b)")
    pf0f = ctx.sb["pf0"][:].rearrange("p a b -> p (a b)")
    masks = ctx.carve(0, [128, 3, 384], F32)
    o2 = ctx.carve(4608, [128, 2, 512], BF16)
    low = ctx.carve(6656, [128, 4, 1024], BF16)
    ev = ctx.sb["pf1"]
    st0 = ctx.sb["st0"]
    cst = ctx.sb["cst"]
    pp = ctx.sb["pp0"]
    pp1 = ctx.sb["pp1"]
    ppt = ctx.sb["ppt"]
    idt = ctx.sb["idt"]
    s_w, s_ld, s_mm, s_sf, s_ex, s_ab, s_tp, s_te, s_av, s_oe, s_om, s_fe, s_st = [
        f"la{x}" for x in "wlmsxbtuavofq"]
    SLOT = 3584

    def mtype(tt):
        return 0 if tt == 0 else (2 if tt == 31 else 1)

    with nc.Block("lattn") as blk:
        @blk.gpsimd
        def _(g):
            ctx.wait_sb(g)
            ctx.inc(g.dma_start(out=masks[:], in_=ctx.aps["lmask"].rearrange("a p b -> p a b")), s_w, 16)
            ctx.inc(g.dma_start(out=low[:], in_=ctx.aps["L1_attn_light_out_w"].rearrange("(c p) e -> p c e", p=128)), s_w, 16)
            for tt in range(32):
                t0 = tt * 128
                sl = (tt % 2) * SLOT
                if tt >= 2:
                    ctx.wge(g, s_om, tt - 1)
                ctx.inc(g.dma_start(out=pb1f[:, sl:sl + 512].rearrange("p (c t) -> p c t", c=4),
                                    in_=ctx.aps["qkvsc"][:, 0:4, t0:t0 + 128]), s_ld, 16)
                klo, khi = max(0, t0 - 128), min(T, t0 + 256)
                off = klo - (t0 - 128)
                if tt == 0:
                    g.memset(pb1f[:, sl + 512:sl + 2048], 0.0)
                if tt == 31:
                    g.memset(pb1f[:, sl + 512:sl + 2048], 0.0)
                ins = g.dma_start(out=pb1f[:, sl + 512:sl + 2048].rearrange("p (c t) -> p c t", c=4)[:, :, off:off + (khi - klo)],
                                  in_=ctx.aps["qkvsc"][:, 4:8, klo:khi])
                ctx.inc(ins, s_ld, 16)
                if tt == 0:
                    g.memset(pb1f[:, sl + 2048:sl + 3584], 0.0)
                if tt == 31:
                    g.memset(pb1f[:, sl + 2048:sl + 3584], 0.0)
                nct_v = (khi - klo) // 128
                coff = off // 128
                ins = g.dma_start(out=pb1f[:, sl + 2048:sl + 3584].rearrange("p (c t) -> p c t", c=3)[:, coff:coff + nct_v, :],
                                  in_=ctx.aps["vtm"][klo:khi, :].rearrange("(c p) v -> p c v", p=128))
                ctx.inc(ins, s_ld, 16)
        @blk.tensor
        def _(t):
            ctx.wait_sb(t)
            ctx.wge(t, s_w, 32)
            ktp = 0
            def outproj(tt2):
                ctx.wge(t, s_oe, tt2 * 8 + 8)
                if tt2 >= 1:
                    ctx.wge(t, s_fe, tt2)
                for nh in range(2):
                    for c in range(4):
                        ins = t.matmul(pp[:, 2 + nh, :], o2[:, tt2 % 2, c * 128:(c + 1) * 128],
                                       low[:, c, nh * 512:(nh + 1) * 512], start=(c == 0), stop=(c == 3))
                ctx.inc(ins, s_om, 1)
            for tt in range(32):
                sl = (tt % 2) * SLOT
                ctx.wge(t, s_ld, (tt + 1) * 48)
                for h in range(8):
                    k = tt * 8 + h
                    if k >= 2:
                        ctx.wge(t, s_sf, k - 1)
                    ins = t.matmul(pp[:, h % 2, 0:384],
                                   pb1f[:, sl:sl + 512].rearrange("p (c t) -> p c t", c=4)[64 * (h % 2):64 * (h % 2) + 64, h // 2, :],
                                   pb1f[:, sl + 512:sl + 2048].rearrange("p (c t) -> p c t", c=4)[64 * (h % 2):64 * (h % 2) + 64, h // 2, :],
                                   start=True, stop=True)
                    ctx.inc(ins, s_mm, 1)
                    for c in range(3):
                        ctx.wge(t, s_ab, k + 1)
                        if ktp >= 2:
                            ctx.wge(t, s_te, ktp - 1)
                        ins = t.transpose(ppt[:, ktp % 2, 0:128],
                                          bcsf[:, (k % 2) * 384:(k % 2) * 384 + 384][:, c * 128:(c + 1) * 128], idt[:])
                        ctx.inc(ins, s_tp, 1)
                        ktp += 1
                    ctx.wge(t, s_te, ktp)
                    if k >= 2:
                        ctx.wge(t, s_oe, k - 1)
                    for c in range(3):
                        ins = t.matmul(pp1[0:64, k % 2, 0:128],
                                       pb1f[:, sl + 2048:sl + 3584].rearrange("p (c t) -> p c t", c=3)[:, c, h * 64:(h + 1) * 64],
                                       bcsf[:, 768 + (k % 2) * 384 + c * 128:768 + (k % 2) * 384 + (c + 1) * 128],
                                       start=(c == 0), stop=(c == 2))
                    ctx.inc(ins, s_av, 1)
                    if h == 1 and tt >= 1:
                        outproj(tt - 1)
            outproj(31)
        @blk.vector
        def _(v):
            ctx.wait_sb(v)
            def fevac(ttv):
                ctx.wge(v, s_om, ttv + 1)
                if ttv >= 2:
                    ctx.wge(v, s_st, (ttv - 1) * 16)
                ins = v.tensor_copy(ev[:, (ttv % 2) * 1024:(ttv % 2) * 1024 + 1024],
                                    pp[:, 2:4, :].rearrange("p a b -> p (a b)"))
                ctx.inc(ins, s_fe, 1)
            for tt in range(32):
                mt = mtype(tt)
                for h in range(8):
                    k = tt * 8 + h
                    ctx.wge(v, s_mm, k + 1)
                    ins = v.scalar_tensor_tensor(out=pf0f[:, (k % 2) * 384:(k % 2) * 384 + 384],
                                                 in0=pp[:, h % 2, 0:384], scalar=cst[:, C_QSC:C_QSC + 1],
                                                 in1=masks[:, mt, :], op0=ALU.mult, op1=ALU.add)
                    v.reduce_max(st0[:, h % 4:h % 4 + 1], pf0f[:, (k % 2) * 384:(k % 2) * 384 + 384],
                                 axis=AX.X, negate=True)
                    ctx.inc(ins, s_sf, 1)
                    ctx.wge(v, s_ex, k + 1)
                    v.drain()
                    v.reciprocal(st0[:, 8 + h % 4:9 + h % 4], st0[:, 16 + h % 4:17 + h % 4])
                    v.drain()
                    ins = v.tensor_scalar(out=bcsf[:, (k % 2) * 384:(k % 2) * 384 + 384],
                                          in0=pf0f[:, 768 + (k % 2) * 384:768 + (k % 2) * 384 + 384],
                                          scalar1=st0[:, 8 + h % 4:9 + h % 4], scalar2=None, op0=ALU.mult)
                    ctx.inc(ins, s_ab, 1)
                    for c in range(3):
                        kt3 = k * 3 + c
                        ctx.wge(v, s_tp, kt3 + 1)
                        ins = v.tensor_copy(bcsf[:, 768 + (k % 2) * 384 + c * 128:768 + (k % 2) * 384 + (c + 1) * 128],
                                            ppt[:, kt3 % 2, 0:128])
                        ctx.inc(ins, s_te, 1)
                    if h == 2 and tt >= 1:
                        fevac(tt - 1)
            fevac(31)
        @blk.scalar
        def _(s):
            ctx.wait_sb(s)
            def oevac(j):
                tt2, h2 = divmod(j, 8)
                ctx.wge(s, s_av, j + 1)
                if h2 == 0 and tt2 >= 2:
                    ctx.wge(s, s_om, tt2 - 1)
                ins = s.activation(out=o2[64 * (h2 % 2):64 * (h2 % 2) + 64, tt2 % 2,
                                          (h2 // 2) * 128:(h2 // 2) * 128 + 128],
                                   in_=pp1[0:64, j % 2, 0:128], func=AF.Copy)
                ctx.inc(ins, s_oe, 1)
            for tt in range(32):
                for h in range(8):
                    k = tt * 8 + h
                    ctx.wge(s, s_sf, k + 1)
                    ins = s.activation(out=pf0f[:, 768 + (k % 2) * 384:768 + (k % 2) * 384 + 384],
                                       in_=pf0f[:, (k % 2) * 384:(k % 2) * 384 + 384],
                                       func=AF.Exp, bias=st0[:, h % 4:h % 4 + 1], scale=1.0,
                                       accum_out=st0[:, 16 + h % 4:17 + h % 4])
                    ctx.inc(ins, s_ex, 1)
                    if k >= 2:
                        oevac(k - 2)
            oevac(254)
            oevac(255)
        @blk.sync
        def _(sy):
            ctx.wait_sb(sy)
            for tt in range(32):
                ctx.wge(sy, s_fe, tt + 1)
                ins = sy.dma_start(out=ctx.aps["lighto"][tt * 128:(tt + 1) * 128, :],
                                   in_=ev[:, (tt % 2) * 1024:(tt % 2) * 1024 + 1024])
                ctx.inc(ins, s_st, 16)
            ctx.wge(sy, s_st, 32 * 16)
            ctx.bar_inc(sy)
    ctx.stage_done()


DEBUG_ROUTE = False
DISABLE_RECYCLE = False
DEBUG_ROUTE_LOOP = False


def emit_route(ctx):
    nc = ctx.nc
    tok2 = ctx.carve(0, [128, NDC, 2], F32)
    xch = ctx.sb["pf0"]
    pf1 = ctx.sb["pf1"]
    ysc = ctx.sb["ysc"]
    sc2 = ctx.sb["bcs"][:].rearrange("p a b -> p (a b)").bitcast(F32)[0:2, 0:4096]
    bb = ctx.carve(17408, [2, 4096], F32) if False else None
    st0 = ctx.sb["st0"]
    cst = ctx.sb["cst"]
    cur1, cur2, curB = ctx.sb["cur1"], ctx.sb["cur2"], ctx.sb["cur3"]
    pp1 = ctx.sb["pp1"]
    pp = ctx.sb["pp0"]
    st1 = ctx.sb["st1"]
    idtf = ctx.sb["idtf"]
    s_w, s_ld, s_mm, s_ev, s_vi, s_si, s_sel, s_rr, s_st = [f"rt{x}" for x in "wlmeabczq"]
    t0v = pf1[0:2, 0:4096]
    bv = ysc[0:2, 0, :]
    sco = ysc[0:2, 1, :]
    junk = ctx.sb["pb1"][0:2, 0, :]
    with nc.Block("route") as blk:
        @blk.gpsimd
        def _(g):
            ctx.wait_sb(g)
            ctx.inc(g.dma_start(out=tok2[:, :, 0:1],
                                in_=ctx.aps["L1_attn_q_route_tok"].rearrange("(c p) -> p c", p=128).unsqueeze(2),
                                allow_slow_non_contiguous=True), s_w, 16)
            ctx.inc(g.dma_start(out=tok2[:, :, 1:2],
                                in_=ctx.aps["L1_attn_kv_route_tok"].rearrange("(c p) -> p c", p=128).unsqueeze(2),
                                allow_slow_non_contiguous=True), s_w, 16)
            for ch in range(8):
                if ch >= 2:
                    ctx.wge(g, s_mm, ch - 1)
                ctx.inc(g.dma_start(out=xch[:, ch % 2, :].rearrange("p (c t) -> p c t", c=NDC),
                                    in_=ctx.aps["xn1f32"][:, :, ch * 512:(ch + 1) * 512]), s_ld, 16)
            ctx.wge(g, s_ld, 8 * 16)
            ctx.wge(g, s_mm, 8)
            g.iota(ctx.sb["pf0"][:, 1, 0:1024], pattern=[[1, 1024]], base=0, channel_multiplier=0,
                   allow_small_or_imprecise_dtypes=True)
            ins = g.iota(ctx.sb["st1"][:, 0:32], pattern=[[128, 32]], base=0, channel_multiplier=1,
                         allow_small_or_imprecise_dtypes=True)
            ctx.inc(ins, "rtio", 1)
            if DEBUG_ROUTE:
                ctx.wge(g, s_sel, 1)
                ctx.inc(g.dma_start(out=ctx.aps["scorerow"], in_=sco[:]), "rtdbq", 16)
                ctx.inc(g.dma_start(out=ctx.aps["rankrow"][0:2, 0:32], in_=st0[0:2, 24:56].bitcast(I32)), "rtdbq", 16)
                ctx.inc(g.dma_start(out=ctx.aps["rankrow"][0:2, 64:4096], in_=sc2[0:2, 64:4096].bitcast(I32)), "rtdbq", 16)
                ctx.wge(g, "rtdbq", 48)
            for r in range(2):
                ctx.wge(g, "rtsr", r + 1)
                dst = ctx.aps["selq"] if r == 0 else ctx.aps["selkv"]
                ins = g.dma_start(out=dst.rearrange("n one -> (one n)").unsqueeze(0),
                                  in_=ctx.sb["pf0"][0:1, 0, 2048 + r * 1024:2048 + (r + 1) * 1024].bitcast(I32))
                ctx.inc(ins, s_st, 16)
            ctx.wge(g, s_st, 2 * 16)
            ctx.bar_inc(g)
        @blk.tensor
        def _(t):
            ctx.wait_sb(t)
            ctx.wge(t, s_w, 32)
            for ch in range(8):
                ctx.wge(t, s_ld, (ch + 1) * 16)
                if ch >= 1:
                    ctx.wge(t, s_ev, ch)
                for kc in range(NDC):
                    ins = t.matmul(pp1[0:2, 0, :], tok2[:, kc, :],
                                   xch[:, ch % 2, :].rearrange("p (c t) -> p c t", c=NDC)[:, kc, :],
                                   start=(kc == 0), stop=(kc == NDC - 1))
                ctx.inc(ins, s_mm, 1)
            ctx.wge(t, s_sel, 1)
            for c in range(32):
                if c >= 2:
                    ctx.wge(t, "rtte", c - 1)
                ins = t.transpose(pp[0:128, c % 2, 0:2], sc2[:, c * 128:(c + 1) * 128], idtf[0:2, 0:2])
                ctx.inc(ins, "rttp", 1)
            ctx.wge(t, "rtio", 1)
            for r in range(2):
                if r == 1:
                    ctx.wge(t, "rtev", 1)
                for ch in range(32):
                    ctx.wge(t, "rtcm", r * 32 + ch + 1)
                    for nh in range(2):
                        ins = t.matmul(pp1[0:1, nh, :], st1[:, ch:ch + 1],
                                       pf1[:, (ch % 2) * 1024 + nh * 512:(ch % 2) * 1024 + (nh + 1) * 512],
                                       start=(ch == 0), stop=(ch == 31))
                    ctx.inc(ins, "rtmm", 1)
        @blk.vector
        def _(v):
            ctx.wait_sb(v)
            for ch in range(8):
                ctx.wge(v, s_mm, ch + 1)
                ctx.inc(v.tensor_copy(sc2[:, ch * 512:(ch + 1) * 512], pp1[0:2, 0, :]), s_ev, 1)
            v.tensor_scalar(out=bv, in0=sc2[:], scalar1=cst[0:2, C_NEG1:C_NEG1 + 1], scalar2=None, op0=ALU.mult)
            for it in range(20):
                v.tensor_tensor(out=t0v, in0=sc2[:], in1=bv, op=ALU.add)
                v.reduce_max(st0[0:2, 0:1], t0v, axis=AX.X, negate=True)
                v.drain()
                ins = v.tensor_scalar(out=st0[0:2, 1:2], in0=st0[0:2, 0:1], scalar1=cur1[0:2, it:it + 1],
                                      scalar2=None, op0=ALU.mult)
                ctx.inc(ins, s_vi, 1)
                ctx.wge(v, s_si, it + 1)
                v.scalar_tensor_tensor(out=st0[0:2, 3:4], in0=st0[0:2, 0:1], scalar=cur1[0:2, it:it + 1],
                                       in1=st0[0:2, 2:3], op0=ALU.mult, op1=ALU.subtract)
                v.drain()
                v.tensor_scalar(out=st0[0:2, 4:5], in0=st0[0:2, 3:4], scalar1=cst[0:2, C_LOGK:C_LOGK + 1],
                                scalar2=curB[0:2, it:it + 1], op0=ALU.add, op1=ALU.mult)
                v.drain()
                if DEBUG_ROUTE or DEBUG_ROUTE_LOOP:
                    v.tensor_copy(st0[0:2, 24 + it:25 + it], st0[0:2, 4:5])
                    v.drain()
                v.tensor_scalar(out=t0v, in0=sc2[:], scalar1=st0[0:2, 4:5], scalar2=None, op0=ALU.add)
                ins = v.tensor_scalar(out=bv, in0=t0v, scalar1=cst[0:2, C_ZERO:C_ZERO + 1],
                                      scalar2=cst[0:2, C_NEG1:C_NEG1 + 1], op0=ALU.max, op1=ALU.mult)
                ctx.inc(ins, s_vi, 1)
            v.tensor_scalar(out=t0v, in0=t0v, scalar1=cst[0:2, C_ZERO:C_ZERO + 1], scalar2=None, op0=ALU.is_ge)
            if DEBUG_ROUTE:
                v.tensor_copy(sco, t0v)
            v.memset(bv, 1.0)
            v.drain()
            ctx.inc(v.tensor_copy(st0[0:2, 60:61], cst[0:2, C_ZERO:C_ZERO + 1]), "rtsc", 1)
            ctx.wge(v, "rtsc", 1)
            v.tensor_tensor_scan(out=sc2[:], data0=bv, data1=t0v, initial=0.0, op0=ALU.mult, op1=ALU.add)
            v.tensor_tensor(out=sc2[:], in0=sc2[:], in1=t0v, op=ALU.subtract)
            v.tensor_scalar(out=bv, in0=sc2[:], scalar1=cst[0:2, C_K1024:C_K1024 + 1], scalar2=None, op0=ALU.is_lt)
            v.tensor_tensor(out=bv, in0=bv, in1=t0v, op=ALU.mult)
            v.tensor_scalar(out=sc2[:], in0=sc2[:], scalar1=cst[0:2, C_BIG:C_BIG + 1], scalar2=None, op0=ALU.add)
            v.scalar_tensor_tensor(out=sc2[:], in0=bv, scalar=cst[0:2, C_BIGN:C_BIGN + 1],
                                   in1=sc2[:], op0=ALU.mult, op1=ALU.add)
            ctx.inc(v.engine_nop(), s_sel, 1)
            rankT = ctx.carve(256, [128, 64], F32)
            for c in range(32):
                ctx.wge(v, "rttp", c + 1)
                ins = v.tensor_copy(rankT[:, 2 * c:2 * c + 2], pp[0:128, c % 2, 0:2])
                ctx.inc(ins, "rtte", 1)
            ctx.wge(v, "rtio", 1)
            for r in range(2):
                for ch in range(32):
                    j = r * 32 + ch
                    if j >= 2:
                        ctx.wge(v, "rtmm", j - 1)
                    ins = v.tensor_tensor(out=pf1[:, (ch % 2) * 1024:(ch % 2) * 1024 + 1024],
                                          in0=rankT[:, 2 * ch + r:2 * ch + r + 1].broadcast_to([128, 1024]),
                                          in1=ctx.sb["pf0"][:, 1, 0:1024], op=ALU.is_equal)
                    ctx.inc(ins, "rtcm", 1)
                ctx.wge(v, "rtmm", (r + 1) * 32)
                v.tensor_copy(ctx.sb["pf0"][0:1, 0, r * 1024:(r + 1) * 1024],
                              pp1[0:1, 0:2, :].rearrange("p a b -> p (a b)"))
                v.drain()
                ins = v.tensor_copy(ctx.sb["pf0"][0:1, 0, 2048 + r * 1024:2048 + (r + 1) * 1024].bitcast(I32),
                                    ctx.sb["pf0"][0:1, 0, r * 1024:(r + 1) * 1024])
                ctx.inc(ins, "rtev", 1)
                ctx.inc(v.engine_nop(), "rtsr", 1)
        @blk.scalar
        def _(s):
            ctx.wait_sb(s)
            for it in range(20):
                ctx.wge(s, s_vi, 2 * it + 1)
                s.activation(out=sco, in_=t0v, func=AF.Exp,
                             scale=cur1[0:2, it:it + 1], bias=st0[0:2, 1:2], accum_out=st0[0:2, 2:3])
                s.drain()
                ins = s.activation(out=st0[0:2, 2:3], in_=st0[0:2, 2:3], func=AF.Ln)
                ctx.inc(ins, s_si, 1)
    ctx.stage_done()


def emit_heavy1(ctx):
    nc = ctx.nc
    pf0 = ctx.sb["pf0"]
    tabs = pf0[0:64, 0, 0:4096].rearrange("p (k n) -> p k n", k=4)
    fsb = pf0[0:32, 1, 0:1024]
    arg = pf0[0:32, 1, 1024:2048]
    red = pf0[0:32, 1, 2048:3072]
    pf1 = ctx.sb["pf1"]
    ki = pf1[0:32, 0:1024].bitcast(I32)
    st1 = ctx.sb["st1"]
    cst = ctx.sb["cst"]
    pp1 = ctx.sb["pp1"]
    s_w, s_mm, s_h = "h1w", "h1m", "h1h"
    with nc.Block("heavy1") as blk:
        @blk.gpsimd
        def _(g):
            ctx.wait_sb(g)
            yi = ctx.sb["ysc"][:, 0, :].bitcast(I32)
            ctx.inc(g.dma_start(out=yi[0:1, 0:1024],
                                in_=ctx.aps["selq"].rearrange("(one n) one2 -> one (n one2)", one=1)), s_w, 16)
            ctx.inc(g.dma_start(out=yi[0:1, 1024:2048],
                                in_=ctx.aps["selkv"].rearrange("(one n) one2 -> one (n one2)", one=1)), s_w, 16)
            ctx.inc(g.dma_start(out=st1[0:1, 0:32], in_=ctx.aps["invf"]), s_w, 16)
        @blk.vector
        def _(v):
            ctx.wait_sb(v)
            ctx.wge(v, s_w, 48)
            yi = ctx.sb["ysc"][:, 0, :].bitcast(I32)
            v.tensor_copy(pf1[0:1, 1024:2048], yi[0:1, 0:1024])
            ins = v.tensor_copy(pf1[0:1, 2048:3072], yi[0:1, 1024:2048])
            ctx.inc(ins, s_h, 1)
            for side in range(2):
                for trig in range(2):
                    ctx.wge(v, s_mm, side * 2 + 2)
                    if trig == 0:
                        v.tensor_copy(fsb[:], pp1[0:32, 0:2, :].rearrange("p a b -> p (a b)"))
                        v.drain()
                    if trig == 0:
                        v.tensor_scalar(out=arg[:], in0=fsb[:], scalar1=cst[0:32, C_HPI:C_HPI + 1],
                                        scalar2=None, op0=ALU.add)
                    else:
                        v.tensor_copy(arg[:], fsb[:])
                    v.tensor_scalar(out=red[:], in0=arg[:], scalar1=cst[0:32, C_INV2PI:C_INV2PI + 1],
                                    scalar2=None, op0=ALU.mult)
                    v.tensor_copy(ki[:], red[:])
                    v.tensor_copy(red[:], ki[:])
                    v.scalar_tensor_tensor(out=red[:], in0=red[:], scalar=cst[0:32, C_N2PI:C_N2PI + 1],
                                           in1=arg[:], op0=ALU.mult, op1=ALU.add)
                    v.tensor_single_scalar(out=ki[:].bitcast(F32) if False else pf1[0:32, 0:1024], in_=red[:], scalar=PI, op=ALU.is_gt)
                    v.scalar_tensor_tensor(out=red[:], in0=pf1[0:32, 0:1024], scalar=cst[0:32, C_N2PI:C_N2PI + 1],
                                           in1=red[:], op0=ALU.mult, op1=ALU.add)
                    ctx.inc(v.engine_nop(), s_h, 1)
                    ctx.wge(v, s_h + "s", side * 2 + trig + 1)
                    ti = side * 2 + trig
                    ins = v.tensor_copy(tabs[32:64, ti, :], tabs[0:32, ti, :])
                    ctx.inc(ins, s_h, 1)
        @blk.tensor
        def _(t):
            ctx.wait_sb(t)
            ctx.wge(t, s_h, 1)
            for side in range(2):
                if side == 1:
                    ctx.wge(t, s_h, 2)
                rhs = pf1[0:1, 1024 + side * 1024:2048 + side * 1024]
                for nh in range(2):
                    ins = t.matmul(pp1[0:32, nh, :], st1[0:1, 0:32], rhs[0:1, nh * 512:(nh + 1) * 512],
                                   start=True, stop=True)
                ctx.inc(ins, s_mm, 2)
        @blk.scalar
        def _(s):
            ctx.wait_sb(s)
            for side in range(2):
                for trig in range(2):
                    k = side * 2 + trig
                    ctx.wge(s, s_h, 1 + k + 1 + k)
                    ins = s.activation(out=tabs[0:32, k, :], in_=red[:], func=AF.Sin)
                    ctx.inc(ins, s_h + "s", 1)
    ctx.stage_done()


def emit_heavy2(ctx, side):
    nc = ctx.nc
    pf1 = ctx.sb["pf1"]
    pb1 = ctx.sb["pb1"][:].rearrange("p a b -> p (a b)")
    bcsf = ctx.sb["bcs"][:].rearrange("p a b -> p (a b)")
    st0 = ctx.sb["st0"]
    cst = ctx.sb["cst"]
    ppt = ctx.sb["ppt"]
    idt = ctx.sb["idt"]
    s_w, s_ld, s_n, s_tp, s_te = [f"h2{side}{x}" for x in "wlntu"]
    selname = "selq" if side == 0 else "selkv"
    with nc.Block(f"heavy2{side}") as blk:
        @blk.gpsimd
        def _(g):
            ctx.wait_sb(g)
            ctx.inc(g.dma_start(out=ctx.sb["sti2"][:, side * 8:side * 8 + 8],
                                in_=ctx.aps[selname].rearrange("(gg p) one -> p (gg one)", p=128),
                                allow_slow_non_contiguous=True), s_w, 16)
            ctx.wge(g, s_w, 16)
            for gg in range(8):
                if gg >= 1:
                    ctx.wge(g, s_n, gg * 2)
                ins = g.indirect_dma_start(
                    out=pf1[:, 0:1024], out_offset=None, in_=ctx.aps["xn1tm"],
                    in_offset=IndirectOffsetOnAxis(ctx.sb["sti2"][:, side * 8 + gg:side * 8 + gg + 1], 0))
                ctx.inc(ins, s_ld, 16)
        @blk.scalar
        def _(s):
            ctx.wait_sb(s)
            for gg in range(8):
                ctx.wge(s, s_ld, (gg + 1) * 16)
                s.activation(out=ctx.sb["ysc"][:, 1, 0:1024], in_=pf1[:, 0:1024],
                             func=AF.Square, accum_out=st0[:, 0:1])
                s.drain()
                s.activation(out=st0[:, 1:2], in_=st0[:, 0:1], func=AF.Ln)
                s.drain()
                ins = s.activation(out=st0[:, 2:3], in_=st0[:, 1:2], func=AF.Exp, scale=cst[:, C_HALF:C_HALF + 1])
                ctx.inc(ins, s_n, 1)
        @blk.vector
        def _(v):
            ctx.wait_sb(v)
            for gg in range(8):
                ctx.wge(v, s_n, gg * 2 + 1)
                v.tensor_scalar(out=st0[:, 3:4], in0=st0[:, 2:3], scalar1=cst[:, C_E12:C_E12 + 1],
                                scalar2=None, op0=ALU.max)
                v.drain()
                v.reciprocal(st0[:, 4:5], st0[:, 3:4])
                v.drain()
                v.tensor_scalar(out=st0[:, 5:6], in0=st0[:, 4:5], scalar1=cst[:, C_SQD:C_SQD + 1],
                                scalar2=None, op0=ALU.mult)
                v.drain()
                ins = v.tensor_scalar(out=pb1[:, gg * 1024:(gg + 1) * 1024], in0=pf1[:, 0:1024],
                                      scalar1=st0[:, 5:6], scalar2=None, op0=ALU.mult)
                ctx.inc(ins, s_n, 1)
                for c in range(NDC):
                    kt = gg * 8 + c
                    ctx.wge(v, s_tp, kt + 1)
                    ins = v.tensor_copy(bcsf[:, c * 1024 + gg * 128:c * 1024 + (gg + 1) * 128],
                                        ppt[:, kt % 2, 0:128])
                    ctx.inc(ins, s_te, 1)
        @blk.tensor
        def _(t):
            ctx.wait_sb(t)
            k = 0
            for gg in range(8):
                ctx.wge(t, s_n, gg * 2 + 2)
                for c in range(NDC):
                    if k >= 2:
                        ctx.wge(t, s_te, k - 1)
                    ins = t.transpose(ppt[:, k % 2, 0:128],
                                      pb1[:, gg * 1024 + c * 128:gg * 1024 + (c + 1) * 128], idt[:])
                    ctx.inc(ins, s_tp, 1)
                    k += 1
        @blk.sync
        def _(sy):
            ctx.wait_sb(sy)
            ctx.wge(sy, s_te, 64)
            ctx.bar_inc(sy)
    ctx.stage_done()


def emit_heavy3(ctx):
    nc = ctx.nc
    pb0f = ctx.sb["pb0"][:].rearrange("p a b -> p (a b)")
    q_sb = pb0f[0:64, 0:8192].rearrange("p (h n) -> p h n", h=8)
    bcsf = ctx.sb["bcs"][:].rearrange("p a b -> p (a b)")
    xqf = bcsf.rearrange("p (c n) -> p c n", c=8)
    pb1f32 = ctx.sb["pb1"][:].rearrange("p a b -> p (a b)").bitcast(F32)
    tabs = ctx.sb["pf0"][0:64, 0, 0:4096].rearrange("p (k n) -> p k n", k=4)
    st0 = ctx.sb["st0"]
    cst = ctx.sb["cst"]
    qwg = ctx.carve(0, [128, NDC, 512], BF16)
    pp = ctx.sb["pp0"]
    s_w, s_g, s_mm, s_qe, s_rp = [f"h3{x}" for x in "wgmer"]
    with nc.Block("heavy3") as blk:
        @blk.gpsimd
        def _(g):
            ctx.wait_sb(g)
            ctx.inc(g.dma_start(out=qwg[:], in_=ctx.aps["L1_attn_heavy_q_w"].rearrange("(c p) j -> p c j", p=128)), s_w, 16)
            ctx.inc(g.dma_start(out=st0[:, 32:40], in_=ctx.aps["L1_attn_heavy_gamma"].rearrange("(c p) -> p c", p=128),
                                allow_slow_non_contiguous=True), s_w, 16)
        @blk.vector
        def _(v):
            ctx.wait_sb(v)
            ctx.wge(v, s_w, 32)
            for kc in range(NDC):
                ins = v.tensor_scalar(out=qwg[:, kc, :], in0=qwg[:, kc, :], scalar1=st0[:, 32 + kc:33 + kc],
                                      scalar2=None, op0=ALU.mult)
            ctx.inc(ins, s_g, 1)
            for h in range(8):
                ctx.wge(v, s_qe, (h // 2) * 4 + 4)
                q = q_sb[:, h, :]
                t1 = pb1f32[0:64, 0:1024]
                rq = pb1f32[0:64, 1024:2048]
                v.tensor_tensor(out=t1, in0=q, in1=tabs[:, 0, :], op=ALU.mult)
                v.tensor_scalar(out=rq[0:32, :], in0=q_sb[32:64, h, :], scalar1=cst[0:32, C_NEG1:C_NEG1 + 1],
                                scalar2=None, op0=ALU.mult)
                v.tensor_copy(rq[32:64, :], q_sb[0:32, h, :])
                v.tensor_tensor(out=rq, in0=rq, in1=tabs[:, 1, :], op=ALU.mult)
                ins = v.tensor_tensor(out=q, in0=t1, in1=rq, op=ALU.add)
                ctx.inc(ins, s_rp, 1)
        @blk.tensor
        def _(t):
            ctx.wait_sb(t)
            ctx.wge(t, s_g, 1)
            for m in range(4):
                for nh in range(2):
                    k = m * 2 + nh
                    if k >= 2:
                        ctx.wge(t, s_qe, (k - 2) * 2 + 2)
                    for kc in range(NDC):
                        ins = t.matmul(pp[:, k % 2, :], qwg[:, kc, m * 128:(m + 1) * 128],
                                       xqf[:, kc, nh * 512:(nh + 1) * 512], start=(kc == 0), stop=(kc == NDC - 1))
                    ctx.inc(ins, s_mm, 1)
        @blk.scalar
        def _(s):
            ctx.wait_sb(s)
            for m in range(4):
                for nh in range(2):
                    k = m * 2 + nh
                    ctx.wge(s, s_mm, k + 1)
                    ins = s.activation(out=q_sb[0:64, 2 * m, nh * 512:(nh + 1) * 512],
                                       in_=pp[0:64, k % 2, :], func=AF.Copy)
                    ctx.inc(ins, s_qe, 1)
                    ins = s.activation(out=q_sb[0:64, 2 * m + 1, nh * 512:(nh + 1) * 512],
                                       in_=pp[64:128, k % 2, :], func=AF.Copy)
                    ctx.inc(ins, s_qe, 1)
        @blk.sync
        def _(sy):
            ctx.wait_sb(sy)
            ctx.wge(sy, s_rp, 8)
            ctx.bar_inc(sy)
    ctx.stage_done()


def emit_heavy4(ctx):
    nc = ctx.nc
    pb0f = ctx.sb["pb0"][:].rearrange("p a b -> p (a b)")
    kfull = pb0f[0:64, 8192:16392].rearrange("p (h n) -> p h n", h=8)
    vT = pb0f[:, 24576:28672]
    yscb = ctx.sb["ysc"][:].rearrange("p a b -> p (a b)").bitcast(BF16)
    kvT = yscb[:, 0:8192].rearrange("p (c h) -> p c h", c=8).rearrange("p c (h j) -> p c h j", h=8) if False else None
    kvTf = yscb[:, 0:8192]
    bcsf = ctx.sb["bcs"][:].rearrange("p a b -> p (a b)")
    xcf = bcsf.rearrange("p (c n) -> p c n", c=8)
    pb1 = ctx.sb["pb1"][:].rearrange("p a b -> p (a b)")
    tabs = ctx.sb["pf0"][0:64, 0, 0:4096].rearrange("p (k n) -> p k n", k=4)
    st0 = ctx.sb["st0"]
    cst = ctx.sb["cst"]
    pp = ctx.sb["pp0"]
    ppt = ctx.sb["ppt"]
    idt = ctx.sb["idt"]
    s_w, s_ld, s_tp, s_te, s_g, s_mm, s_ke, s_ve, s_vt, s_vte, s_rp, s_nc = [f"h4{x}" for x in
        ("w", "l", "t", "u", "g", "m", "k", "v", "T", "U", "r", "n")]
    with nc.Block("heavy4") as blk:
        @blk.gpsimd
        def _(g):
            ctx.wait_sb(g)
            ctx.inc(g.dma_start(out=ctx.sb["nkvrow"][0:1, :],
                                in_=ctx.aps["L1_attn_heavy_null_kv"].rearrange("a h d -> a (h d)").rearrange("(one a) hd -> one (a hd)", one=1)), s_w, 16)
            for h in range(8):
                if h >= 1:
                    ctx.wge(g, s_tp, h * 8)
                ctx.inc(g.dma_start(out=pb1[:, 0:1024], in_=ctx.aps["L1_attn_heavy_kv_w"][h]), s_ld, 16)
        @blk.tensor
        def _(t):
            ctx.wait_sb(t)
            kt = 0
            for h in range(8):
                ctx.wge(t, s_ld, (h + 1) * 16)
                for c in range(NDC):
                    if kt >= 2:
                        ctx.wge(t, s_te, kt - 1)
                    ins = t.transpose(ppt[:, kt % 2, 0:128], pb1[:, c * 128:(c + 1) * 128], idt[:])
                    ctx.inc(ins, s_tp, 1)
                    kt += 1
            ctx.wge(t, s_w, 16)
            for h in range(8):
                ctx.wge(t, s_te, 64 + h)
                ins = t.transpose(ppt[0:64, kt % 2, 0:1], ctx.sb["nkvrow"][0:1, h * 64:(h + 1) * 64], idt[0:1, 0:1])
                ctx.inc(ins, s_tp, 1)
                kt += 1
            ctx.wge(t, s_g, 1)
            kt2 = 0
            for h in range(8):
                for nh in range(2):
                    k = h * 2 + nh
                    if k >= 2:
                        ctx.wge(t, s_ke, k - 1)
                        ctx.wge(t, s_ve, k - 1)
                    for kc in range(NDC):
                        ins = t.matmul(pp[:, k % 2, :], kvTf[:, kc * 1024 + h * 128:kc * 1024 + (h + 1) * 128],
                                       xcf[:, kc, nh * 512:(nh + 1) * 512], start=(kc == 0), stop=(kc == NDC - 1))
                    ctx.inc(ins, s_mm, 1)
                ctx.wge(t, s_ve, (h + 1) * 2)
                for c in range(NDC):
                    if kt2 >= 2:
                        ctx.wge(t, s_vte, kt2 - 1)
                    ins = t.transpose(ppt[:, kt2 % 2, 0:64], pb1[0:64, 2048 + c * 128:2048 + (c + 1) * 128], idt[0:64, 0:64])
                    ctx.inc(ins, s_vt, 1)
                    kt2 += 1
        @blk.vector
        def _(v):
            ctx.wait_sb(v)
            kt = 0
            for h in range(8):
                for c in range(NDC):
                    ctx.wge(v, s_tp, kt + 1)
                    ins = v.tensor_copy(kvTf[:, c * 1024 + h * 128:c * 1024 + (h + 1) * 128], ppt[:, kt % 2, 0:128])
                    ctx.inc(ins, s_te, 1)
                    kt += 1
            for c in range(NDC):
                ins = v.tensor_scalar(out=kvTf[:, c * 1024:(c + 1) * 1024], in0=kvTf[:, c * 1024:(c + 1) * 1024],
                                      scalar1=st0[:, 32 + c:33 + c], scalar2=None, op0=ALU.mult)
            ctx.inc(ins, s_g, 1)
            for h in range(8):
                ctx.wge(v, s_tp, 64 + h + 1)
                ins = v.tensor_copy(kfull[:, h, 0:1], ppt[0:64, (64 + h) % 2, 0:1])
                ctx.inc(ins, s_te, 1)
            def vte_evacs(hh):
                for c in range(NDC):
                    kt2 = hh * NDC + c
                    ctx.wge(v, s_vt, kt2 + 1)
                    ins = v.tensor_copy(vT[:, hh * 512:(hh + 1) * 512].rearrange("p (c8 j) -> p c8 j", c8=8)[:, c, :],
                                        ppt[0:128, kt2 % 2, 0:64])
                    ctx.inc(ins, s_vte, 1)
            for h in range(8):
                if h >= 1:
                    vte_evacs(h - 1)
                ctx.wge(v, s_ke, h * 2 + 2)
                kk = kfull[:, h, 1:1025]
                t1 = ctx.sb["pf1"][0:64, 0:1024]
                rq = ctx.sb["pf1"][0:64, 1024:2048]
                v.tensor_tensor(out=t1, in0=kk, in1=tabs[:, 2, :], op=ALU.mult)
                v.tensor_scalar(out=rq[0:32, :], in0=kfull[32:64, h, 1:1025], scalar1=cst[0:32, C_NEG1:C_NEG1 + 1],
                                scalar2=None, op0=ALU.mult)
                v.tensor_copy(rq[32:64, :], kfull[0:32, h, 1:1025])
                v.tensor_tensor(out=rq, in0=rq, in1=tabs[:, 3, :], op=ALU.mult)
                ins = v.tensor_tensor(out=kk, in0=t1, in1=rq, op=ALU.add)
                ctx.inc(ins, s_rp, 1)
            vte_evacs(7)
        @blk.scalar
        def _(s):
            ctx.wait_sb(s)
            for h in range(8):
                for nh in range(2):
                    k = h * 2 + nh
                    ctx.wge(s, s_mm, k + 1)
                    if h >= 1:
                        ctx.wge(s, s_vt, h * 8)
                    ins = s.activation(out=kfull[0:64, h, 1 + nh * 512:1 + (nh + 1) * 512],
                                       in_=pp[0:64, k % 2, :], func=AF.Copy)
                    ctx.inc(ins, s_ke, 1)
                    ins = s.activation(out=pb1[0:64, 2048 + nh * 512:2048 + (nh + 1) * 512].bitcast(BF16)
                                       if False else pb1[0:64, 2048 + nh * 512:2048 + (nh + 1) * 512],
                                       in_=pp[64:128, k % 2, :], func=AF.Copy)
                    ctx.inc(ins, s_ve, 1)
        @blk.sync
        def _(sy):
            ctx.wait_sb(sy)
            ctx.wge(sy, s_rp, 8)
            ctx.wge(sy, s_vte, 64)
            ctx.bar_inc(sy)
    ctx.stage_done()


def emit_heavy5(ctx):
    nc = ctx.nc
    pb0f = ctx.sb["pb0"][:].rearrange("p a b -> p (a b)")
    q_sb = pb0f[0:64, 0:8192].rearrange("p (h n) -> p h n", h=8)
    kfull = pb0f[0:64, 8192:16392].rearrange("p (h n) -> p h n", h=8)
    vT = pb0f[:, 24576:28672]
    yscb = ctx.sb["ysc"][:].rearrange("p a b -> p (a b)").bitcast(BF16)
    o2 = yscb[:, 8192:16384].rearrange("p (c n) -> p c n", c=4)
    bcsf = ctx.sb["bcs"][:].rearrange("p a b -> p (a b)")
    at_sb = bcsf[:, 0:1024].rearrange("p (c j) -> p c j", c=8)
    a0 = bcsf[0:1, 1024:1152]
    ab = ctx.sb["pb1"][:].rearrange("p a b -> p (a b)")
    simf = ctx.sb["pf0"][:, 1, 0:1032]
    nkvrow = ctx.sb["nkvrow"]
    st0 = ctx.sb["st0"]
    cst = ctx.sb["cst"]
    pp = ctx.sb["pp0"]
    pp1 = ctx.sb["pp1"]
    ppt = ctx.sb["ppt"]
    idt = ctx.sb["idt"]
    s_mm, s_sf, s_ex, s_ab, s_tp, s_te, s_av, s_oe = [f"h5{x}" for x in "msxbtuao"]
    with nc.Block("heavy5") as blk:
        @blk.tensor
        def _(t):
            ctx.wait_sb(t)
            ktp = 0
            for qt in range(8):
                for h in range(8):
                    k = qt * 8 + h
                    if k >= 1:
                        ctx.wge(t, s_sf, k)
                    for c in range(2):
                        ins = t.matmul(pp[:, c, :], q_sb[:, h, qt * 128:(qt + 1) * 128],
                                       kfull[:, h, 1 + c * 512:1 + (c + 1) * 512], start=True, stop=True)
                    ins = t.matmul(pp[:, 2, 0:1], q_sb[:, h, qt * 128:(qt + 1) * 128],
                                   kfull[:, h, 0:1], start=True, stop=True)
                    ctx.inc(ins, s_mm, 1)
                    ctx.wge(t, s_ab, k + 1)
                    for c in range(9):
                        if ktp >= 2:
                            ctx.wge(t, s_te, ktp - 1)
                        if c < 8:
                            ins = t.transpose(ppt[:, ktp % 2, 0:128],
                                              ab[:, (k % 2) * 1032 + c * 128:(k % 2) * 1032 + (c + 1) * 128], idt[:])
                        else:
                            ins = t.transpose(ppt[0:1, ktp % 2, 0:128],
                                              ab[:, (k % 2) * 1032 + 1024:(k % 2) * 1032 + 1025], idt[:])
                        ctx.inc(ins, s_tp, 1)
                        ktp += 1
                    ctx.wge(t, s_te, ktp)
                    if k >= 2:
                        ctx.wge(t, s_oe, k - 1)
                    for c in range(8):
                        t.matmul(pp1[0:64, k % 2, 0:128],
                                 vT[:, h * 512:(h + 1) * 512].rearrange("p (c8 j) -> p c8 j", c8=8)[:, c, :],
                                 at_sb[:, c, :], start=(c == 0), stop=False)
                    ins = t.matmul(pp1[0:64, k % 2, 0:128], nkvrow[0:1, 512 + h * 64:512 + (h + 1) * 64],
                                   a0[:], start=False, stop=True)
                    ctx.inc(ins, s_av, 1)
        @blk.vector
        def _(v):
            ctx.wait_sb(v)
            for qt in range(8):
                for h in range(8):
                    k = qt * 8 + h
                    ctx.wge(v, s_mm, k + 1)
                    v.tensor_scalar(out=simf[:, 0:1], in0=pp[:, 2, 0:1], scalar1=cst[:, C_QSC:C_QSC + 1],
                                    scalar2=None, op0=ALU.mult)
                    v.tensor_scalar(out=simf[:, 1:513], in0=pp[:, 0, :], scalar1=cst[:, C_QSC:C_QSC + 1],
                                    scalar2=None, op0=ALU.mult)
                    ins = v.tensor_scalar(out=simf[:, 513:1025], in0=pp[:, 1, :], scalar1=cst[:, C_QSC:C_QSC + 1],
                                          scalar2=None, op0=ALU.mult)
                    ctx.inc(ins, s_sf, 1)
                    v.reduce_max(st0[:, 0:1], simf[:, 0:1025], axis=AX.X, negate=True)
                    ctx.wge(v, s_ex, k + 1)
                    v.drain()
                    v.reciprocal(st0[:, 2:3], st0[:, 1:2])
                    v.drain()
                    ins = v.tensor_scalar(out=ab[:, (k % 2) * 1032:(k % 2) * 1032 + 1025],
                                          in0=simf[:, 1032:2057] if False else ctx.sb["pf0"][:, 0, 0:1025],
                                          scalar1=st0[:, 2:3], scalar2=None, op0=ALU.mult)
                    ctx.inc(ins, s_ab, 1)
                    for c in range(9):
                        kt9 = k * 9 + c
                        ctx.wge(v, s_tp, kt9 + 1)
                        if c < 8:
                            ins = v.tensor_copy(at_sb[:, c, :], ppt[:, kt9 % 2, 0:128])
                        else:
                            ins = v.tensor_copy(a0[:], ppt[0:1, kt9 % 2, 0:128])
                        ctx.inc(ins, s_te, 1)
        @blk.scalar
        def _(s):
            ctx.wait_sb(s)
            for qt in range(8):
                for h in range(8):
                    k = qt * 8 + h
                    ctx.wge(s, s_sf, k + 1)
                    ins = s.activation(out=ctx.sb["pf0"][:, 0, 0:1025], in_=simf[:, 0:1025], func=AF.Exp,
                                       bias=st0[:, 0:1], scale=1.0, accum_out=st0[:, 1:2])
                    ctx.inc(ins, s_ex, 1)
                    ctx.wge(s, s_av, k + 1)
                    if k >= 2:
                        ctx.wge(s, s_oe, k - 1)
                    ins = s.activation(out=o2[64 * (h % 2):64 * (h % 2) + 64, h // 2, qt * 128:(qt + 1) * 128],
                                       in_=pp1[0:64, k % 2, 0:128], func=AF.Copy)
                    ctx.inc(ins, s_oe, 1)
        @blk.sync
        def _(sy):
            ctx.wait_sb(sy)
            ctx.wge(sy, s_oe, 64)
            ctx.bar_inc(sy)
    ctx.stage_done()


def emit_heavy6(ctx):
    nc = ctx.nc
    yscb = ctx.sb["ysc"][:].rearrange("p a b -> p (a b)").bitcast(BF16)
    o2 = yscb[:, 8192:16384].rearrange("p (c n) -> p c n", c=4)
    how = ctx.carve(8192, [128, 4, 1024], BF16)
    pf1 = ctx.sb["pf1"]
    pp1 = ctx.sb["pp1"]
    s_w, s_mm, s_ev, s_pn, s_ps, s_sc = [f"h6{x}" for x in "wmenpc"]
    with nc.Block("heavy6") as blk:
        @blk.gpsimd
        def _(g):
            ctx.wait_sb(g)
            ctx.inc(g.dma_start(out=how[:], in_=ctx.aps["L1_attn_heavy_out_w"].rearrange("(c p) e -> p c e", p=128)), s_w, 16)
            ctx.inc(g.dma_start(out=ctx.sb["nqrow"][0:1, :],
                                in_=ctx.aps["L1_attn_null_q_token"].rearrange("(one n) -> one n", one=1)), s_w, 16)
            ctx.inc(g.dma_start(out=ctx.sb["sti2"][:, 0:8],
                                in_=ctx.aps["selq"].rearrange("(gg p) one -> p (gg one)", p=128),
                                allow_slow_non_contiguous=True), s_w, 16)
            ctx.wge(g, s_pn, 1)
            for ttp in range(32):
                ins = g.dma_start(out=ctx.aps["hscat"][ttp * 128:(ttp + 1) * 128, :], in_=pf1[:, 2048:3072])
                ctx.inc(ins, s_ps, 16)
            ctx.wge(g, s_ps, 32 * 16)
            for qt in range(8):
                ctx.wge(g, s_ev, qt + 1)
                ins = g.indirect_dma_start(
                    out=ctx.aps["hscat"], out_offset=IndirectOffsetOnAxis(ctx.sb["sti2"][:, qt:qt + 1], 0),
                    in_=pf1[:, (qt % 2) * 1024:(qt % 2) * 1024 + 1024], in_offset=None, bounds_check=4095)
                ctx.inc(ins, s_sc, 16)
            ctx.wge(g, s_sc, 8 * 16)
            ctx.bar_inc(g)
        @blk.tensor
        def _(t):
            ctx.wait_sb(t)
            ctx.wge(t, s_w, 32)
            for nh in range(2):
                ins = t.matmul(pp1[:, nh, :], ctx.sb["ones1f"][0:1, 0:128],
                               ctx.sb["nqrow"][0:1, nh * 512:(nh + 1) * 512], start=True, stop=True)
            ctx.inc(ins, s_mm, 1)
            for qt in range(8):
                if qt == 0:
                    ctx.wge(t, s_pn, 1)
                else:
                    ctx.wge(t, s_ev, qt)
                for nh in range(2):
                    for c in range(4):
                        ins = t.matmul(pp1[:, nh, :], o2[:, c, qt * 128:(qt + 1) * 128],
                                       how[:, c, nh * 512:(nh + 1) * 512], start=(c == 0), stop=(c == 3))
                ctx.inc(ins, s_mm, 1)
        @blk.vector
        def _(v):
            ctx.wait_sb(v)
            ctx.wge(v, s_mm, 1)
            ins = v.tensor_copy(pf1[:, 2048:3072], pp1[:, 0:2, :].rearrange("p a b -> p (a b)"))
            ctx.inc(ins, s_pn, 1)
            for qt in range(8):
                ctx.wge(v, s_mm, qt + 2)
                if qt >= 2:
                    ctx.wge(v, s_sc, (qt - 1) * 16)
                ins = v.tensor_copy(pf1[:, (qt % 2) * 1024:(qt % 2) * 1024 + 1024],
                                    pp1[:, 0:2, :].rearrange("p a b -> p (a b)"))
                ctx.inc(ins, s_ev, 1)
    ctx.stage_done()


def emit_moe(ctx):
    nc = ctx.nc
    xch = ctx.sb["pf0"]
    rwt = ctx.carve(0, [128, NDC, 8], F32)
    eb = ctx.carve(256, [1, 8], F32)
    act = ctx.sb["pf1"]
    st0 = ctx.sb["st0"]
    cst = ctx.sb["cst"]
    stb = ctx.sb["stb"]
    cfst = ctx.sb["bcs"]
    pp = ctx.sb["pp0"]
    ppt = ctx.sb["ppt"]
    idt = ctx.sb["idt"]
    s_w, s_ld, s_mm, s_ac, s_cf, s_tp, s_te, s_st = [f"mo{x}" for x in "wlmactuq"]
    with nc.Block("moe") as blk:
        @blk.gpsimd
        def _(g):
            ctx.wait_sb(g)
            ctx.inc(g.dma_start(out=rwt[:], in_=ctx.aps["rWTf"]), s_w, 16)
            ctx.inc(g.dma_start(out=eb[:], in_=ctx.aps["L1_ffn_expert_bias"].rearrange("(one e) -> one e", one=1)), s_w, 16)
            for ch in range(10):
                if 2 <= ch < 10 and ch < 8:
                    ctx.wge(g, s_mm, (ch - 1) * 4)
                if ch < 8:
                    ctx.inc(g.dma_start(out=xch[:, ch % 2, :].rearrange("p (c t) -> p c t", c=NDC),
                                        in_=ctx.aps["xn1f32"][:, :, ch * 512:(ch + 1) * 512]), s_ld, 16)
                if ch >= 2:
                    for tt in range((ch - 2) * 4, (ch - 1) * 4):
                        ctx.wge(g, s_te, tt + 1)
                        ins = g.dma_start(out=ctx.aps["cfrow"][:, tt * 128:(tt + 1) * 128],
                                          in_=cfst[0:8, tt % 2, 0:128])
                        ctx.inc(ins, s_st, 16)
            ctx.wge(g, s_st, 32 * 16)
            ctx.bar_inc(g)
        @blk.tensor
        def _(t):
            ctx.wait_sb(t)
            ctx.wge(t, s_w, 32)
            for ch in range(8):
                ctx.wge(t, s_ld, (ch + 1) * 16)
                for sub in range(4):
                    tt = ch * 4 + sub
                    if tt >= 4:
                        ctx.wge(t, s_ac, tt - 3)
                    for kc in range(NDC):
                        t.matmul(pp[:, tt % 4, 0:8],
                                 xch[:, ch % 2, :].rearrange("p (c t) -> p c t", c=NDC)[:, kc, sub * 128:(sub + 1) * 128],
                                 rwt[:, kc, :], start=(kc == 0), stop=False)
                    ins = t.matmul(pp[:, tt % 4, 0:8], ctx.sb["ones1f"][0:1, 0:128], eb[:], start=False, stop=True)
                    ctx.inc(ins, s_mm, 1)
                    ctx.wge(t, s_cf, tt + 1)
                    if tt >= 2:
                        ctx.wge(t, s_te, tt - 1)
                    ins = t.transpose(ppt[0:8, tt % 2, 0:128], stb[:, 48 + (tt % 2) * 8:48 + (tt % 2) * 8 + 8], idt[:])
                    ctx.inc(ins, s_tp, 1)
        @blk.scalar
        def _(s):
            ctx.wait_sb(s)
            for tt in range(32):
                ctx.wge(s, s_mm, tt + 1)
                if tt >= 4:
                    ctx.wge(s, s_cf, tt - 3)
                a = act[:, (tt % 4) * 8:(tt % 4) * 8 + 8]
                s.activation(out=a, in_=pp[:, tt % 4, 0:8], func=AF.Exp)
                s.drain()
                s.activation(out=a, in_=a, func=AF.Ln, bias=1.0)
                s.drain()
                ins = s.activation(out=a, in_=a, func=AF.Sqrt)
                ctx.inc(ins, s_ac, 1)
        @blk.vector
        def _(v):
            ctx.wait_sb(v)
            for tt in range(32):
                ctx.wge(v, s_ac, tt + 1)
                if tt >= 2:
                    ctx.wge(v, s_tp, tt - 1)
                a = act[:, (tt % 4) * 8:(tt % 4) * 8 + 8]
                for e in range(8):
                    v.tensor_scalar(out=act[:, 48 + 8 * e:56 + 8 * e], in0=a, scalar1=a[:, e:e + 1],
                                    scalar2=None, op0=ALU.is_gt)
                for e in range(8):
                    v.reduce_sum(st0[:, 8 + e:9 + e], act[:, 48 + 8 * e:56 + 8 * e], axis=AX.X)
                for e in range(1, 8):
                    v.tensor_scalar(out=act[:, 112 + 8 * e:112 + 8 * e + e], in0=a[:, 0:e], scalar1=a[:, e:e + 1],
                                    scalar2=None, op0=ALU.is_equal)
                for e in range(1, 8):
                    v.reduce_sum(st0[:, 16 + e:17 + e], act[:, 112 + 8 * e:112 + 8 * e + e], axis=AX.X)
                v.memset(st0[:, 16:17], 0.0)
                for e in range(8):
                    v.tensor_tensor(out=st0[:, 8 + e:9 + e], in0=st0[:, 8 + e:9 + e], in1=st0[:, 16 + e:17 + e], op=ALU.add)
                for e in range(8):
                    v.tensor_single_scalar(out=st0[:, 24 + e:25 + e], in_=st0[:, 8 + e:9 + e], scalar=2.0, op=ALU.is_lt)
                for e in range(8):
                    ins = v.tensor_tensor(out=stb[:, 48 + (tt % 2) * 8 + e:48 + (tt % 2) * 8 + e + 1],
                                          in0=a[:, e:e + 1], in1=st0[:, 24 + e:25 + e], op=ALU.mult)
                ctx.inc(ins, s_cf, 1)
                ctx.wge(v, s_tp, tt + 1)
                if tt >= 2:
                    ctx.wge(v, s_st, (tt - 1) * 16)
                ins = v.tensor_copy(cfst[0:8, tt % 2, 0:128], ppt[0:8, tt % 2, 0:128])
                ctx.inc(ins, s_te, 1)
    ctx.stage_done()


def emit_moeh(ctx, eidx):
    nc = ctx.nc
    tag = f"mh{eidx + 1}"
    w1 = ctx.carve(0, [128, NDC, 512], BF16)
    b1 = ctx.carve(8192, [128, 4], F32)
    xch = ctx.sb["pb1"]
    hb = ctx.sb["bcs"][:].rearrange("p a b -> p (a b)")
    pp = ctx.sb["pp0"]
    pp1 = ctx.sb["pp1"]
    base = 0 if eidx < 0 else 4 + 4 * eidx
    s_w, s_ld, s_mm, s_cb, s_hv, s_sc, s_st = [f"{tag}{x}" for x in "wlmbhcq"]
    w1name = "L1_ffn_sW1" if eidx < 0 else "L1_ffn_rW1"
    b1name = "L1_ffn_sb1" if eidx < 0 else "L1_ffn_rb1"
    with nc.Block("moeh" + tag) as blk:
        @blk.gpsimd
        def _(g):
            ctx.wait_sb(g)
            wsrc = ctx.aps[w1name] if eidx < 0 else ctx.aps[w1name][eidx]
            ctx.inc(g.dma_start(out=w1[:], in_=wsrc.rearrange("(c p) h -> p c h", p=128)), s_w, 16)
            bsrc = ctx.aps[b1name] if eidx < 0 else ctx.aps[b1name][eidx]
            ctx.inc(g.dma_start(out=b1[:], in_=bsrc.rearrange("(c p) -> p c", p=128),
                                allow_slow_non_contiguous=True), s_w, 16)
            for ch in range(10):
                if ch < 8:
                    if ch >= 2:
                        ctx.wge(g, s_mm, (ch - 1) * 4)
                    ctx.inc(g.dma_start(out=xch[:, ch % 2, :].rearrange("p (c t) -> p c t", c=NDC),
                                        in_=ctx.aps["xn1f"][:, :, ch * 512:(ch + 1) * 512]), s_ld, 16)
                    if eidx >= 0:
                        ctx.inc(g.dma_start(out=ctx.sb["nkvrow"][0:1, (ch % 2) * 512:(ch % 2) * 512 + 512],
                                            in_=ctx.aps["cfrow"][eidx:eidx + 1, ch * 512:(ch + 1) * 512]), s_ld, 16)
                if ch >= 2:
                    for k in range((ch - 2) * 4, (ch - 1) * 4):
                        ch2, ht = divmod(k, 4)
                        ctx.wge(g, s_hv, k + 1)
                        ins = g.dma_start(out=ctx.aps["shsc"][:, base + ht, ch2 * 512:(ch2 + 1) * 512],
                                          in_=hb[:, (k % 4) * 512:(k % 4 + 1) * 512])
                        ctx.inc(ins, s_st, 16)
            ctx.wge(g, s_st, 32 * 16)
            ctx.bar_inc(g)
        @blk.tensor
        def _(t):
            ctx.wait_sb(t)
            ctx.wge(t, s_w, 32)
            nld = 32 if eidx >= 0 else 16
            for ch in range(8):
                ctx.wge(t, s_ld, (ch + 1) * nld)
                if eidx >= 0:
                    if ch >= 2:
                        ctx.wge(t, s_cb, ch - 1)
                    ins = t.matmul(pp1[:, ch % 2, :], ctx.sb["ones1"][0:1, 0:128],
                                   ctx.sb["nkvrow"][0:1, (ch % 2) * 512:(ch % 2) * 512 + 512], start=True, stop=True)
                    ctx.inc(ins, s_cb + "m", 1)
                for ht in range(4):
                    k = ch * 4 + ht
                    if k >= 4:
                        ctx.wge(t, s_hv, k - 3)
                    for kc in range(NDC):
                        ins = t.matmul(pp[:, k % 4, :], w1[:, kc, ht * 128:(ht + 1) * 128],
                                       xch[:, ch % 2, :].rearrange("p (c t) -> p c t", c=NDC)[:, kc, :],
                                       start=(kc == 0), stop=(kc == NDC - 1))
                    ctx.inc(ins, s_mm, 1)
        @blk.scalar
        def _(s):
            ctx.wait_sb(s)
            for k in range(32):
                ch, ht = divmod(k, 4)
                ctx.wge(s, s_mm, k + 1)
                if k >= 4:
                    ctx.wge(s, s_hv, k - 3)
                    ctx.wge(s, s_st, (k - 3) * 16)
                ins = s.activation(out=hb[:, (k % 4) * 512:(k % 4 + 1) * 512], in_=pp[:, k % 4, :],
                                   func=AF.Silu, bias=b1[:, ht:ht + 1], scale=1.0)
                ctx.inc(ins, s_sc, 1)
        @blk.vector
        def _(v):
            ctx.wait_sb(v)
            for k in range(32):
                ch, ht = divmod(k, 4)
                ctx.wge(v, s_sc, k + 1)
                if k >= 4:
                    ctx.wge(v, s_st, (k - 3) * 16)
                if eidx >= 0:
                    ctx.wge(v, s_cb + "m", ch + 1)
                    ins = v.tensor_tensor(out=hb[:, (k % 4) * 512:(k % 4 + 1) * 512],
                                          in0=hb[:, (k % 4) * 512:(k % 4 + 1) * 512],
                                          in1=pp1[:, ch % 2, :], op=ALU.mult)
                else:
                    ins = v.tensor_copy(hb[:, (k % 4) * 512:(k % 4 + 1) * 512],
                                        hb[:, (k % 4) * 512:(k % 4 + 1) * 512])
                ctx.inc(ins, s_hv, 1)
                if eidx >= 0 and ht == 3:
                    ctx.inc(v.engine_nop(), s_cb, 1)
    ctx.stage_done()


def emit_final(ctx):
    nc = ctx.nc
    w2e = ctx.sb["pb0"][:].rearrange("p a b -> p (a b)").rearrange("p (c e) -> p c e", c=32)
    w2s = ctx.sb["bcs"][:].rearrange("p a b -> p (a b)").rearrange("p (c e) -> p c e", c=8)[:, 0:4, :]
    pb1f = ctx.sb["pb1"][:].rearrange("p a b -> p (a b)")
    yscb2 = ctx.sb["ysc"][:].rearrange("p a b -> p (a b)").bitcast(BF16)

    def r2slot(tt):
        base = pb1f if tt % 2 == 0 else yscb2
        return base[:, 0:4608].rearrange("p (c t) -> p c t", c=36)
    cfs = ctx.carve(0, [8, 4096], BF16)
    rb2 = ctx.carve(8192, [8, 1024], BF16)
    sb2r = ctx.carve(10240, [1, 1024], BF16)
    xres = ctx.sb["pf0"]
    ev = ctx.sb["pf1"]
    pp1 = ctx.sb["pp1"]
    s_w, s_ld, s_mm, s_ev, s_st = [f"fn{x}" for x in "wlmeq"]
    with nc.Block("final") as blk:
        @blk.gpsimd
        def _(g):
            ctx.wait_sb(g)
            ctx.inc(g.dma_start(out=w2e[:, 0:32, :].rearrange("p c e -> p (c e)").rearrange("p (e2 c4 d) -> p e2 c4 d", e2=8, c4=4),
                                in_=ctx.aps["L1_ffn_rW2"].rearrange("e (c p) d -> p e c d", p=128)), s_w, 16)
            ctx.inc(g.dma_start(out=w2s[:],
                                in_=ctx.aps["L1_ffn_sW2"].rearrange("(c p) d -> p c d", p=128)), s_w, 16)
            ctx.inc(g.dma_start(out=cfs[:], in_=ctx.aps["cfrow"]), s_w, 16)
            ctx.inc(g.dma_start(out=rb2[:], in_=ctx.aps["L1_ffn_rb2"]), s_w, 16)
            ctx.inc(g.dma_start(out=sb2r[:], in_=ctx.aps["L1_ffn_sb2"].rearrange("(one d) -> one d", one=1)), s_w, 16)
            for tt in range(32):
                t0 = tt * 128
                if tt >= 2:
                    ctx.wge(g, s_mm, tt - 1)
                ctx.inc(g.dma_start(out=r2slot(tt), in_=ctx.aps["shsc"][:, :, t0:t0 + 128]), s_ld, 16)
                if tt >= 2:
                    ctx.wge(g, s_ev, tt - 1)
                ctx.inc(g.dma_start(out=xres[:, tt % 2, 0:1024], in_=ctx.aps["x1"][t0:t0 + 128, :]), s_ld, 16)
                ctx.inc(g.dma_start(out=xres[:, tt % 2, 1024:2048], in_=ctx.aps["lighto"][t0:t0 + 128, :]), s_ld, 16)
                ctx.inc(g.dma_start(out=xres[:, tt % 2, 2048:3072], in_=ctx.aps["hscat"][t0:t0 + 128, :]), s_ld, 16)
        @blk.tensor
        def _(t):
            ctx.wait_sb(t)
            ctx.wge(t, s_w, 80)
            for tt in range(32):
                ctx.wge(t, s_ld, (tt + 1) * 64)
                if tt >= 1:
                    ctx.wge(t, s_ev, tt)
                for nh in range(2):
                    for c in range(36):
                        lhs = r2slot(tt)[:, c, :]
                        rhs = w2s[:, c, nh * 512:(nh + 1) * 512] if c < 4 else w2e[:, c - 4, nh * 512:(nh + 1) * 512]
                        t.matmul(pp1[:, nh, :], lhs, rhs, start=(c == 0), stop=False)
                    t.matmul(pp1[:, nh, :], cfs[:, tt * 128:(tt + 1) * 128],
                             rb2[:, nh * 512:(nh + 1) * 512], start=False, stop=False)
                    ins = t.matmul(pp1[:, nh, :], ctx.sb["ones1"][0:1, 0:128],
                                   sb2r[0:1, nh * 512:(nh + 1) * 512], start=False, stop=True)
                ctx.inc(ins, s_mm, 1)
        @blk.vector
        def _(v):
            ctx.wait_sb(v)
            for tt in range(32):
                ctx.wge(v, s_mm, tt + 1)
                if tt >= 2:
                    ctx.wge(v, s_st, (tt - 1) * 16)
                v.tensor_add(ev[:, (tt % 2) * 1024:(tt % 2) * 1024 + 1024],
                             pp1[:, 0:2, :].rearrange("p a b -> p (a b)"), xres[:, tt % 2, 0:1024])
                v.tensor_add(ev[:, (tt % 2) * 1024:(tt % 2) * 1024 + 1024],
                             ev[:, (tt % 2) * 1024:(tt % 2) * 1024 + 1024], xres[:, tt % 2, 1024:2048])
                ins = v.tensor_add(ev[:, (tt % 2) * 1024:(tt % 2) * 1024 + 1024],
                                   ev[:, (tt % 2) * 1024:(tt % 2) * 1024 + 1024], xres[:, tt % 2, 2048:3072])
                ctx.inc(ins, s_ev, 1)
        @blk.sync
        def _(sy):
            ctx.wait_sb(sy)
            for tt in range(32):
                ctx.wge(sy, s_ev, tt + 1)
                ins = sy.dma_start(out=ctx.aps["out"][tt * 128:(tt + 1) * 128, :],
                                   in_=ev[:, (tt % 2) * 1024:(tt % 2) * 1024 + 1024])
                ctx.inc(ins, s_st, 16)
            ctx.wge(sy, s_st, 32 * 16)
            ctx.bar_inc(sy)
    ctx.stage_done()


L1_STAGES = [
    ST("ln1", L(emit_ln, "l1", "x1", dst_f32="xn1f32", dst_tm="xn1tm"), ["x1"], ["xn1f32", "xn1tm"]),
    ST("cst1", L(emit_cast_f32_bf16, "c1", "xn1f32", "xn1f", NDC * T), ["xn1f32"], ["xn1f"]),
    ST("lnl", L(emit_ln, "lg", "xn1tm", dst_bf="xnlf"), ["xn1tm"], ["xnlf"]),
    ST("tprw", L(emit_transpose_mat, "rw", "L1_ffn_routing_W", 8, 1024, "rWTf", F32), ["L1_ffn_routing_W"], ["rWTf"]),
    ST("lqkv", L(emit_lqkv), ["L1_attn_light_qkv_w", "L1_attn_ln_gamma", "L1_attn_ln_beta", "xnlf"],
       ["qkvsc", "vtm", "bwrow"]),
    ST("latt", L(emit_lattn), ["qkvsc", "vtm", "lmask", "L1_attn_light_out_w"], ["lighto"]),
    ST("rout", L(emit_route), ["xn1f32", "L1_attn_q_route_tok", "L1_attn_kv_route_tok"],
       ["selq", "selkv", "scorerow", "rankrow"]),
    ST("hv1", L(emit_heavy1), ["selq", "selkv", "invf"], []),
    ST("hv2q", L(emit_heavy2, 0), ["selq", "xn1tm"], []),
    ST("hv3", L(emit_heavy3), ["L1_attn_heavy_q_w", "L1_attn_heavy_gamma"], []),
    ST("hv2k", L(emit_heavy2, 1), ["selkv", "xn1tm"], []),
    ST("hv4", L(emit_heavy4), ["L1_attn_heavy_kv_w", "L1_attn_heavy_null_kv", "L1_attn_heavy_gamma"], []),
    ST("hv5", L(emit_heavy5), [], []),
    ST("hv6", L(emit_heavy6), ["L1_attn_heavy_out_w", "L1_attn_null_q_token", "selq"], ["hscat"]),
    ST("moe", L(emit_moe), ["xn1f32", "rWTf", "L1_ffn_expert_bias"], ["cfrow"]),
    ST("mhs", L(emit_moeh, -1), ["xn1f", "L1_ffn_sW1", "L1_ffn_sb1"], ["shsc"]),
    *[ST(f"mh{e}", L(emit_moeh, e), ["xn1f", "L1_ffn_rW1", "L1_ffn_rb1", "cfrow"], ["shsc"]) for e in range(8)],
    ST("fin", L(emit_final), ["shsc", "cfrow", "L1_ffn_rW2", "L1_ffn_sW2", "L1_ffn_rb2", "L1_ffn_sb2",
                              "x1", "lighto", "hscat"], ["out"]),
]


L1A = L1_STAGES[0:6]
L1B = L1_STAGES[6:14]
L1C = L1_STAGES[14:]
GROUPS = [PREP_STAGES + L0_STAGES + L1_STAGES]


def _host_consts():
    masks = np.zeros((3, 128, 384), np.float32)
    for kind in range(3):
        for j in range(128):
            for blk in range(3):
                pass
    tq = np.arange(128)
    tk = np.arange(384) - 128
    for kind in range(3):
        m = np.zeros((128, 384), np.float32)
        for ci, base in enumerate((0,)):
            pass
        tt_q = tq[:, None]
        tt_k = tk[None, :]
        w = tq // 64
        lo = (w[:, None] - 1) * 64
        hi = (w[:, None] + 2) * 64
        valid = (tt_k >= lo) & (tt_k < hi) & (np.abs(tt_q - tt_k) <= 64)
        if kind == 0:
            valid &= (tt_k >= 0)
        if kind == 2:
            valid &= (tt_k < 128 + 256 - 256)
        m[:] = np.where(valid, 0.0, np.float32(-3.4028235e38))
        masks[kind] = m
    invf = (1.0 / 10000.0 ** (np.arange(0, 64, 2, dtype=np.float32) / 64)).reshape(1, 32).astype(np.float32)
    return {"lmask": masks, "invf": invf}


# version 06 (final)
# speedup vs baseline: 1.0813x; 1.0813x over previous
import numpy as np
from contextlib import ExitStack
import concourse.bass as bass
import concourse.mybir as mybir
from concourse.bass_utils import run_bass_kernel_spmd
from concourse.bass import IndirectOffsetOnAxis

F32 = mybir.dt.float32
BF16 = mybir.dt.bfloat16
I32 = mybir.dt.int32
AF = mybir.ActivationFunctionType
ALU = mybir.AluOpType
AX = mybir.AxisListType

T = 4096
D = 1024
C = 2048
NS = 16
NTT = T // 128
NDC = D // 128
NCC = C // 128
NEGF = float(np.float32(-3.4028235e38))
PI = float(np.float32(np.pi))

C_INVD, C_EPS, C_NEGHALF, C_NEG1, C_CLIP, C_INV003, C_SQD, C_HALF, \
    C_ONE, C_INV4096, C_K1024, C_BIG, C_LOGK, C_INV2PI, C_N2PI, C_HPI, \
    C_MAGIC, C_NMAGIC, C_ZERO, C_QSC, C_BIGN, C_E12 = range(22)
CONST_COLS = [1.0 / D, 1e-5, -0.5, -1.0, 1e-5, float(np.float32(1.0 / 0.03)), 32.0, 0.5,
              1.0, 1.0 / 4096.0, 1024.0, 8192.0, float(np.log(1152.0)), float(np.float32(1.0 / (2 * np.pi))),
              float(np.float32(-2 * np.pi)), float(np.float32(np.pi / 2)),
              8388608.0, -8388608.0, 0.0, 0.125, -8192.0, 1e-12]
CURS = []
_c = 4.0
for _ in range(20):
    CURS.append(_c)
    _c = max(_c * 0.7, 0.03)

SCRATCH = {
    "xnf":    ([128, NDC, T], BF16),
    "x1":     ([T, D], F32),
    "x1a":    ([T, D], F32),
    "xn1tm":  ([T, D], F32),
    "xn1f32": ([128, NDC, T], F32),
    "xn1f":   ([128, NDC, T], BF16),
    "xnlf":   ([128, NDC, T], BF16),
    "qkvsc":  ([128, 12, T], BF16),
    "vtm":    ([T, 512], BF16),
    "lighto": ([T, D], F32),
    "hscat":  ([T, D], F32),
    "scoreq": ([128, 64], F32),
    "selq":   ([1024, 1], I32),
    "selkv":  ([1024, 1], I32),
    "bwrow":  ([1, 1536], F32),
    "scorerow": ([2, 4096], F32),
    "cfrow":  ([8, 4096], BF16),
    "rankrow": ([2, 4096], I32),
    "shsc":   ([128, 36, T], BF16),
    "out":    ([T, D], F32),
    **{f"xs_{d}":  ([128, NCC, T], BF16) for d in "fb"},
    **{f"zg_{d}":  ([128, NCC, T], BF16) for d in "fb"},
    **{f"u_{d}":   ([128, NCC, T], BF16) for d in "fb"},
    **{f"dl_{d}":  ([128, NCC, T], F32)  for d in "fb"},
    **{f"du_{d}":  ([128, NCC, T], BF16) for d in "fb"},
    **{f"bc_{d}":  ([32, T], BF16)       for d in "fb"},
    **{f"y_{d}":   ([128, NCC, T], F32)  for d in "fb"},
    **{f"winT_{d}": ([128, NDC, 2 * C], BF16) for d in "fb"},
    **{f"wxpT_{d}": ([128, NCC, 96], BF16) for d in "fb"},
    **{f"wdtT_{d}": ([64, 1, C], BF16) for d in "fb"},
    **{f"wtlT_{d}": ([128, NCC, D], BF16) for d in "fb"},
    "weff1":  ([4096, D], BF16),
    "weff2":  ([D, 4096], BF16),
    "w1T":    ([128, NDC, 4096], BF16),
    "w2T":    ([128, 32, D], BF16),
    "fusT":   ([128, 16, D], BF16),
    "vTT":    ([128, NDC, 16], BF16),
    "uTT":    ([16, 1, 4096], BF16),
    "fbT":    ([16, 1, 16], BF16),
    "r2l0":   ([128, 32, T], BF16),
    "rWTf":   ([128, NDC, 8], F32),
    "kvT":    ([128, NDC, 1024], BF16),
}


class Ctx:
    PHYS_MAX = 100

    def __init__(self, nc, es, kinds=None):
        self.nc = nc
        self.es = es
        self.sb = {}
        self.aps = {}
        self.barrier_target = 0
        self.kinds = kinds
        self.kinds_out = {}
        self.phys = {}
        self.physv = {}
        self.name2slot = {}
        self.base = {}
        self.cur_names = []
        self.hold1 = []
        self.hold2 = []
        self.free = {"sw": [], "hw": [], "eng": []}
        self.slotkind = {}
        self.nslots = 0
        self.maxw = {}
        self.dangling = []

    def _adopt(self, name, pinned=False):
        kind = "eng" if self.kinds is None else self.kinds.get(name, "eng")
        if self.free[kind]:
            slot = self.free[kind].pop(0)
        else:
            slot = self.nslots
            self.nslots += 1
            assert self.nslots <= self.PHYS_MAX, "semaphore pool exhausted"
            self.phys[slot] = self.es.enter_context(self.nc.semaphore(f"S{slot}"))
            self.physv[slot] = 0
            self.slotkind[slot] = kind
        self.name2slot[name] = slot
        self.base[name] = self.physv[slot]
        if not pinned:
            self.cur_names.append(name)

    def sem(self, name):
        if name not in self.name2slot:
            self._adopt(name, pinned=(name == "SB"))
        return self.phys[self.name2slot[name]]

    def inc(self, instr, name, n=1):
        instr.then_inc(self.sem(name), n)
        self.physv[self.name2slot[name]] += n
        if n >= 16:
            q = str(getattr(instr, "ins", ""))
            kind = "sw" if "qPool" in q else "hw"
            if self.kinds_out.get(name) != "sw":
                self.kinds_out[name] = kind
        else:
            self.kinds_out.setdefault(name, "eng")
        return instr

    def eng_inc(self, eng, name, n=1):
        eng.sem_inc(self.sem(name), n)
        self.physv[self.name2slot[name]] += n

    def wge(self, eng, name, v):
        eng.wait_ge(self.sem(name), self.base[name] + v)
        if v > self.maxw.get(name, -1):
            self.maxw[name] = v

    def val(self, name):
        self.sem(name)
        return self.physv[self.name2slot[name]] - self.base[name]

    def sbuf(self, name, shape=None, dtype=None):
        if name not in self.sb:
            self.sb[name] = self.es.enter_context(self.nc.sbuf_tensor(name, shape, dtype))
        return self.sb[name]

    def psum(self, name, shape=None, dtype=F32):
        if name not in self.sb:
            self.sb[name] = self.es.enter_context(self.nc.psum_tensor(name, shape, dtype))
        return self.sb[name]

    def carve(self, off, shape, dtype):
        import math
        per = int(np.prod(shape[1:])) * mybir.dt.size(dtype)
        v = self.sb["ext"][0:shape[0], off:off + per].bitcast(dtype)
        if len(shape) == 2:
            return v
        return v.rearrange("p (a b) -> p a b", a=shape[1])

    def wait_sb(self, e):
        e.wait_ge(self.sem("SB"), self.barrier_target)

    def bar(self, instr, n=1):
        self.inc(instr, "SB", n)

    def bar_inc(self, eng):
        self.eng_inc(eng, "SB", 1)

    def stage_done(self):
        self.barrier_target = self.physv[self.name2slot["SB"]]
        recyclable = []
        for nm in self.cur_names:
            tot = self.physv[self.name2slot[nm]] - self.base[nm]
            if self.maxw.get(nm, 0) < tot:
                self.dangling.append((nm, self.maxw.get(nm, 0), tot))
            else:
                recyclable.append(nm)
        for nm in self.hold2:
            slot = self.name2slot[nm]
            self.free[self.slotkind[slot]].append(slot)
            del self.name2slot[nm]
            del self.base[nm]
        self.hold2 = self.hold1
        self.hold1 = [] if DISABLE_RECYCLE else recyclable
        self.cur_names = []


def alloc_pools(ctx):
    ctx.sbuf("pb0", [128, 2, 16384], BF16)
    ctx.sbuf("pb1", [128, 2, 4096], BF16)
    ctx.sbuf("pf0", [128, 2, 4096], F32)
    ctx.sbuf("pf1", [128, 4096], F32)
    ctx.sbuf("ysc", [128, 2, 4096], F32)
    ctx.sbuf("bcs", [128, 2, 4096], BF16)
    ctx.sbuf("st0", [128, 64], F32)
    ctx.sbuf("st1", [128, 64], F32)
    ctx.sbuf("stb", [128, 64], BF16)
    ctx.sbuf("ext", [128, 16384], mybir.dt.uint8)
    ctx.psum("pp0", [128, 4, 512], F32)
    ctx.psum("pp1", [128, 2, 512], F32)
    ctx.psum("ppt", [128, 2, 1024], BF16)
    ctx.sbuf("idt", [128, 128], BF16)
    ctx.sbuf("idtf", [128, 128], F32)
    ctx.sbuf("cst", [128, 24], F32)
    ctx.sbuf("cur1", [128, 20], F32)
    ctx.sbuf("cur2", [128, 20], F32)
    ctx.sbuf("cur3", [128, 20], F32)
    ctx.sbuf("sti2", [128, 64], I32)
    ctx.sbuf("sti3", [128, 64], I32)
    ctx.sbuf("nkvrow", [1, 1024], BF16)
    ctx.sbuf("nqrow", [1, 1024], F32)
    ctx.sbuf("ones1", [1, 1024], BF16)
    ctx.sbuf("ones1f", [1, 1024], F32)


def emit_init(ctx):
    nc = ctx.nc
    idt, idtf, cst = ctx.sb["idt"], ctx.sb["idtf"], ctx.sb["cst"]
    with nc.Block("init") as blk:
        @blk.gpsimd
        def _(g):
            g.memset(idt[:], 1.0)
            g.affine_select(out=idt[:], in_=idt[:], pattern=[[-1, 128]],
                            compare_op=ALU.is_equal, fill=0.0, base=0, channel_multiplier=1)
            g.memset(idtf[:], 1.0)
            g.affine_select(out=idtf[:], in_=idtf[:], pattern=[[-1, 128]],
                            compare_op=ALU.is_equal, fill=0.0, base=0, channel_multiplier=1)
            g.memset(ctx.sb["ones1"][:], 1.0)
            g.memset(ctx.sb["ones1f"][:], 1.0)
            for i, v in enumerate(CONST_COLS):
                g.memset(cst[:, i:i + 1], float(v))
            for i, cu in enumerate(CURS):
                g.memset(ctx.sb["cur1"][:, i:i + 1], float(np.float32(1.0 / cu)))
                g.memset(ctx.sb["cur3"][:, i:i + 1], float(np.float32(cu)))
                ins = g.memset(ctx.sb["cur2"][:, i:i + 1], float(np.float32(-cu)))
            ctx.bar(ins, 1)
    ctx.stage_done()


def emit_transpose_mat(ctx, tag, src_name, rows, cols, dst_name, dt=BF16):
    src_ap = ctx.aps[src_name]
    dst_ap = ctx.aps[dst_name]
    """dst = src^T stored [pcols(<=128), cols//128(>=1), rows] with
    dst[p, c, j] = src[j, c*128+p]. rows%128 may be partial on the last tile;
    cols may be < 128 (then single chunk of width cols)."""
    nc = ctx.nc
    nrt = (rows + 127) // 128
    nct = max(1, cols // 128)
    cw = min(128, cols)
    tin = ctx.sb["pb1"] if dt == BF16 else ctx.sb["pf0"]
    pst = ctx.sb["ppt"] if dt == BF16 else ctx.sb["pp0"]
    dst = ctx.sb["pb0"] if dt == BF16 else ctx.sb["pf1"]
    dstf = dst[:].rearrange("p a b -> p (a b)") if dt == BF16 else dst[:]
    idt = ctx.sb["idt"] if dt == BF16 else ctx.sb["idtf"]
    s_ld, s_mm, s_ev, s_st = tag + "l", tag + "m", tag + "e", tag + "s"
    npb = 2 if dt == BF16 else 4
    with nc.Block("tp" + tag) as blk:
        @blk.gpsimd
        def _(g):
            ctx.wait_sb(g)
            for r in range(nrt):
                rs = min(128, rows - r * 128)
                if r >= 2:
                    ctx.wge(g, s_mm, (r - 1) * nct)
                ctx.inc(g.dma_start(out=tin[0:rs, r % 2, 0:cols],
                                    in_=src_ap[r * 128:r * 128 + rs, :]), s_ld, 16)
        @blk.tensor
        def _(t):
            ctx.wait_sb(t)
            k = 0
            for r in range(nrt):
                rs = min(128, rows - r * 128)
                ctx.wge(t, s_ld, (r + 1) * 16)
                for c in range(nct):
                    if k >= npb:
                        ctx.wge(t, s_ev, k - npb + 1)
                    pv = pst[:, k % 2, 0:128] if dt == BF16 else pst[:, k % 4, 0:128]
                    ins = t.transpose(pv[0:cw, 0:rs], tin[0:rs, r % 2, c * 128:c * 128 + cw], idt[0:rs, 0:rs])
                    ctx.inc(ins, s_mm, 1)
                    k += 1
        @blk.vector
        def _(v):
            ctx.wait_sb(v)
            k = 0
            for r in range(nrt):
                rs = min(128, rows - r * 128)
                for c in range(nct):
                    ctx.wge(v, s_mm, k + 1)
                    pv = pst[:, k % 2, 0:128] if dt == BF16 else pst[:, k % 4, 0:128]
                    ins = v.tensor_copy(dstf[0:cw, c * rows + r * 128: c * rows + r * 128 + rs],
                                        pv[0:cw, 0:rs])
                    ctx.inc(ins, s_ev, 1)
                    k += 1
        @blk.sync
        def _(sy):
            ctx.wge(sy, s_ev, nrt * nct)
            ins = sy.dma_start(out=dst_ap.rearrange("p c r -> p (c r)"),
                               in_=dstf[0:cw, 0:nct * rows])
            ctx.inc(ins, s_st, 16)
            ctx.wge(sy, s_st, 16)
            ctx.bar_inc(sy)
    ctx.stage_done()


def emit_ternarize(ctx, tag, src_name, rows, cols, dst_name, invn_col):
    src_ap = ctx.aps[src_name]
    dst_ap = ctx.aps[dst_name]
    """BitLinear effective weight: per row scale=max(mean|w|,1e-5), w_eff =
    clip(round(w/scale),-1,1)*scale -> bf16 DRAM [rows, cols]."""
    nc = ctx.nc
    nrt = rows // 128
    tin = ctx.sb["pf0"]
    tw = ctx.sb["pf1"]
    ti = ctx.sb["ysc"][:, 0, :].bitcast(I32)
    tb = ctx.sb["pb1"]
    st0 = ctx.sb["st0"]
    cst = ctx.sb["cst"]
    s_ld, s_sc, s_ve, s_st = tag + "l", tag + "c", tag + "v", tag + "s"
    with nc.Block("tern" + tag) as blk:
        @blk.sync
        def _(sy):
            ctx.wait_sb(sy)
            for r in range(nrt):
                if r >= 2:
                    ctx.wge(sy, s_ve, r - 1)
                ctx.inc(sy.dma_start(out=tin[:, r % 2, 0:cols],
                                     in_=src_ap[r * 128:(r + 1) * 128, :]), s_ld, 16)
        @blk.scalar
        def _(s):
            ctx.wait_sb(s)
            for r in range(nrt):
                ctx.wge(s, s_ld, (r + 1) * 16)
                if r >= 1:
                    ctx.wge(s, s_ve, r)
                ins = s.activation(out=tb[:, 0, 0:cols], in_=tin[:, r % 2, 0:cols],
                                   func=AF.Abs, accum_out=st0[:, 0:1])
                ctx.inc(ins, s_sc, 1)
        @blk.vector
        def _(v):
            ctx.wait_sb(v)
            for r in range(nrt):
                ctx.wge(v, s_sc, r + 1)
                if r > 0:
                    ctx.wge(v, s_st, r * 16)
                v.tensor_scalar(out=st0[:, 1:2], in0=st0[:, 0:1], scalar1=cst[:, invn_col:invn_col + 1],
                                scalar2=cst[:, C_CLIP:C_CLIP + 1], op0=ALU.mult, op1=ALU.max)
                v.drain()
                v.reciprocal(st0[:, 2:3], st0[:, 1:2])
                v.drain()
                v.tensor_scalar(out=tw[:, 0:cols], in0=tin[:, r % 2, 0:cols], scalar1=st0[:, 2:3],
                                scalar2=None, op0=ALU.mult)
                yv = ctx.sb["ysc"][:, 0, :]
                v.tensor_scalar(out=yv[:, 0:cols], in0=tw[:, 0:cols], scalar1=cst[:, C_HALF:C_HALF + 1],
                                scalar2=None, op0=ALU.is_ge)
                v.tensor_scalar(out=tw[:, 0:cols], in0=tw[:, 0:cols], scalar1=cst[:, C_NEG1:C_NEG1 + 1],
                                scalar2=cst[:, C_HALF:C_HALF + 1], op0=ALU.mult, op1=ALU.is_ge)
                v.tensor_tensor(out=tw[:, 0:cols], in0=yv[:, 0:cols], in1=tw[:, 0:cols], op=ALU.subtract)
                ins = v.tensor_scalar(out=tb[:, 1, 0:cols], in0=tw[:, 0:cols], scalar1=st0[:, 1:2],
                                      scalar2=None, op0=ALU.mult)
                ctx.inc(ins, s_ve, 1)
        @blk.gpsimd
        def _(g):
            ctx.wait_sb(g)
            for r in range(nrt):
                ctx.wge(g, s_ve, r + 1)
                ins = g.dma_start(out=dst_ap[r * 128:(r + 1) * 128, :], in_=tb[:, 1, 0:cols])
                ctx.inc(ins, s_st, 16)
            ctx.wge(g, s_st, nrt * 16)
            ctx.bar_inc(g)
    ctx.stage_done()


def emit_wtl(ctx, dir_, outw_name, dst_name):
    nc = ctx.nc
    wout = ctx.sb["pb0"][:].rearrange("p a b -> p (a b)")
    fusT = ctx.sb["pb0"][:].rearrange("p a b -> p (a b)")[:, 16384:32768].rearrange("p (a b) -> p a b", a=16)
    stg = ctx.sb["ysc"][:].rearrange("p a b -> p (a b)").bitcast(BF16).rearrange("p (a b) -> p a b", a=16)
    pp = ctx.sb["pp1"]
    goff = 0 if dir_ == "f" else 8
    s_ld, s_mm, s_ev, s_st = [f"wt{dir_}{x}" for x in "lmes"]
    with nc.Block("wtl" + dir_) as blk:
        @blk.gpsimd
        def _(g):
            ctx.wait_sb(g)
            ctx.inc(g.dma_start(out=wout[:, 0:16384].rearrange("p (a b) -> p a b", a=8),
                                in_=ctx.aps[outw_name].rearrange("(a p) c -> p a c", p=128)), s_ld, 16)
            ctx.inc(g.dma_start(out=fusT[:], in_=ctx.aps["fusT"]), s_ld, 16)
        @blk.tensor
        def _(t):
            ctx.wait_sb(t)
            ctx.wge(t, s_ld, 32)
            k = 0
            for m in range(16):
                for nh in range(2):
                    if k >= 2:
                        ctx.wge(t, s_ev, k - 1)
                    for g in range(8):
                        ins = t.matmul(pp[:, k % 2, :],
                                       wout[:, 0:16384].rearrange("p (a b) -> p a b", a=8)[:, g, m * 128:(m + 1) * 128],
                                       fusT[:, goff + g, nh * 512:(nh + 1) * 512],
                                       start=(g == 0), stop=(g == 7))
                    ctx.inc(ins, s_mm, 1)
                    k += 1
        @blk.vector
        def _(v):
            ctx.wait_sb(v)
            for k in range(32):
                m, nh = divmod(k, 2)
                ctx.wge(v, s_mm, k + 1)
                ctx.inc(v.tensor_copy(stg[:, m, nh * 512:(nh + 1) * 512], pp[:, k % 2, :]), s_ev, 1)
        @blk.sync
        def _(sy):
            ctx.wge(sy, s_ev, 32)
            ctx.inc(sy.dma_start(out=ctx.aps[dst_name], in_=stg[:]), s_st, 16)
            ctx.wge(sy, s_st, 16)
            ctx.bar_inc(sy)
    ctx.stage_done()


def emit_ln(ctx, tag, src_name, dst_bf=None, dst_f32=None, dst_tm=None):
    nc = ctx.nc
    G = 4
    src_ap = ctx.aps[src_name]
    xin = ctx.sb["pf0"]
    xnb = ctx.sb["pb1"]
    ysc = ctx.sb["ysc"]
    st0 = ctx.sb["st0"]
    cst = ctx.sb["cst"]
    junk = ctx.sb["bcs"]
    pst = ctx.sb["ppt"]
    pstf = ctx.sb["pp0"]
    stgb = ctx.sb["pb0"][:].rearrange("p a b -> p (a b)")
    stgf = ctx.sb["pf1"]
    idt, idtf = ctx.sb["idt"], ctx.sb["idtf"]
    s_ld, s_r, s_sq, s_v2, s_rs, s_xn, s_mm, s_ev, s_st, s_tm = [tag + x for x in
        ("l", "r", "q", "v", "s", "n", "m", "e", "t", "o")]
    need_f32_tile = (dst_f32 is not None) or (dst_tm is not None)
    with nc.Block("ln" + tag) as blk:
        @blk.sync
        def _(sy):
            ctx.wait_sb(sy)
            for g in range(8):
                if g >= 2:
                    ctx.wge(sy, s_xn, (g - 1) * G)
                ctx.inc(sy.dma_start(out=xin[:, g % 2, :].rearrange("p (f d) -> p f d", f=G),
                                     in_=src_ap.rearrange("(g f p) d -> g p f d", g=8, f=G)[g]), s_ld, 16)
        @blk.vector
        def _(v):
            ctx.wait_sb(v)
            kev = 0
            for g in range(8):
                ctx.wge(v, s_ld, (g + 1) * 16)
                for i in range(G):
                    v.reduce_sum(st0[:, i:i + 1], xin[:, g % 2, i * 1024:(i + 1) * 1024],
                                 axis=AX.X, negate=True)
                v.drain()
                for i in range(G):
                    ins = v.tensor_scalar(out=st0[:, 8 + i:9 + i], in0=st0[:, i:i + 1],
                                          scalar1=cst[:, C_INVD:C_INVD + 1], scalar2=None, op0=ALU.mult)
                ctx.inc(ins, s_r, 1)
                ctx.wge(v, s_sq, g + 1)
                for i in range(G):
                    ins = v.tensor_scalar(out=st0[:, 16 + i:17 + i], in0=st0[:, 24 + i:25 + i],
                                          scalar1=cst[:, C_INVD:C_INVD + 1], scalar2=cst[:, C_EPS:C_EPS + 1],
                                          op0=ALU.mult, op1=ALU.add)
                ctx.inc(ins, s_v2, 1)
                ctx.wge(v, s_rs, g + 1)
                if g >= 2:
                    ctx.wge(v, s_mm, (g - 1) * G * NDC)
                if dst_tm is not None and g >= 2:
                    ctx.wge(v, s_tm, (g - 1) * G * 16)
                for i in range(G):
                    if dst_bf is not None:
                        ins = v.tensor_scalar(out=xnb[:, g % 2, i * 1024:(i + 1) * 1024],
                                              in0=xin[:, g % 2, i * 1024:(i + 1) * 1024],
                                              scalar1=st0[:, 8 + i:9 + i], scalar2=st0[:, 32 + i:33 + i],
                                              op0=ALU.add, op1=ALU.mult)
                    if need_f32_tile:
                        ins = v.tensor_scalar(out=ysc[:, g % 2, i * 1024:(i + 1) * 1024],
                                              in0=xin[:, g % 2, i * 1024:(i + 1) * 1024],
                                              scalar1=st0[:, 8 + i:9 + i], scalar2=st0[:, 32 + i:33 + i],
                                              op0=ALU.add, op1=ALU.mult)
                ctx.inc(ins, s_xn, G)
                for ge in ([g - 1] if g >= 1 else []) + ([7] if g == 7 else []):
                    for i in range(G):
                        ti = ge * G + i
                        for c in range(NDC):
                            ctx.wge(v, s_mm, kev + 1)
                            if dst_bf is not None:
                                ins = v.tensor_copy(stgb[:, c * 4096 + ti * 128: c * 4096 + (ti + 1) * 128],
                                                    pst[:, kev % 2, 0:128])
                            else:
                                if kev >= 32 and c == 0 and i == 0:
                                    ctx.wge(v, s_st, ge * NDC * 16)
                                ins = v.tensor_copy(stgf[:, c * 512 + i * 128: c * 512 + (i + 1) * 128],
                                                    pstf[:, kev % 4, 0:128])
                            ctx.inc(ins, s_ev, 1)
                            kev += 1
        @blk.scalar
        def _(s):
            ctx.wait_sb(s)
            for g in range(8):
                ctx.wge(s, s_r, g + 1)
                for i in range(G):
                    ins = s.activation(out=junk[:, 0, i * 1024:(i + 1) * 1024],
                                       in_=xin[:, g % 2, i * 1024:(i + 1) * 1024],
                                       func=AF.Square, bias=st0[:, 8 + i:9 + i], scale=1.0,
                                       accum_out=st0[:, 24 + i:25 + i])
                ctx.inc(ins, s_sq, 1)
                ctx.wge(s, s_v2, g + 1)
                for i in range(G):
                    s.activation(out=st0[:, 40 + i:41 + i], in_=st0[:, 16 + i:17 + i], func=AF.Ln)
                s.drain()
                for i in range(G):
                    ins = s.activation(out=st0[:, 32 + i:33 + i], in_=st0[:, 40 + i:41 + i],
                                       func=AF.Exp, scale=cst[:, C_NEGHALF:C_NEGHALF + 1])
                ctx.inc(ins, s_rs, 1)
        @blk.tensor
        def _(t):
            ctx.wait_sb(t)
            k = 0
            for g in range(8):
                for i in range(G):
                    ti = g * G + i
                    ctx.wge(t, s_xn, g * G + i + 1)
                    for c in range(NDC):
                        if dst_bf is not None:
                            if k >= 2:
                                ctx.wge(t, s_ev, k - 1)
                            src_t = xnb[:, g % 2, i * 1024 + c * 128:i * 1024 + (c + 1) * 128]
                            ins = t.transpose(pst[:, k % 2, 0:128], src_t, idt[:])
                        else:
                            if k >= 4:
                                ctx.wge(t, s_ev, k - 3)
                            src_t = ysc[:, g % 2, i * 1024 + c * 128:i * 1024 + (c + 1) * 128]
                            ins = t.transpose(pstf[:, k % 4, 0:128], src_t, idtf[:])
                        ctx.inc(ins, s_mm, 1)
                        k += 1
        @blk.gpsimd
        def _(g_):
            ctx.wait_sb(g_)
            for g in range(9):
                if g < 8 and dst_tm is not None:
                    for i in range(G):
                        ti = g * G + i
                        ctx.wge(g_, s_xn, g * G + i + 1)
                        ctx.inc(g_.dma_start(out=ctx.aps[dst_tm][ti * 128:(ti + 1) * 128, :],
                                             in_=ysc[:, g % 2, i * 1024:(i + 1) * 1024]), s_tm, 16)
                if dst_f32 is not None and g >= 1:
                    ge = g - 1
                    for c in range(NDC):
                        ctx.wge(g_, s_ev, ge * 32 + 24 + c + 1)
                        ins = g_.dma_start(out=ctx.aps[dst_f32][:, c, ge * 512:(ge + 1) * 512],
                                           in_=stgf[:, c * 512:(c + 1) * 512])
                        ctx.inc(ins, s_st, 16)

        @blk.scalar
        def _(s2):
            pass
    with nc.Block("ln_st" + tag) as blk:
        @blk.sync
        def _(sy):
            if dst_bf is not None:
                ctx.wge(sy, s_ev, NTT * NDC)
                ins = sy.dma_start(out=ctx.aps[dst_bf].rearrange("p c t -> p (c t)"), in_=stgb[:, 0:NDC * T])
                ctx.inc(ins, s_st, 16)
                ctx.wge(sy, s_st, ctx.val(s_st))
            else:
                ctx.wge(sy, s_st, ctx.val(s_st))
            if dst_tm is not None:
                ctx.wge(sy, s_tm, NTT * 16)
            ctx.bar_inc(sy)
    ctx.stage_done()


def emit_inproj(ctx, dir_):
    nc = ctx.nc
    w = ctx.sb["pb0"][:].rearrange("p a b -> p (a b)").rearrange("p (c m) -> p c m", c=NDC)
    xch = ctx.sb["pb1"]
    evb = ctx.sb["bcs"][:].rearrange("p a b -> p (a b)")
    pp = ctx.sb["pp0"]
    s_w, s_ld, s_mm, s_ev, s_st = [f"ip{dir_}{x}" for x in "wlmes"]
    s_ev2 = f"ip{dir_}z"
    xs_ap = ctx.aps[f"xs_{dir_}"]
    zg_ap = ctx.aps[f"zg_{dir_}"]
    with nc.Block("inproj" + dir_) as blk:
        @blk.gpsimd
        def _(g):
            ctx.wait_sb(g)
            ctx.inc(g.dma_start(out=w[:, :, :], in_=ctx.aps[f"winT_{dir_}"]), s_w, 16)
        @blk.sync
        def _(sy):
            ctx.wait_sb(sy)
            for ch in range(8):
                if ch >= 2:
                    ctx.wge(sy, s_mm, (ch - 1) * 32)
                ctx.inc(sy.dma_start(out=xch[:, ch % 2, :].rearrange("p (c t) -> p c t", c=NDC),
                                     in_=ctx.aps["xnf"][:, :, ch * 512:(ch + 1) * 512]), s_ld, 16)
        @blk.tensor
        def _(t):
            ctx.wait_sb(t)
            ctx.wge(t, s_w, 16)
            k = 0
            for ch in range(8):
                ctx.wge(t, s_ld, (ch + 1) * 16)
                for m in range(32):
                    if k >= 4:
                        kk = k - 4
                        nxs = sum(1 for j in range(kk + 1) if (j % 32) < 16)
                        nz = (kk + 1) - nxs
                        if nxs:
                            ctx.wge(t, s_ev, nxs)
                        if nz:
                            ctx.wge(t, s_ev2, nz)
                    for kc in range(NDC):
                        ins = t.matmul(pp[:, k % 4, :], w[:, kc, m * 128:(m + 1) * 128],
                                       xch[:, ch % 2, :].rearrange("p (c t) -> p c t", c=NDC)[:, kc, :],
                                       start=(kc == 0), stop=(kc == NDC - 1))
                    ctx.inc(ins, s_mm, 1)
                    k += 1
        @blk.vector
        def _(v):
            ctx.wait_sb(v)
            for k in range(256):
                ch, m = divmod(k, 32)
                if m >= 16:
                    continue
                ctx.wge(v, s_mm, k + 1)
                if k >= 16:
                    ctx.wge(v, s_st, (k - 15) * 16)
                ins = v.tensor_copy(evb[:, (k % 16) * 512:(k % 16 + 1) * 512], pp[:, k % 4, :])
                ctx.inc(ins, s_ev, 1)
        @blk.scalar
        def _(s):
            ctx.wait_sb(s)
            for k in range(256):
                ch, m = divmod(k, 32)
                if m < 16:
                    continue
                ctx.wge(s, s_mm, k + 1)
                if k >= 16:
                    ctx.wge(s, s_st, (k - 15) * 16)
                ins = s.activation(out=evb[:, (k % 16) * 512:(k % 16 + 1) * 512], in_=pp[:, k % 4, :],
                                   func=AF.Silu)
                ctx.inc(ins, s_ev2, 1)
        @blk.gpsimd
        def _(g):
            for k in range(256):
                ch, m = divmod(k, 32)
                ctx.wge(g, s_ev2 if m >= 16 else s_ev, (ch * 16 + (m - 16 if m >= 16 else m)) + 1)
                dst = xs_ap[:, m, ch * 512:(ch + 1) * 512] if m < 16 else zg_ap[:, m - 16, ch * 512:(ch + 1) * 512]
                ins = g.dma_start(out=dst, in_=evb[:, (k % 16) * 512:(k % 16 + 1) * 512])
                ctx.inc(ins, s_st, 16)
            ctx.wge(g, s_st, 256 * 16)
            ctx.bar_inc(g)
    ctx.stage_done()


def emit_conv(ctx, dir_):
    nc = ctx.nc
    st0, st1 = ctx.sb["st0"], ctx.sb["st1"]
    win = ctx.sb["pb1"]
    acc = ctx.sb["pb0"][:].rearrange("p a b -> p (a b)")
    s_c, s_ld, s_ve, s_si, s_st = [f"cv{dir_}{x}" for x in "clvis"]
    xs_ap, u_ap = ctx.aps[f"xs_{dir_}"], ctx.aps[f"u_{dir_}"]
    fwd = dir_ == "f"
    with nc.Block("conv" + dir_) as blk:
        @blk.gpsimd
        def _(g):
            ctx.wait_sb(g)
            ctx.inc(g.dma_start(out=st1[:, 0:64].rearrange("p (c k) -> p c k", k=4),
                    in_=ctx.aps["conv_w"].rearrange("(c p) one k -> p c (one k)", p=128)), s_c, 16)
            ins = g.dma_start(out=st0[:, 0:16], in_=ctx.aps["conv_b"].rearrange("(c p) -> p c", p=128), allow_slow_non_contiguous=True)
            ctx.inc(ins, s_c, 16)
            for idx in range(32):
                ct, hf = divmod(idx, 2)
                c0 = hf * 2048
                sl = idx % 2
                if idx >= 2:
                    ctx.wge(g, s_ve, (idx - 1) * 4)
                if fwd:
                    if ct == 0 or True:
                        pass
                    if hf == 0:
                        g.memset(win[:, sl, 0:3], 0.0)
                        ins = g.dma_start(out=win[:, sl, 3:2051], in_=xs_ap[:, ct, 0:2048])
                    else:
                        ins = g.dma_start(out=win[:, sl, 0:2051], in_=xs_ap[:, ct, 2045:4096])
                else:
                    if hf == 1:
                        g.memset(win[:, sl, 2048:2051], 0.0)
                        ins = g.dma_start(out=win[:, sl, 0:2048], in_=xs_ap[:, ct, 2048:4096])
                    else:
                        ins = g.dma_start(out=win[:, sl, 0:2051], in_=xs_ap[:, ct, 0:2051])
                ctx.inc(ins, s_ld, 16)
        @blk.vector
        def _(v):
            ctx.wait_sb(v)
            ctx.wge(v, s_c, 32)
            for idx in range(32):
                ct, hf = divmod(idx, 2)
                sl = idx % 2
                a = acc[:, sl * 2048:(sl + 1) * 2048]
                ctx.wge(v, s_ld, (idx + 1) * 16)
                if idx >= 2:
                    ctx.wge(v, s_si, idx - 1)
                taps = range(4)
                for j, tap in enumerate(taps):
                    wcol = st1[:, ct * 4 + (tap if fwd else 3 - tap): ct * 4 + (tap if fwd else 3 - tap) + 1]
                    if j == 0:
                        ins = v.tensor_scalar(out=a, in0=win[:, sl, tap:tap + 2048], scalar1=wcol,
                                              scalar2=None, op0=ALU.mult)
                    else:
                        ins = v.scalar_tensor_tensor(out=a, in0=win[:, sl, tap:tap + 2048], scalar=wcol,
                                                     in1=a, op0=ALU.mult, op1=ALU.add)
                    ctx.inc(ins, s_ve, 1)
        @blk.scalar
        def _(s):
            ctx.wait_sb(s)
            for idx in range(32):
                ct, hf = divmod(idx, 2)
                sl = idx % 2
                ctx.wge(s, s_ve, (idx + 1) * 4)
                if idx >= 2:
                    ctx.wge(s, s_st, (idx - 1) * 16)
                ins = s.activation(out=acc[:, 4096 + sl * 2048: 4096 + (sl + 1) * 2048],
                                   in_=acc[:, sl * 2048:(sl + 1) * 2048],
                                   func=AF.Silu, bias=st0[:, ct:ct + 1], scale=1.0)
                ctx.inc(ins, s_si, 1)
        @blk.sync
        def _(sy):
            for idx in range(32):
                ct, hf = divmod(idx, 2)
                sl = idx % 2
                ctx.wge(sy, s_si, idx + 1)
                ins = sy.dma_start(out=u_ap[:, ct, hf * 2048:(hf + 1) * 2048],
                                   in_=acc[:, 4096 + sl * 2048: 4096 + (sl + 1) * 2048])
                ctx.inc(ins, s_st, 16)
            ctx.wge(sy, s_st, 32 * 16)
            ctx.bar_inc(sy)
    ctx.stage_done()


def emit_xdbl(ctx, dir_):
    nc = ctx.nc
    wxp = ctx.carve(0, [128, NCC, 96], BF16)
    wdt = ctx.carve(3072, [64, C], BF16)
    st0 = ctx.sb["st0"]
    uch = ctx.sb["pb0"][:].rearrange("p a b -> p (a b)")
    xd = ctx.carve(7168, [96, 2, 512], BF16)
    ebuf = ctx.sb["pf1"]
    dlo = ctx.sb["pf0"]
    dub = ctx.sb["bcs"][:].rearrange("p a b -> p (a b)")
    pp1 = ctx.sb["pp1"]
    pp = ctx.sb["pp0"]
    s_w, s_ld, s_m1, s_e1, s_m2, s_sp, s_du, s_st, s_bc = [f"xd{dir_}{x}" for x in "wlaebpdsc"]
    u_ap, dl_ap, du_ap, bc_ap = [ctx.aps[f"{n}_{dir_}"] for n in ("u", "dl", "du", "bc")]
    with nc.Block("xdbl" + dir_) as blk:
        @blk.gpsimd
        def _(g):
            ctx.wait_sb(g)
            ctx.inc(g.dma_start(out=wxp[:], in_=ctx.aps[f"wxpT_{dir_}"]), s_w, 16)
            ctx.inc(g.dma_start(out=wdt[:], in_=ctx.aps[f"wdtT_{dir_}"].rearrange("a one b -> a (one b)")), s_w, 16)
            ctx.inc(g.dma_start(out=st0[:, 0:16], in_=ctx.aps["dt_proj_b"].rearrange("(c p) -> p c", p=128), allow_slow_non_contiguous=True), s_w, 16)
            for ch in range(8):
                if ch >= 2:
                    ctx.wge(g, s_du, (ch - 1) * 16)
                ins = g.dma_start(out=uch[:, (ch % 2) * 8192:(ch % 2) * 8192 + 8192].rearrange("p (c t) -> p c t", c=NCC),
                                  in_=u_ap[:, :, ch * 512:(ch + 1) * 512])
                ctx.inc(ins, s_ld, 16)
        @blk.tensor
        def _(t):
            ctx.wait_sb(t)
            ctx.wge(t, s_w, 48)
            for ch in range(8):
                ctx.wge(t, s_ld, (ch + 1) * 16)
                if ch >= 2:
                    ctx.wge(t, s_bc, (ch - 1) * 16)
                uv = uch[:, (ch % 2) * 8192:(ch % 2) * 8192 + 8192].rearrange("p (c t) -> p c t", c=NCC)
                for kc in range(NCC):
                    ins = t.matmul(pp1[0:96, 0, :], wxp[:, kc, :], uv[:, kc, :],
                                   start=(kc == 0), stop=(kc == NCC - 1))
                ctx.inc(ins, s_m1, 1)
                ctx.wge(t, s_e1, ch + 1)
                for mt in range(16):
                    k = ch * 16 + mt
                    if k >= 4:
                        ctx.wge(t, s_sp, k - 3)
                    ins = t.matmul(pp[:, mt % 4, :], wdt[0:64, mt * 128:(mt + 1) * 128], xd[0:64, ch % 2, 0:512],
                                   start=True, stop=True)
                    ctx.inc(ins, s_m2, 1)
        @blk.vector
        def _(v):
            ctx.wait_sb(v)
            for ch in range(8):
                ctx.wge(v, s_m1, ch + 1)
                ins = v.tensor_copy(xd[0:96, ch % 2, 0:512], pp1[0:96, 0, :])
                ctx.inc(ins, s_e1, 1)
                uv = uch[:, (ch % 2) * 8192:(ch % 2) * 8192 + 8192].rearrange("p (c t) -> p c t", c=NCC)
                for mt in range(16):
                    k = ch * 16 + mt
                    ctx.wge(v, s_sp, k + 1)
                    if k >= 8:
                        ctx.wge(v, s_st, (k - 7) * 32)
                    ins = v.tensor_mul(dub[:, (k % 8) * 512:(k % 8 + 1) * 512],
                                       dlo[:, mt % 2, (ch % 4) * 1024:(ch % 4) * 1024 + 512] if False else
                                       dlo[:, k % 2, 0:512], uv[:, mt, :])
                    ctx.inc(ins, s_du, 1)
        @blk.scalar
        def _(s):
            ctx.wait_sb(s)
            for ch in range(8):
                for mt in range(16):
                    k = ch * 16 + mt
                    ctx.wge(s, s_m2, k + 1)
                    if k >= 2:
                        ctx.wge(s, s_du, k - 1)
                        ctx.wge(s, s_st, (2 * (k - 2) + 1) * 16)
                    s.activation(out=ebuf[:, (k % 2) * 2048:(k % 2) * 2048 + 512], in_=pp[:, mt % 4, :],
                                 func=AF.Exp, bias=st0[:, ct_col(mt):ct_col(mt) + 1], scale=1.0)
                    ins = s.activation(out=dlo[:, k % 2, 0:512], in_=ebuf[:, (k % 2) * 2048:(k % 2) * 2048 + 512],
                                       func=AF.Ln, bias=1.0, scale=1.0)
                    ctx.inc(ins, s_sp, 1)
        @blk.sync
        def _(sy):
            for ch in range(8):
                ctx.wge(sy, s_e1, ch + 1)
                ins = sy.dma_start(out=bc_ap[:, ch * 512:(ch + 1) * 512], in_=xd[64:96, ch % 2, 0:512])
                ctx.inc(ins, s_bc, 16)
                for mt in range(16):
                    k = ch * 16 + mt
                    ctx.wge(sy, s_sp, k + 1)
                    ins = sy.dma_start(out=dl_ap[:, mt, ch * 512:(ch + 1) * 512], in_=dlo[:, k % 2, 0:512])
                    ctx.inc(ins, s_st, 16)
                    ctx.wge(sy, s_du, k + 1)
                    ins = sy.dma_start(out=du_ap[:, mt, ch * 512:(ch + 1) * 512],
                                       in_=dub[:, (k % 8) * 512:(k % 8 + 1) * 512])
                    ctx.inc(ins, s_st, 16)
            ctx.wge(sy, s_st, 256 * 16)
            ctx.wge(sy, s_bc, 8 * 16)
            ctx.bar_inc(sy)
    ctx.stage_done()


def ct_col(mt):
    return mt


def emit_scan(ctx, dir_):
    nc = ctx.nc
    dl_ap, du_ap, bc_ap, y_ap = [ctx.aps[f"{n}_{dir_}"] for n in ("dl", "du", "bc", "y")]
    dlt = ctx.sb["pf0"]
    dut = ctx.sb["pb1"]
    bct = ctx.sb["bcs"]
    av = ctx.sb["pf1"]
    pb0f = ctx.sb["pb0"][:].rearrange("p a b -> p (a b)")
    yt = ctx.sb["ysc"]
    asb = ctx.carve(0, [128, 256], F32)
    s_A, s_ld, s_bc2, s_a, s_scan, s_y, s_st = [f"sc{dir_}{x}" for x in "Albzcys"]
    s_A2 = f"sc{dir_}B"
    HL = 2048
    with nc.Block("scan" + dir_) as blk:
        @blk.gpsimd
        def _(g):
            ctx.wait_sb(g)
            ctx.inc(g.dma_start(out=asb[:].rearrange("p (c s) -> p c s", s=16), in_=ctx.aps["A_log"].rearrange("(c p) s -> p c s", p=128)), s_A, 16)
            for grp in range(8):
                if grp >= 1:
                    ctx.wge(g, s_st, grp * 2 * 16)
                for j in range(2):
                    ct = grp * 2 + j
                    ctx.inc(g.dma_start(out=dlt[:, j, :], in_=dl_ap[:, ct, :]), s_ld, 16)
                    ctx.inc(g.dma_start(out=dut[:, j, :], in_=du_ap[:, ct, :]), s_ld, 16)
                for s in range(16):
                    gs = grp * 16 + s
                    if gs >= 2:
                        ctx.wge(g, s_y, (gs - 1) * 2)
                    bdst = bct[:, 0, :] if s % 2 == 0 else pb0f[:, 12288:16384]
                    cdst = bct[:, 1, :] if s % 2 == 0 else pb0f[:, 16384:20480]
                    ctx.inc(g.dma_start(out=bdst, in_=bc_ap[s:s + 1, :].partition_broadcast(128)), s_bc2, 16)
                    ctx.inc(g.dma_start(out=cdst, in_=bc_ap[16 + s:17 + s, :].partition_broadcast(128)), s_bc2, 16)
        @blk.scalar
        def _(sc):
            ctx.wait_sb(sc)
            ctx.wge(sc, s_A, 16)
            ctx.inc(sc.activation(out=asb[:], in_=asb[:], func=AF.Exp), s_A2, 1)
            ctx.wge(sc, s_A2, 2)
            na = 0
            hfs = (0, 1) if dir_ == "f" else (1, 0)
            for grp in range(8):
                ctx.wge(sc, s_ld, (grp + 1) * 4 * 16)
                for s in range(16):
                    for j in range(2):
                        ct = grp * 2 + j
                        for hf in hfs:
                            if na >= 2:
                                ctx.wge(sc, s_scan, na - 1)
                            if dir_ == "f":
                                ins = sc.activation(out=av[:, hf * HL:(hf + 1) * HL],
                                                    in_=dlt[:, j, hf * HL:(hf + 1) * HL],
                                                    func=AF.Exp, scale=asb[:, ct * 16 + s:ct * 16 + s + 1])
                            else:
                                ins = sc.activation(out=av[:, hf * HL:(hf + 1) * HL][:, ::-1],
                                                    in_=dlt[:, j, hf * HL:(hf + 1) * HL][:, ::-1],
                                                    func=AF.Exp, scale=asb[:, ct * 16 + s:ct * 16 + s + 1])
                            ctx.inc(ins, s_a, 1)
                            na += 1
        @blk.vector
        def _(v):
            ctx.wait_sb(v)
            ctx.wge(v, s_A2, 1)
            ins = v.tensor_scalar(out=asb[:], in0=asb[:], scalar1=ctx.sb["cst"][:, C_NEG1:C_NEG1 + 1],
                                  scalar2=None, op0=ALU.mult)
            ctx.inc(ins, s_A2, 1)
            na = 0
            for grp in range(8):
                ctx.wge(v, s_ld, (grp + 1) * 4 * 16)
                if grp >= 1:
                    ctx.wge(v, s_st, grp * 2 * 16)
                v.memset(yt[:], 0.0)
                for s in range(16):
                    gs = grp * 16 + s
                    ctx.wge(v, s_bc2, (gs + 1) * 32)
                    brep = bct[:, 0, :] if s % 2 == 0 else pb0f[:, 12288:16384]
                    crep = bct[:, 1, :] if s % 2 == 0 else pb0f[:, 16384:20480]
                    hfs = (0, 1) if dir_ == "f" else (1, 0)
                    for j in range(2):
                        for ihf, hf in enumerate(hfs):
                            if dir_ == "f":
                                ins = v.tensor_tensor(out=pb0f[:, hf * HL:(hf + 1) * HL],
                                                      in0=dut[:, j, hf * HL:(hf + 1) * HL],
                                                      in1=brep[:, hf * HL:(hf + 1) * HL], op=ALU.mult)
                            else:
                                ins = v.tensor_tensor(out=pb0f[:, hf * HL:(hf + 1) * HL][:, ::-1],
                                                      in0=dut[:, j, hf * HL:(hf + 1) * HL][:, ::-1],
                                                      in1=brep[:, hf * HL:(hf + 1) * HL][:, ::-1], op=ALU.mult)
                            ctx.wge(v, s_a, na + 1)
                            na += 1
                            if dir_ == "f":
                                init = 0.0 if ihf == 0 else pb0f[:, 4096 + HL - 1:4096 + HL]
                                ins = v.tensor_tensor_scan(out=pb0f[:, 4096 + hf * HL:4096 + (hf + 1) * HL],
                                                           data0=av[:, hf * HL:(hf + 1) * HL],
                                                           data1=pb0f[:, hf * HL:(hf + 1) * HL],
                                                           initial=init, op0=ALU.mult, op1=ALU.add)
                            else:
                                init = 0.0 if ihf == 0 else pb0f[:, 4096 + HL:4096 + HL + 1]
                                ins = v.tensor_tensor_scan(out=pb0f[:, 4096 + hf * HL:4096 + (hf + 1) * HL][:, ::-1],
                                                           data0=av[:, hf * HL:(hf + 1) * HL][:, ::-1],
                                                           data1=pb0f[:, hf * HL:(hf + 1) * HL][:, ::-1],
                                                           initial=init, op0=ALU.mult, op1=ALU.add)
                            ctx.inc(ins, s_scan, 1)
                        if dir_ == "f":
                            v.tensor_tensor(out=pb0f[:, 8192:12288], in0=pb0f[:, 4096:8192], in1=crep, op=ALU.mult)
                            ins = v.tensor_tensor(out=yt[:, j, :], in0=yt[:, j, :], in1=pb0f[:, 8192:12288], op=ALU.add)
                            ctx.inc(ins, s_y, 1)
                        else:
                            v.tensor_tensor(out=pb0f[:, 8192:12288][:, ::-1], in0=pb0f[:, 4096:8192][:, ::-1],
                                            in1=crep[:, ::-1], op=ALU.mult)
                            if s == 0:
                                ins = v.tensor_copy(yt[:, j, :][:, ::-1], pb0f[:, 8192:12288][:, ::-1])
                            else:
                                ins = v.tensor_tensor(out=yt[:, j, :][:, ::-1], in0=yt[:, j, :][:, ::-1],
                                                      in1=pb0f[:, 8192:12288][:, ::-1], op=ALU.add)
                            if s == 15:
                                pass
                            else:
                                ctx.inc(ins, s_y, 1)
                    if dir_ == "b" and s == 15:
                        v.drain()
                        ins = v.tensor_copy(ctx.sb["st0"][:, 62:63], ctx.sb["cst"][:, C_ZERO:C_ZERO + 1])
                        ctx.inc(ins, s_y, 2)
        @blk.sync
        def _(sy):
            for grp in range(8):
                for j in range(2):
                    ct = grp * 2 + j
                    ctx.wge(sy, s_y, grp * 32 + 32)
                    ins = sy.dma_start(out=y_ap[:, ct, :], in_=yt[:, j, :])
                    ctx.inc(ins, s_st, 16)
            ctx.wge(sy, s_st, 16 * 16)
            ctx.bar_inc(sy)
    ctx.stage_done()


def dve_count_scan(si):
    return (si + 1) * 2


def emit_ffnh(ctx):
    nc = ctx.nc
    w1 = ctx.sb["pb0"][:].rearrange("p a b -> p (a b)").rearrange("p (c m) -> p c m", c=NDC)
    xch = ctx.sb["pb1"]
    vtt = ctx.carve(0, [128, NDC, 16], BF16)
    utt = ctx.carve(512, [16, 4096], BF16)
    fbt = ctx.carve(8704, [16, 16], BF16)
    g16 = ctx.carve(8768, [16, 1], F32)
    bcs = ctx.sb["bcs"]
    l1b = ctx.sb["bcs"][0:16, 0, 3584:4096]
    l2g = ctx.sb["bcs"][0:16, 1, 3584:4096]
    relb = ctx.sb["pf1"]
    pp = ctx.sb["pp0"]
    pp1 = ctx.sb["pp1"]
    s_w, s_ld, s_l1, s_l1e, s_l2, s_l2e, s_mm, s_re, s_sq, s_st = [f"fh{x}" for x in
        ("w", "l", "a", "b", "c", "d", "m", "r", "q", "s")]
    with nc.Block("ffnh") as blk:
        @blk.gpsimd
        def _(g):
            ctx.wait_sb(g)
            ctx.inc(g.dma_start(out=w1[:, :, :], in_=ctx.aps["w1T"]), s_w, 16)
            ctx.inc(g.dma_start(out=vtt[:], in_=ctx.aps["vTT"]), s_w, 16)
            ctx.inc(g.dma_start(out=utt[:], in_=ctx.aps["uTT"].rearrange("a one b -> a (one b)")), s_w, 16)
            ctx.inc(g.dma_start(out=fbt[:], in_=ctx.aps["fbT"].rearrange("a one b -> a (one b)")), s_w, 16)
            ctx.inc(g.dma_start(out=g16[:], in_=ctx.aps["L0_ffn_gate"].unsqueeze(1).partition_broadcast(16)), s_w, 16)
            for ch in range(8):
                if ch >= 2:
                    ctx.wge(g, s_mm, (ch - 1) * 32)
                ctx.inc(g.dma_start(out=xch[:, ch % 2, :].rearrange("p (c t) -> p c t", c=NDC),
                                    in_=ctx.aps["xnf"][:, :, ch * 512:(ch + 1) * 512]), s_ld, 16)
        @blk.tensor
        def _(t):
            ctx.wait_sb(t)
            ctx.wge(t, s_w, 80)
            for ch in range(8):
                ctx.wge(t, s_ld, (ch + 1) * 16)
                xv = xch[:, ch % 2, :].rearrange("p (c t) -> p c t", c=NDC)
                if ch >= 1:
                    ctx.wge(t, s_l2e, ch)
                for kc in range(NDC):
                    ins = t.matmul(pp1[0:16, 0, :], vtt[:, kc, :], xv[:, kc, :],
                                   start=(kc == 0), stop=(kc == NDC - 1))
                ctx.inc(ins, s_l1, 1)
                ctx.wge(t, s_l1e, ch + 1)
                ins = t.matmul(pp1[0:16, 1, :], fbt[:], l1b[:], start=True, stop=True)
                ctx.inc(ins, s_l2, 1)
                ctx.wge(t, s_l2e, ch + 1)
                for ht in range(32):
                    k = ch * 32 + ht
                    if k >= 4:
                        ctx.wge(t, s_re, k - 3)
                    for kc in range(NDC):
                        t.matmul(pp[:, k % 4, :], w1[:, kc, ht * 128:(ht + 1) * 128], xv[:, kc, :],
                                 start=(kc == 0), stop=False)
                    ins = t.matmul(pp[:, k % 4, :], utt[0:16, ht * 128:(ht + 1) * 128], l2g[:],
                                   start=False, stop=True)
                    ctx.inc(ins, s_mm, 1)
        @blk.vector
        def _(v):
            ctx.wait_sb(v)
            for ch in range(8):
                ctx.wge(v, s_l1, ch + 1)
                ctx.inc(v.tensor_copy(l1b[:], pp1[0:16, 0, :]), s_l1e, 1)
                ctx.wge(v, s_l2, ch + 1)
                ctx.inc(v.tensor_scalar(out=l2g[:], in0=pp1[0:16, 1, :], scalar1=g16[:, 0:1],
                                        scalar2=None, op0=ALU.mult), s_l2e, 1)
                for ht in range(32):
                    k = ch * 32 + ht
                    ctx.wge(v, s_mm, k + 1)
                    if k >= 2:
                        ctx.wge(v, s_sq, k - 1)
                    ins = v.tensor_relu(relb[:, (k % 2) * 2048:(k % 2) * 2048 + 512], pp[:, k % 4, :])
                    ctx.inc(ins, s_re, 1)
        @blk.scalar
        def _(s):
            ctx.wait_sb(s)
            for k in range(256):
                ctx.wge(s, s_re, k + 1)
                if k >= 4:
                    ctx.wge(s, s_st, (k - 3) * 16)
                ins = s.activation(out=bcs[:, 0, (k % 4) * 512:(k % 4 + 1) * 512],
                                   in_=relb[:, (k % 2) * 2048:(k % 2) * 2048 + 512], func=AF.Square)
                ctx.inc(ins, s_sq, 1)
        @blk.sync
        def _(sy):
            for k in range(256):
                ch, ht = divmod(k, 32)
                ctx.wge(sy, s_sq, k + 1)
                ins = sy.dma_start(out=ctx.aps["r2l0"][:, ht, ch * 512:(ch + 1) * 512],
                                   in_=bcs[:, 0, (k % 4) * 512:(k % 4 + 1) * 512])
                ctx.inc(ins, s_st, 16)
            ctx.wge(sy, s_st, 256 * 16)
            ctx.bar_inc(sy)
    ctx.stage_done()


def emit_l0out_mamba(ctx):
    nc = ctx.nc
    w = ctx.sb["pb0"][:].rearrange("p a b -> p (a b)")
    yb = ctx.sb["pb1"]
    aux = ctx.sb["pf0"][:].rearrange("p a b -> p (a b)").bitcast(BF16)
    dsb = ctx.carve(0, [128, 32], BF16)
    fbr = ctx.carve(64, [1, 1024], BF16)
    ev = ctx.sb["pf1"]
    pp1 = ctx.sb["pp1"]
    s_w, s_ld, s_yp, s_mm, s_ev, s_st = [f"lm{x}" for x in "wlpmes"]
    with nc.Block("l0outm") as blk:
        @blk.gpsimd
        def _(g):
            ctx.wait_sb(g)
            ctx.inc(g.dma_start(out=w[:, 0:16384].rearrange("p (c m) -> p c m", c=NCC), in_=ctx.aps["wtlT_f"]), s_w, 16)
            ctx.inc(g.dma_start(out=w[:, 16384:32768].rearrange("p (c m) -> p c m", c=NCC), in_=ctx.aps["wtlT_b"]), s_w, 16)
            ctx.inc(g.dma_start(out=dsb[:, 0:16], in_=ctx.aps["L0_attn_fwd_D"].rearrange("(c p) -> p c", p=128), allow_slow_non_contiguous=True), s_w, 16)
            ctx.inc(g.dma_start(out=dsb[:, 16:32], in_=ctx.aps["L0_attn_bwd_D"].rearrange("(c p) -> p c", p=128), allow_slow_non_contiguous=True), s_w, 16)
            ctx.inc(g.dma_start(out=fbr[:], in_=ctx.aps["L0_attn_fusion_b"].rearrange("(one e) -> one e", one=1)), s_w, 16)
            for pr in range(16):
                if pr >= 1:
                    ctx.wge(g, s_yp, pr * 4)
                    ctx.wge(g, s_mm, pr * 4)
                t0 = pr * 256
                for di, d in enumerate("fb"):
                    ctx.inc(g.dma_start(out=aux[:, di * 4096:(di + 1) * 4096].rearrange("p (c t) -> p c t", c=NCC),
                                        in_=ctx.aps[f"y_{d}"][:, :, t0:t0 + 256]), s_ld, 16)
                    ctx.inc(g.dma_start(out=yb[:, di, :].rearrange("p (c t) -> p c t", c=NCC),
                                        in_=ctx.aps[f"u_{d}"][:, :, t0:t0 + 256]), s_ld, 16)
                    ctx.inc(g.dma_start(out=aux[:, 8192 + di * 4096:8192 + (di + 1) * 4096].rearrange("p (c t) -> p c t", c=NCC),
                                        in_=ctx.aps[f"zg_{d}"][:, :, t0:t0 + 256]), s_ld, 16)
        @blk.vector
        def _(v):
            ctx.wait_sb(v)
            ctx.wge(v, s_w, 80)
            for pr in range(16):
                ctx.wge(v, s_ld, (pr + 1) * 96)
                for di in range(2):
                    uv = yb[:, di, :].rearrange("p (c t) -> p c t", c=NCC)
                    yv = aux[:, di * 4096:(di + 1) * 4096].rearrange("p (c t) -> p c t", c=NCC)
                    zv = aux[:, 8192 + di * 4096:8192 + (di + 1) * 4096].rearrange("p (c t) -> p c t", c=NCC)
                    dcol = dsb[:, di * 16:di * 16 + 16]
                    v.tensor_tensor(out=uv[:, :, :], in0=uv[:, :, :],
                                    in1=dcol.unsqueeze(2).broadcast_to([128, 16, 256]), op=ALU.mult)
                    v.tensor_tensor(out=uv[:, :, :], in0=uv[:, :, :], in1=yv[:, :, :], op=ALU.add)
                    ins = v.tensor_tensor(out=uv[:, :, :], in0=uv[:, :, :], in1=zv[:, :, :], op=ALU.mult)
                    ctx.inc(ins, s_yp, 2)
        @blk.tensor
        def _(t):
            ctx.wait_sb(t)
            for pr in range(16):
                ctx.wge(t, s_yp, (pr + 1) * 4)
                for sub in range(2):
                    k = pr * 2 + sub
                    if k >= 1:
                        ctx.wge(t, s_ev, k)
                    for nh in range(2):
                        first = True
                        for di in range(2):
                            yv = yb[:, di, :].rearrange("p (c t) -> p c t", c=NCC)
                            for kc in range(NCC):
                                t.matmul(pp1[:, nh, :], yv[:, kc, sub * 128:(sub + 1) * 128],
                                         w[:, di * 16384:(di + 1) * 16384].rearrange("p (c m) -> p c m", c=NCC)[:, kc, nh * 512:(nh + 1) * 512],
                                         start=first, stop=False)
                                first = False
                        ins = t.matmul(pp1[:, nh, :], ctx.sb["ones1"][0:1, 0:128],
                                       fbr[0:1, nh * 512:(nh + 1) * 512], start=False, stop=True)
                    ctx.inc(ins, s_mm, 2)
        @blk.scalar
        def _(s):
            ctx.wait_sb(s)
            for k in range(32):
                ctx.wge(s, s_mm, (k + 1) * 2)
                if k >= 2:
                    ctx.wge(s, s_st, (k - 1) * 16)
                ins = s.activation(out=ev[:, (k % 2) * 1024:(k % 2) * 1024 + 1024],
                                   in_=pp1[:, 0:2, :].rearrange("p a b -> p (a b)"), func=AF.Copy)
                ctx.inc(ins, s_ev, 1)
        @blk.sync
        def _(sy):
            for k in range(32):
                ctx.wge(sy, s_ev, k + 1)
                ins = sy.dma_start(out=ctx.aps["x1a"][k * 128:(k + 1) * 128, :],
                                   in_=ev[:, (k % 2) * 1024:(k % 2) * 1024 + 1024])
                ctx.inc(ins, s_st, 16)
            ctx.wge(sy, s_st, 32 * 16)
            ctx.bar_inc(sy)
    ctx.stage_done()


def emit_l0out_ffn(ctx):
    nc = ctx.nc
    w = ctx.sb["pb0"][:].rearrange("p a b -> p (a b)").rearrange("p (c m) -> p c m", c=32)
    r2 = ctx.sb["pb1"]
    xres = ctx.sb["pf0"]
    ev = ctx.sb["pf1"]
    pp1 = ctx.sb["pp1"]
    s_w, s_ld, s_mm, s_ev, s_st = [f"lf{x}" for x in "wlmes"]
    with nc.Block("l0outf") as blk:
        @blk.gpsimd
        def _(g):
            ctx.wait_sb(g)
            ctx.inc(g.dma_start(out=w[:, :, :], in_=ctx.aps["w2T"]), s_w, 16)
            for pr in range(16):
                t0 = pr * 256
                if pr >= 1:
                    ctx.wge(g, s_mm, pr * 4)
                ctx.inc(g.dma_start(out=r2[:, 0, :].rearrange("p (c t) -> p c t", c=32) if False else
                                    r2[:].rearrange("p a (c t) -> p (a c) t", c=16)[:, :, :],
                                    in_=ctx.aps["r2l0"][:, :, t0:t0 + 256]), s_ld, 16)
                if pr >= 2:
                    ctx.wge(g, s_ev, (pr - 1) * 2)
                ctx.inc(g.dma_start(out=xres[:, 0, :].rearrange("p (h s d) -> p h s d", h=2, s=2)[:, pr % 2],
                                    in_=ctx.aps["x_in"][t0:t0 + 256, :].rearrange("(s p) d -> p s d", p=128)), s_ld, 16)
                ctx.inc(g.dma_start(out=xres[:, 1, :].rearrange("p (h s d) -> p h s d", h=2, s=2)[:, pr % 2],
                                    in_=ctx.aps["x1a"][t0:t0 + 256, :].rearrange("(s p) d -> p s d", p=128)), s_ld, 16)
        @blk.tensor
        def _(t):
            ctx.wait_sb(t)
            ctx.wge(t, s_w, 16)
            for pr in range(16):
                ctx.wge(t, s_ld, (pr + 1) * 48)
                for sub in range(2):
                    k = pr * 2 + sub
                    if k >= 1:
                        ctx.wge(t, s_ev, k)
                    for nh in range(2):
                        for kc in range(32):
                            ins = t.matmul(pp1[:, nh, :],
                                           r2[:].rearrange("p a (c t) -> p (a c) t", c=16)[:, kc, sub * 128:(sub + 1) * 128],
                                           w[:, kc, nh * 512:(nh + 1) * 512],
                                           start=(kc == 0), stop=(kc == 31))
                    ctx.inc(ins, s_mm, 2)
        @blk.vector
        def _(v):
            ctx.wait_sb(v)
            for k in range(32):
                pr, sub = divmod(k, 2)
                ctx.wge(v, s_mm, (k + 1) * 2)
                if k >= 2:
                    ctx.wge(v, s_st, (k - 1) * 16)
                xv = xres[:, 0, :].rearrange("p (h s d) -> p h s d", h=2, s=2)[:, pr % 2, sub]
                av = xres[:, 1, :].rearrange("p (h s d) -> p h s d", h=2, s=2)[:, pr % 2, sub]
                v.tensor_add(ev[:, (k % 2) * 1024:(k % 2) * 1024 + 1024],
                             pp1[:, 0:2, :].rearrange("p a b -> p (a b)"), xv)
                ins = v.tensor_add(ev[:, (k % 2) * 1024:(k % 2) * 1024 + 1024],
                                   ev[:, (k % 2) * 1024:(k % 2) * 1024 + 1024], av)
                ctx.inc(ins, s_ev, 1)
        @blk.sync
        def _(sy):
            for k in range(32):
                ctx.wge(sy, s_ev, k + 1)
                ins = sy.dma_start(out=ctx.aps["x1"][k * 128:(k + 1) * 128, :],
                                   in_=ev[:, (k % 2) * 1024:(k % 2) * 1024 + 1024])
                ctx.inc(ins, s_st, 16)
            ctx.wge(sy, s_st, 32 * 16)
            ctx.bar_inc(sy)
    ctx.stage_done()


def L(fn, *args, **kw):
    return lambda ctx: fn(ctx, *args, **kw)


def ST(name, fn, reads, writes, aliases=None):
    return dict(name=name, fn=fn, reads=list(reads), writes=list(writes), aliases=aliases or {})


PREP_STAGES = [
    ST("tern1", L(emit_ternarize, "t1", "L0_ffn_w_fc", 4096, 1024, "weff1", C_INVD), ["L0_ffn_w_fc"], ["weff1"]),
    ST("tern2", L(emit_ternarize, "t2", "L0_ffn_w_proj", 1024, 4096, "weff2", C_INV4096), ["L0_ffn_w_proj"], ["weff2"]),
    ST("tpw1", L(emit_transpose_mat, "w1", "weff1", 4096, 1024, "w1T"), ["weff1"], ["w1T"]),
    ST("tpw2", L(emit_transpose_mat, "w2", "weff2", 1024, 4096, "w2T"), ["weff2"], ["w2T"]),
    ST("tpwinf", L(emit_transpose_mat, "wif", "L0_attn_fwd_in_proj_w", 4096, 1024, "winT_f"), ["L0_attn_fwd_in_proj_w"], ["winT_f"]),
    ST("tpwinb", L(emit_transpose_mat, "wib", "L0_attn_bwd_in_proj_w", 4096, 1024, "winT_b"), ["L0_attn_bwd_in_proj_w"], ["winT_b"]),
    ST("tpfus", L(emit_transpose_mat, "fu", "L0_attn_fusion_w", 1024, 2048, "fusT"), ["L0_attn_fusion_w"], ["fusT"]),
    ST("wtlf", L(emit_wtl, "f", "L0_attn_fwd_out_proj_w", "wtlT_f"),
       ["fusT", "L0_attn_fwd_out_proj_w"], ["wtlT_f"]),
    ST("wtlb", L(emit_wtl, "b", "L0_attn_bwd_out_proj_w", "wtlT_b"),
       ["fusT", "L0_attn_bwd_out_proj_w"], ["wtlT_b"]),
    ST("tpxpf", L(emit_transpose_mat, "xf", "L0_attn_fwd_x_proj_w", 96, 2048, "wxpT_f"), ["L0_attn_fwd_x_proj_w"], ["wxpT_f"]),
    ST("tpxpb", L(emit_transpose_mat, "xb", "L0_attn_bwd_x_proj_w", 96, 2048, "wxpT_b"), ["L0_attn_bwd_x_proj_w"], ["wxpT_b"]),
    ST("tpdtf", L(emit_transpose_mat, "df", "L0_attn_fwd_dt_proj_w", 2048, 64, "wdtT_f"), ["L0_attn_fwd_dt_proj_w"], ["wdtT_f"]),
    ST("tpdtb", L(emit_transpose_mat, "db", "L0_attn_bwd_dt_proj_w", 2048, 64, "wdtT_b"), ["L0_attn_bwd_dt_proj_w"], ["wdtT_b"]),
    ST("tpv", L(emit_transpose_mat, "vv", "L0_ffn_v", 16, 1024, "vTT"), ["L0_ffn_v"], ["vTT"]),
    ST("tpu", L(emit_transpose_mat, "uu", "L0_ffn_u", 4096, 16, "uTT"), ["L0_ffn_u"], ["uTT"]),
    ST("tpfb", L(emit_transpose_mat, "fb", "L0_ffn_fast_b", 16, 16, "fbT"), ["L0_ffn_fast_b"], ["fbT"]),
]

L0_STAGES = [
    ST("ln0", L(emit_ln, "l0", "x_in", dst_bf="xnf"), ["x_in"], ["xnf"]),
    ST("ipf", L(emit_inproj, "f"), ["winT_f", "xnf"], ["xs_f", "zg_f"]),
    ST("ipb", L(emit_inproj, "b"), ["winT_b", "xnf"], ["xs_b", "zg_b"]),
    ST("cvf", L(emit_conv, "f"), ["xs_f", "L0_attn_fwd_conv_w", "L0_attn_fwd_conv_b"], ["u_f"],
       {"conv_w": "L0_attn_fwd_conv_w", "conv_b": "L0_attn_fwd_conv_b"}),
    ST("cvb", L(emit_conv, "b"), ["xs_b", "L0_attn_bwd_conv_w", "L0_attn_bwd_conv_b"], ["u_b"],
       {"conv_w": "L0_attn_bwd_conv_w", "conv_b": "L0_attn_bwd_conv_b"}),
    ST("xdf", L(emit_xdbl, "f"), ["u_f", "wxpT_f", "wdtT_f", "L0_attn_fwd_dt_proj_b"],
       ["dl_f", "du_f", "bc_f"], {"dt_proj_b": "L0_attn_fwd_dt_proj_b"}),
    ST("xdb", L(emit_xdbl, "b"), ["u_b", "wxpT_b", "wdtT_b", "L0_attn_bwd_dt_proj_b"],
       ["dl_b", "du_b", "bc_b"], {"dt_proj_b": "L0_attn_bwd_dt_proj_b"}),
    ST("snf", L(emit_scan, "f"), ["dl_f", "du_f", "bc_f", "L0_attn_fwd_A_log"], ["y_f"],
       {"A_log": "L0_attn_fwd_A_log"}),
    ST("snb", L(emit_scan, "b"), ["dl_b", "du_b", "bc_b", "L0_attn_bwd_A_log"], ["y_b"],
       {"A_log": "L0_attn_bwd_A_log"}),
    ST("ffh", L(emit_ffnh), ["w1T", "xnf", "vTT", "uTT", "fbT", "L0_ffn_gate"], ["r2l0"]),
    ST("lom", L(emit_l0out_mamba),
       ["wtlT_f", "wtlT_b", "y_f", "y_b", "u_f", "u_b", "zg_f", "zg_b",
        "L0_attn_fwd_D", "L0_attn_bwd_D", "L0_attn_fusion_b"], ["x1a"]),
    ST("lof", L(emit_l0out_ffn), ["w2T", "r2l0", "x_in", "x1a"], ["x1"]),
]

INPUT_F32 = True


def _build_and_run(stages, state, n_cores=8):
    produced = set()
    reads, writes = [], []
    for st in stages:
        for r in st["reads"]:
            if r not in produced and r not in reads:
                reads.append(r)
        for w in st["writes"]:
            if w not in writes:
                writes.append(w)
            produced.add(w)
    ext_in = [r for r in reads if r not in writes]

    def build(es, kinds):
        nc = bass.Bass("TRN2", target_bir_lowering=False, detect_race_conditions=False)
        ctx = Ctx(nc, es, kinds=kinds)
        for name in ext_in:
            if name in SCRATCH:
                shape, dt = SCRATCH[name]
            else:
                shape, dt = list(state[0][name].shape), F32
            ctx.aps[name] = nc.dram_tensor(name, shape, dt, kind="ExternalInput").ap()
        for name in writes:
            shape, dt = SCRATCH[name]
            ctx.aps[name] = nc.dram_tensor(name, shape, dt, kind="ExternalOutput").ap()
        ctx.aps["__bar"] = nc.dram_tensor("__bar", [1, 128], F32, kind="Internal").ap()
        alloc_pools(ctx)
        emit_init(ctx)
        for st in stages:
            for a, real in st["aliases"].items():
                if not a.startswith("__"):
                    ctx.aps[a] = ctx.aps[real]
            st["fn"](ctx)
        return nc, ctx

    with ExitStack() as es0:
        _, ctx0 = build(es0, None)
        kinds = dict(ctx0.kinds_out)
    with ExitStack() as es:
        nc, ctx = build(es, kinds)
        in_maps = []
        for c in range(n_cores):
            m = {}
            for name in ext_in:
                a = state[c][name]
                if a.dtype != np.float32 and name not in SCRATCH:
                    a = a
                m[name] = a
            in_maps.append(m)
        res = run_bass_kernel_spmd(nc, in_maps, core_ids=list(range(n_cores)))
        for c in range(n_cores):
            for name in writes:
                state[c][name] = res.results[c][name]


GROUPS = None


def kernel(**inputs):
    state = []
    x = np.asarray(inputs["L0_attn_x"])
    for c in range(8):
        d = {"x_in": np.ascontiguousarray(x[c])}
        for k, v in inputs.items():
            if k == "L0_attn_x":
                continue
            a = np.asarray(v)
            if a.ndim == 0:
                a = a.reshape(1)
            d[k] = np.ascontiguousarray(a)
        d.update(_host_consts())
        state.append(d)
    for grp in GROUPS:
        _build_and_run(grp, state)
    return np.stack([state[c]["out"] for c in range(8)]).astype(np.float32)


def emit_cast_f32_bf16(ctx, tag, src_name, dst_name, n_elem_flat):
    nc = ctx.nc
    s_d = tag + "d"
    src = ctx.aps[src_name].rearrange("p a b -> p (a b)")
    dst = ctx.aps[dst_name].rearrange("p a b -> p (a b)")
    nch = 8
    w = n_elem_flat // nch
    with nc.Block("cast" + tag) as blk:
        @blk.gpsimd
        def _(g):
            ctx.wait_sb(g)
            for i in range(nch):
                ctx.inc(g.dma_start(out=dst[:, i * w:(i + 1) * w], in_=src[:, i * w:(i + 1) * w]), s_d, 16)
            ctx.wge(g, s_d, nch * 16)
            ctx.bar_inc(g)
    ctx.stage_done()


def emit_lqkv(ctx):
    nc = ctx.nc
    qkv = ctx.sb["pb0"][:].rearrange("p a b -> p (a b)")[:, 0:12288].rearrange("p (c j) -> p c j", c=NDC)
    xch = ctx.sb["pb1"]
    gsb = ctx.carve(0, [128, NDC], F32)
    bsb = ctx.carve(4096, [128, NDC], F32)
    bwT = ctx.carve(8192, [128, 12], F32)
    bbf = ctx.carve(12288, [128, NDC], BF16)
    evb = ctx.sb["bcs"][:].rearrange("p a b -> p (a b)")
    pp = ctx.sb["pp0"]
    ppt = ctx.sb["ppt"]
    pp1 = ctx.sb["pp1"]
    idt = ctx.sb["idt"]
    s_w, s_ld, s_bw, s_bwe, s_bb, s_bwc, s_g, s_mm, s_ev, s_tp, s_te, s_st, s_vs = [
        f"lq{x}" for x in ("w", "l", "b", "e", "B", "c", "g", "m", "v", "t", "u", "s", "z")]

    def evac_idx(k):
        return k + 1

    with nc.Block("lqkv") as blk:
        @blk.gpsimd
        def _(g):
            ctx.wait_sb(g)
            ctx.inc(g.dma_start(out=qkv[:, :, :], in_=ctx.aps["L1_attn_light_qkv_w"].rearrange("(c p) j -> p c j", p=128)), s_w, 16)
            ctx.inc(g.dma_start(out=gsb[:], in_=ctx.aps["L1_attn_ln_gamma"].rearrange("(c p) -> p c", p=128),
                                allow_slow_non_contiguous=True), s_w, 16)
            ctx.inc(g.dma_start(out=bsb[:], in_=ctx.aps["L1_attn_ln_beta"].rearrange("(c p) -> p c", p=128),
                                allow_slow_non_contiguous=True), s_w, 16)
            for ch in range(8):
                if ch >= 2:
                    ctx.wge(g, s_mm, (ch - 1) * 12)
                ctx.inc(g.dma_start(out=xch[:, ch % 2, :].rearrange("p (c t) -> p c t", c=NDC),
                                    in_=ctx.aps["xnlf"][:, :, ch * 512:(ch + 1) * 512]), s_ld, 16)
                for m in range(12):
                    k = ch * 12 + m
                    ctx.wge(g, s_ev, k + 1)
                    ins = g.dma_start(out=ctx.aps["qkvsc"][:, m, ch * 512:(ch + 1) * 512],
                                      in_=evb[:, (k % 12) * 512:(k % 12 + 1) * 512])
                    ctx.inc(ins, s_st, 16)
                for kt in range(ch * 16, ch * 16 + 16):
                    ctx.wge(g, s_te, kt + 1)
                    m8, b = divmod(kt % 16, 4)
                    ins = g.dma_start(out=ctx.aps["vtm"][ch * 512 + b * 128:ch * 512 + (b + 1) * 128,
                                                         m8 * 128:(m8 + 1) * 128],
                                      in_=evb[:, 6144 + (kt % 4) * 128:6144 + (kt % 4 + 1) * 128])
                    ctx.inc(ins, s_vs, 16)
            ctx.wge(g, s_st, ctx.val(s_st))
            ctx.wge(g, s_vs, ctx.val(s_vs))
            ctx.bar_inc(g)
        @blk.vector
        def _(v):
            ctx.wait_sb(v)
            ctx.wge(v, s_w, 48)
            ins = v.tensor_copy(bbf[:], bsb[:])
            ctx.inc(ins, s_g, 1)
            for nh in range(3):
                ctx.wge(v, s_bw, nh + 1)
                ins = v.tensor_copy(ctx.sb["pf1"][0:1, nh * 512:(nh + 1) * 512], pp1[0:1, 0, 0:512])
                ctx.inc(ins, s_bwe, 1)
            for kc in range(NDC):
                ins = v.tensor_scalar(out=qkv[:, kc, :], in0=qkv[:, kc, :], scalar1=gsb[:, kc:kc + 1],
                                      scalar2=None, op0=ALU.mult)
            ctx.inc(ins, s_g, 1)
            ctx.wge(v, s_bwc, 1)
            for ch in range(8):
                for m in range(12):
                    k = ch * 12 + m
                    ctx.wge(v, s_mm, k + 1)
                    if k >= 12:
                        kp = k - 12
                        ctx.wge(v, s_st, (kp + 1) * 16)
                        if kp % 12 >= 8:
                            ctx.wge(v, s_tp, (kp // 12) * 16 + (kp % 12 - 8 + 1) * 4)
                    ins = v.tensor_scalar(out=evb[:, (k % 12) * 512:(k % 12 + 1) * 512], in0=pp[:, k % 4, :],
                                          scalar1=bwT[:, m:m + 1], scalar2=None, op0=ALU.add)
                    ctx.inc(ins, s_ev, 1)
                for kt in range(ch * 16, ch * 16 + 16):
                    ctx.wge(v, s_tp, kt + 1)
                    if kt >= 4:
                        ctx.wge(v, s_vs, (kt - 3) * 16)
                    ins = v.tensor_copy(evb[:, 6144 + (kt % 4) * 128:6144 + (kt % 4 + 1) * 128],
                                        ppt[:, kt % 2, 0:128])
                    ctx.inc(ins, s_te, 1)
        @blk.tensor
        def _(t):
            ctx.wait_sb(t)
            ctx.wge(t, s_w, 48)
            ctx.wge(t, s_g, 1)
            for nh in range(3):
                if nh:
                    ctx.wge(t, s_bwe, nh)
                for kc in range(NDC):
                    ins = t.matmul(pp1[0:1, 0, 0:512], bbf[:, kc:kc + 1],
                                   qkv[:, kc, nh * 512:(nh + 1) * 512], start=(kc == 0), stop=(kc == NDC - 1))
                ctx.inc(ins, s_bw, 1)
            ctx.wge(t, s_g, 2)
            ctx.wge(t, s_bwc, 1)
            for ch in range(8):
                ctx.wge(t, s_ld, (ch + 1) * 16)
                for m in range(12):
                    k = ch * 12 + m
                    if k >= 4:
                        ctx.wge(t, s_ev, k - 3)
                    for kc in range(NDC):
                        ins = t.matmul(pp[:, k % 4, :], qkv[:, kc, m * 128:(m + 1) * 128],
                                       xch[:, ch % 2, :].rearrange("p (c t) -> p c t", c=NDC)[:, kc, :],
                                       start=(kc == 0), stop=(kc == NDC - 1))
                    ctx.inc(ins, s_mm, 1)
                for kt in range(ch * 16, ch * 16 + 16):
                    m8, b = divmod(kt % 16, 4)
                    ctx.wge(t, s_ev, ch * 12 + 8 + m8 + 1)
                    if kt >= 2:
                        ctx.wge(t, s_te, kt - 1)
                    ins = t.transpose(ppt[:, kt % 2, 0:128],
                                      evb[:, ((ch * 12 + 8 + m8) % 12) * 512 + b * 128:
                                          ((ch * 12 + 8 + m8) % 12) * 512 + (b + 1) * 128], idt[:])
                    ctx.inc(ins, s_tp, 1)
        @blk.scalar
        def _(s):
            pass
        @blk.sync
        def _(sy):
            ctx.wait_sb(sy)
            ctx.wge(sy, s_bwe, 3)
            ctx.inc(sy.dma_start(out=ctx.aps["bwrow"], in_=ctx.sb["pf1"][0:1, 0:1536]), s_bb, 16)
            ctx.wge(sy, s_bb, 16)
            ctx.inc(sy.dma_start(out=bwT[:], in_=ctx.aps["bwrow"].rearrange("one (m p) -> (one p) m", p=128),
                                 allow_slow_non_contiguous=True), s_bb, 16)
            ctx.wge(sy, s_bb, 32)
            ctx.eng_inc(sy, s_bwc, 1)
    ctx.stage_done()


def emit_lattn(ctx):
    nc = ctx.nc
    pb1f = ctx.sb["pb1"][:].rearrange("p a b -> p (a b)")
    bcsf = ctx.sb["bcs"][:].rearrange("p a b -> p (a b)")
    pf0f = ctx.sb["pf0"][:].rearrange("p a b -> p (a b)")
    masks = ctx.carve(0, [128, 3, 384], F32)
    o2 = ctx.carve(4608, [128, 2, 512], BF16)
    low = ctx.carve(6656, [128, 4, 1024], BF16)
    ev = ctx.sb["pf1"]
    st0 = ctx.sb["st0"]
    cst = ctx.sb["cst"]
    pp = ctx.sb["pp0"]
    pp1 = ctx.sb["pp1"]
    ppt = ctx.sb["ppt"]
    idt = ctx.sb["idt"]
    s_w, s_ld, s_mm, s_sf, s_ex, s_ab, s_tp, s_te, s_av, s_oe, s_om, s_fe, s_st = [
        f"la{x}" for x in "wlmsxbtuavofq"]
    SLOT = 3584

    def mtype(tt):
        return 0 if tt == 0 else (2 if tt == 31 else 1)

    with nc.Block("lattn") as blk:
        @blk.gpsimd
        def _(g):
            ctx.wait_sb(g)
            ctx.inc(g.dma_start(out=masks[:], in_=ctx.aps["lmask"].rearrange("a p b -> p a b")), s_w, 16)
            ctx.inc(g.dma_start(out=low[:], in_=ctx.aps["L1_attn_light_out_w"].rearrange("(c p) e -> p c e", p=128)), s_w, 16)
            for tt in range(32):
                t0 = tt * 128
                sl = (tt % 2) * SLOT
                if tt >= 2:
                    ctx.wge(g, s_om, tt - 1)
                ctx.inc(g.dma_start(out=pb1f[:, sl:sl + 512].rearrange("p (c t) -> p c t", c=4),
                                    in_=ctx.aps["qkvsc"][:, 0:4, t0:t0 + 128]), s_ld, 16)
                klo, khi = max(0, t0 - 128), min(T, t0 + 256)
                off = klo - (t0 - 128)
                if tt == 0:
                    g.memset(pb1f[:, sl + 512:sl + 2048], 0.0)
                if tt == 31:
                    g.memset(pb1f[:, sl + 512:sl + 2048], 0.0)
                ins = g.dma_start(out=pb1f[:, sl + 512:sl + 2048].rearrange("p (c t) -> p c t", c=4)[:, :, off:off + (khi - klo)],
                                  in_=ctx.aps["qkvsc"][:, 4:8, klo:khi])
                ctx.inc(ins, s_ld, 16)
                if tt == 0:
                    g.memset(pb1f[:, sl + 2048:sl + 3584], 0.0)
                if tt == 31:
                    g.memset(pb1f[:, sl + 2048:sl + 3584], 0.0)
                nct_v = (khi - klo) // 128
                coff = off // 128
                ins = g.dma_start(out=pb1f[:, sl + 2048:sl + 3584].rearrange("p (c t) -> p c t", c=3)[:, coff:coff + nct_v, :],
                                  in_=ctx.aps["vtm"][klo:khi, :].rearrange("(c p) v -> p c v", p=128))
                ctx.inc(ins, s_ld, 16)
        @blk.tensor
        def _(t):
            ctx.wait_sb(t)
            ctx.wge(t, s_w, 32)
            ktp = 0
            def outproj(tt2):
                ctx.wge(t, s_oe, tt2 * 8 + 8)
                if tt2 >= 1:
                    ctx.wge(t, s_fe, tt2)
                for nh in range(2):
                    for c in range(4):
                        ins = t.matmul(pp[:, 2 + nh, :], o2[:, tt2 % 2, c * 128:(c + 1) * 128],
                                       low[:, c, nh * 512:(nh + 1) * 512], start=(c == 0), stop=(c == 3))
                ctx.inc(ins, s_om, 1)
            for tt in range(32):
                sl = (tt % 2) * SLOT
                ctx.wge(t, s_ld, (tt + 1) * 48)
                for h in range(8):
                    k = tt * 8 + h
                    if k >= 2:
                        ctx.wge(t, s_sf, k - 1)
                    ins = t.matmul(pp[:, h % 2, 0:384],
                                   pb1f[:, sl:sl + 512].rearrange("p (c t) -> p c t", c=4)[64 * (h % 2):64 * (h % 2) + 64, h // 2, :],
                                   pb1f[:, sl + 512:sl + 2048].rearrange("p (c t) -> p c t", c=4)[64 * (h % 2):64 * (h % 2) + 64, h // 2, :],
                                   start=True, stop=True)
                    ctx.inc(ins, s_mm, 1)
                    for c in range(3):
                        ctx.wge(t, s_ab, k + 1)
                        if ktp >= 2:
                            ctx.wge(t, s_te, ktp - 1)
                        ins = t.transpose(ppt[:, ktp % 2, 0:128],
                                          bcsf[:, (k % 2) * 384:(k % 2) * 384 + 384][:, c * 128:(c + 1) * 128], idt[:])
                        ctx.inc(ins, s_tp, 1)
                        ktp += 1
                    ctx.wge(t, s_te, ktp)
                    if k >= 2:
                        ctx.wge(t, s_oe, k - 1)
                    for c in range(3):
                        ins = t.matmul(pp1[0:64, k % 2, 0:128],
                                       pb1f[:, sl + 2048:sl + 3584].rearrange("p (c t) -> p c t", c=3)[:, c, h * 64:(h + 1) * 64],
                                       bcsf[:, 768 + (k % 2) * 384 + c * 128:768 + (k % 2) * 384 + (c + 1) * 128],
                                       start=(c == 0), stop=(c == 2))
                    ctx.inc(ins, s_av, 1)
                    if h == 1 and tt >= 1:
                        outproj(tt - 1)
            outproj(31)
        @blk.vector
        def _(v):
            ctx.wait_sb(v)
            def fevac(ttv):
                ctx.wge(v, s_om, ttv + 1)
                if ttv >= 2:
                    ctx.wge(v, s_st, (ttv - 1) * 16)
                ins = v.tensor_copy(ev[:, (ttv % 2) * 1024:(ttv % 2) * 1024 + 1024],
                                    pp[:, 2:4, :].rearrange("p a b -> p (a b)"))
                ctx.inc(ins, s_fe, 1)
            for tt in range(32):
                mt = mtype(tt)
                for h in range(8):
                    k = tt * 8 + h
                    ctx.wge(v, s_mm, k + 1)
                    ins = v.scalar_tensor_tensor(out=pf0f[:, (k % 2) * 384:(k % 2) * 384 + 384],
                                                 in0=pp[:, h % 2, 0:384], scalar=cst[:, C_QSC:C_QSC + 1],
                                                 in1=masks[:, mt, :], op0=ALU.mult, op1=ALU.add)
                    v.reduce_max(st0[:, h % 4:h % 4 + 1], pf0f[:, (k % 2) * 384:(k % 2) * 384 + 384],
                                 axis=AX.X, negate=True)
                    ctx.inc(ins, s_sf, 1)
                    ctx.wge(v, s_ex, k + 1)
                    v.drain()
                    v.reciprocal(st0[:, 8 + h % 4:9 + h % 4], st0[:, 16 + h % 4:17 + h % 4])
                    v.drain()
                    ins = v.tensor_scalar(out=bcsf[:, (k % 2) * 384:(k % 2) * 384 + 384],
                                          in0=pf0f[:, 768 + (k % 2) * 384:768 + (k % 2) * 384 + 384],
                                          scalar1=st0[:, 8 + h % 4:9 + h % 4], scalar2=None, op0=ALU.mult)
                    ctx.inc(ins, s_ab, 1)
                    for c in range(3):
                        kt3 = k * 3 + c
                        ctx.wge(v, s_tp, kt3 + 1)
                        ins = v.tensor_copy(bcsf[:, 768 + (k % 2) * 384 + c * 128:768 + (k % 2) * 384 + (c + 1) * 128],
                                            ppt[:, kt3 % 2, 0:128])
                        ctx.inc(ins, s_te, 1)
                    if h == 2 and tt >= 1:
                        fevac(tt - 1)
            fevac(31)
        @blk.scalar
        def _(s):
            ctx.wait_sb(s)
            def oevac(j):
                tt2, h2 = divmod(j, 8)
                ctx.wge(s, s_av, j + 1)
                if h2 == 0 and tt2 >= 2:
                    ctx.wge(s, s_om, tt2 - 1)
                ins = s.activation(out=o2[64 * (h2 % 2):64 * (h2 % 2) + 64, tt2 % 2,
                                          (h2 // 2) * 128:(h2 // 2) * 128 + 128],
                                   in_=pp1[0:64, j % 2, 0:128], func=AF.Copy)
                ctx.inc(ins, s_oe, 1)
            for tt in range(32):
                for h in range(8):
                    k = tt * 8 + h
                    ctx.wge(s, s_sf, k + 1)
                    ins = s.activation(out=pf0f[:, 768 + (k % 2) * 384:768 + (k % 2) * 384 + 384],
                                       in_=pf0f[:, (k % 2) * 384:(k % 2) * 384 + 384],
                                       func=AF.Exp, bias=st0[:, h % 4:h % 4 + 1], scale=1.0,
                                       accum_out=st0[:, 16 + h % 4:17 + h % 4])
                    ctx.inc(ins, s_ex, 1)
                    if k >= 2:
                        oevac(k - 2)
            oevac(254)
            oevac(255)
        @blk.sync
        def _(sy):
            ctx.wait_sb(sy)
            for tt in range(32):
                ctx.wge(sy, s_fe, tt + 1)
                ins = sy.dma_start(out=ctx.aps["lighto"][tt * 128:(tt + 1) * 128, :],
                                   in_=ev[:, (tt % 2) * 1024:(tt % 2) * 1024 + 1024])
                ctx.inc(ins, s_st, 16)
            ctx.wge(sy, s_st, 32 * 16)
            ctx.bar_inc(sy)
    ctx.stage_done()


DEBUG_ROUTE = False
DISABLE_RECYCLE = False
DEBUG_ROUTE_LOOP = False


def emit_route(ctx):
    nc = ctx.nc
    tok2 = ctx.carve(0, [128, NDC, 2], F32)
    xch = ctx.sb["pf0"]
    pf1 = ctx.sb["pf1"]
    ysc = ctx.sb["ysc"]
    sc2 = ctx.sb["bcs"][:].rearrange("p a b -> p (a b)").bitcast(F32)[0:2, 0:4096]
    bb = ctx.carve(17408, [2, 4096], F32) if False else None
    st0 = ctx.sb["st0"]
    cst = ctx.sb["cst"]
    cur1, cur2, curB = ctx.sb["cur1"], ctx.sb["cur2"], ctx.sb["cur3"]
    pp1 = ctx.sb["pp1"]
    pp = ctx.sb["pp0"]
    st1 = ctx.sb["st1"]
    idtf = ctx.sb["idtf"]
    s_w, s_ld, s_mm, s_ev, s_vi, s_si, s_sel, s_rr, s_st = [f"rt{x}" for x in "wlmeabczq"]
    t0v = pf1[0:2, 0:4096]
    bv = ysc[0:2, 0, :]
    sco = ysc[0:2, 1, :]
    junk = ctx.sb["pb1"][0:2, 0, :]
    with nc.Block("route") as blk:
        @blk.gpsimd
        def _(g):
            ctx.wait_sb(g)
            ctx.inc(g.dma_start(out=tok2[:, :, 0:1],
                                in_=ctx.aps["L1_attn_q_route_tok"].rearrange("(c p) -> p c", p=128).unsqueeze(2),
                                allow_slow_non_contiguous=True), s_w, 16)
            ctx.inc(g.dma_start(out=tok2[:, :, 1:2],
                                in_=ctx.aps["L1_attn_kv_route_tok"].rearrange("(c p) -> p c", p=128).unsqueeze(2),
                                allow_slow_non_contiguous=True), s_w, 16)
            for ch in range(8):
                if ch >= 2:
                    ctx.wge(g, s_mm, ch - 1)
                ctx.inc(g.dma_start(out=xch[:, ch % 2, :].rearrange("p (c t) -> p c t", c=NDC),
                                    in_=ctx.aps["xn1f32"][:, :, ch * 512:(ch + 1) * 512]), s_ld, 16)
            ctx.wge(g, s_ld, 8 * 16)
            ctx.wge(g, s_mm, 8)
            g.iota(ctx.sb["pf0"][:, 1, 0:1024], pattern=[[1, 1024]], base=0, channel_multiplier=0,
                   allow_small_or_imprecise_dtypes=True)
            ins = g.iota(ctx.sb["st1"][:, 0:32], pattern=[[128, 32]], base=0, channel_multiplier=1,
                         allow_small_or_imprecise_dtypes=True)
            ctx.inc(ins, "rtio", 1)
            if DEBUG_ROUTE:
                ctx.wge(g, s_sel, 1)
                ctx.inc(g.dma_start(out=ctx.aps["scorerow"], in_=sco[:]), "rtdbq", 16)
                ctx.inc(g.dma_start(out=ctx.aps["rankrow"][0:2, 0:32], in_=st0[0:2, 24:56].bitcast(I32)), "rtdbq", 16)
                ctx.inc(g.dma_start(out=ctx.aps["rankrow"][0:2, 64:4096], in_=sc2[0:2, 64:4096].bitcast(I32)), "rtdbq", 16)
                ctx.wge(g, "rtdbq", 48)
            for r in range(2):
                ctx.wge(g, "rtsr", r + 1)
                dst = ctx.aps["selq"] if r == 0 else ctx.aps["selkv"]
                ins = g.dma_start(out=dst.rearrange("n one -> (one n)").unsqueeze(0),
                                  in_=ctx.sb["pf0"][0:1, 0, 2048 + r * 1024:2048 + (r + 1) * 1024].bitcast(I32))
                ctx.inc(ins, s_st, 16)
            ctx.wge(g, s_st, 2 * 16)
            ctx.bar_inc(g)
        @blk.tensor
        def _(t):
            ctx.wait_sb(t)
            ctx.wge(t, s_w, 32)
            for ch in range(8):
                ctx.wge(t, s_ld, (ch + 1) * 16)
                if ch >= 1:
                    ctx.wge(t, s_ev, ch)
                for kc in range(NDC):
                    ins = t.matmul(pp1[0:2, 0, :], tok2[:, kc, :],
                                   xch[:, ch % 2, :].rearrange("p (c t) -> p c t", c=NDC)[:, kc, :],
                                   start=(kc == 0), stop=(kc == NDC - 1))
                ctx.inc(ins, s_mm, 1)
            ctx.wge(t, s_sel, 1)
            for c in range(32):
                if c >= 2:
                    ctx.wge(t, "rtte", c - 1)
                ins = t.transpose(pp[0:128, c % 2, 0:2], sc2[:, c * 128:(c + 1) * 128], idtf[0:2, 0:2])
                ctx.inc(ins, "rttp", 1)
            ctx.wge(t, "rtio", 1)
            for r in range(2):
                if r == 1:
                    ctx.wge(t, "rtev", 1)
                for ch in range(32):
                    ctx.wge(t, "rtcm", r * 32 + ch + 1)
                    for nh in range(2):
                        ins = t.matmul(pp1[0:1, nh, :], st1[:, ch:ch + 1],
                                       pf1[:, (ch % 2) * 1024 + nh * 512:(ch % 2) * 1024 + (nh + 1) * 512],
                                       start=(ch == 0), stop=(ch == 31))
                    ctx.inc(ins, "rtmm", 1)
        @blk.vector
        def _(v):
            ctx.wait_sb(v)
            for ch in range(8):
                ctx.wge(v, s_mm, ch + 1)
                ctx.inc(v.tensor_copy(sc2[:, ch * 512:(ch + 1) * 512], pp1[0:2, 0, :]), s_ev, 1)
            v.tensor_scalar(out=bv, in0=sc2[:], scalar1=cst[0:2, C_NEG1:C_NEG1 + 1], scalar2=None, op0=ALU.mult)
            for it in range(20):
                v.tensor_tensor(out=t0v, in0=sc2[:], in1=bv, op=ALU.add)
                v.reduce_max(st0[0:2, 0:1], t0v, axis=AX.X, negate=True)
                v.drain()
                ins = v.tensor_scalar(out=st0[0:2, 1:2], in0=st0[0:2, 0:1], scalar1=cur1[0:2, it:it + 1],
                                      scalar2=None, op0=ALU.mult)
                ctx.inc(ins, s_vi, 1)
                ctx.wge(v, s_si, it + 1)
                v.scalar_tensor_tensor(out=st0[0:2, 3:4], in0=st0[0:2, 0:1], scalar=cur1[0:2, it:it + 1],
                                       in1=st0[0:2, 2:3], op0=ALU.mult, op1=ALU.subtract)
                v.drain()
                v.tensor_scalar(out=st0[0:2, 4:5], in0=st0[0:2, 3:4], scalar1=cst[0:2, C_LOGK:C_LOGK + 1],
                                scalar2=curB[0:2, it:it + 1], op0=ALU.add, op1=ALU.mult)
                v.drain()
                if DEBUG_ROUTE or DEBUG_ROUTE_LOOP:
                    v.tensor_copy(st0[0:2, 24 + it:25 + it], st0[0:2, 4:5])
                    v.drain()
                v.tensor_scalar(out=t0v, in0=sc2[:], scalar1=st0[0:2, 4:5], scalar2=None, op0=ALU.add)
                ins = v.tensor_scalar(out=bv, in0=t0v, scalar1=cst[0:2, C_ZERO:C_ZERO + 1],
                                      scalar2=cst[0:2, C_NEG1:C_NEG1 + 1], op0=ALU.max, op1=ALU.mult)
                ctx.inc(ins, s_vi, 1)
            v.tensor_scalar(out=t0v, in0=t0v, scalar1=cst[0:2, C_ZERO:C_ZERO + 1], scalar2=None, op0=ALU.is_ge)
            if DEBUG_ROUTE:
                v.tensor_copy(sco, t0v)
            v.memset(bv, 1.0)
            v.drain()
            ctx.inc(v.tensor_copy(st0[0:2, 60:61], cst[0:2, C_ZERO:C_ZERO + 1]), "rtsc", 1)
            ctx.wge(v, "rtsc", 1)
            v.tensor_tensor_scan(out=sc2[:], data0=bv, data1=t0v, initial=0.0, op0=ALU.mult, op1=ALU.add)
            v.tensor_tensor(out=sc2[:], in0=sc2[:], in1=t0v, op=ALU.subtract)
            v.tensor_scalar(out=bv, in0=sc2[:], scalar1=cst[0:2, C_K1024:C_K1024 + 1], scalar2=None, op0=ALU.is_lt)
            v.tensor_tensor(out=bv, in0=bv, in1=t0v, op=ALU.mult)
            v.tensor_scalar(out=sc2[:], in0=sc2[:], scalar1=cst[0:2, C_BIG:C_BIG + 1], scalar2=None, op0=ALU.add)
            v.scalar_tensor_tensor(out=sc2[:], in0=bv, scalar=cst[0:2, C_BIGN:C_BIGN + 1],
                                   in1=sc2[:], op0=ALU.mult, op1=ALU.add)
            ctx.inc(v.engine_nop(), s_sel, 1)
            rankT = ctx.carve(256, [128, 64], F32)
            for c in range(32):
                ctx.wge(v, "rttp", c + 1)
                ins = v.tensor_copy(rankT[:, 2 * c:2 * c + 2], pp[0:128, c % 2, 0:2])
                ctx.inc(ins, "rtte", 1)
            ctx.wge(v, "rtio", 1)
            for r in range(2):
                for ch in range(32):
                    j = r * 32 + ch
                    if j >= 2:
                        ctx.wge(v, "rtmm", j - 1)
                    ins = v.tensor_tensor(out=pf1[:, (ch % 2) * 1024:(ch % 2) * 1024 + 1024],
                                          in0=rankT[:, 2 * ch + r:2 * ch + r + 1].broadcast_to([128, 1024]),
                                          in1=ctx.sb["pf0"][:, 1, 0:1024], op=ALU.is_equal)
                    ctx.inc(ins, "rtcm", 1)
                ctx.wge(v, "rtmm", (r + 1) * 32)
                v.tensor_copy(ctx.sb["pf0"][0:1, 0, r * 1024:(r + 1) * 1024],
                              pp1[0:1, 0:2, :].rearrange("p a b -> p (a b)"))
                v.drain()
                ins = v.tensor_copy(ctx.sb["pf0"][0:1, 0, 2048 + r * 1024:2048 + (r + 1) * 1024].bitcast(I32),
                                    ctx.sb["pf0"][0:1, 0, r * 1024:(r + 1) * 1024])
                ctx.inc(ins, "rtev", 1)
                ctx.inc(v.engine_nop(), "rtsr", 1)
        @blk.scalar
        def _(s):
            ctx.wait_sb(s)
            for it in range(20):
                ctx.wge(s, s_vi, 2 * it + 1)
                s.activation(out=sco, in_=t0v, func=AF.Exp,
                             scale=cur1[0:2, it:it + 1], bias=st0[0:2, 1:2], accum_out=st0[0:2, 2:3])
                s.drain()
                ins = s.activation(out=st0[0:2, 2:3], in_=st0[0:2, 2:3], func=AF.Ln)
                ctx.inc(ins, s_si, 1)
    ctx.stage_done()


def emit_heavy1(ctx):
    nc = ctx.nc
    pf0 = ctx.sb["pf0"]
    tabs = pf0[0:64, 0, 0:4096].rearrange("p (k n) -> p k n", k=4)
    fsb = pf0[0:32, 1, 0:1024]
    arg = pf0[0:32, 1, 1024:2048]
    red = pf0[0:32, 1, 2048:3072]
    pf1 = ctx.sb["pf1"]
    ki = pf1[0:32, 0:1024].bitcast(I32)
    st1 = ctx.sb["st1"]
    cst = ctx.sb["cst"]
    pp1 = ctx.sb["pp1"]
    s_w, s_mm, s_h = "h1w", "h1m", "h1h"
    with nc.Block("heavy1") as blk:
        @blk.gpsimd
        def _(g):
            ctx.wait_sb(g)
            yi = ctx.sb["ysc"][:, 0, :].bitcast(I32)
            ctx.inc(g.dma_start(out=yi[0:1, 0:1024],
                                in_=ctx.aps["selq"].rearrange("(one n) one2 -> one (n one2)", one=1)), s_w, 16)
            ctx.inc(g.dma_start(out=yi[0:1, 1024:2048],
                                in_=ctx.aps["selkv"].rearrange("(one n) one2 -> one (n one2)", one=1)), s_w, 16)
            ctx.inc(g.dma_start(out=st1[0:1, 0:32], in_=ctx.aps["invf"]), s_w, 16)
        @blk.vector
        def _(v):
            ctx.wait_sb(v)
            ctx.wge(v, s_w, 48)
            yi = ctx.sb["ysc"][:, 0, :].bitcast(I32)
            v.tensor_copy(pf1[0:1, 1024:2048], yi[0:1, 0:1024])
            ins = v.tensor_copy(pf1[0:1, 2048:3072], yi[0:1, 1024:2048])
            ctx.inc(ins, s_h, 1)
            for side in range(2):
                for trig in range(2):
                    ctx.wge(v, s_mm, side * 2 + 2)
                    if trig == 0:
                        v.tensor_copy(fsb[:], pp1[0:32, 0:2, :].rearrange("p a b -> p (a b)"))
                        v.drain()
                    if trig == 0:
                        v.tensor_scalar(out=arg[:], in0=fsb[:], scalar1=cst[0:32, C_HPI:C_HPI + 1],
                                        scalar2=None, op0=ALU.add)
                    else:
                        v.tensor_copy(arg[:], fsb[:])
                    v.tensor_scalar(out=red[:], in0=arg[:], scalar1=cst[0:32, C_INV2PI:C_INV2PI + 1],
                                    scalar2=None, op0=ALU.mult)
                    v.tensor_copy(ki[:], red[:])
                    v.tensor_copy(red[:], ki[:])
                    v.scalar_tensor_tensor(out=red[:], in0=red[:], scalar=cst[0:32, C_N2PI:C_N2PI + 1],
                                           in1=arg[:], op0=ALU.mult, op1=ALU.add)
                    v.tensor_single_scalar(out=ki[:].bitcast(F32) if False else pf1[0:32, 0:1024], in_=red[:], scalar=PI, op=ALU.is_gt)
                    v.scalar_tensor_tensor(out=red[:], in0=pf1[0:32, 0:1024], scalar=cst[0:32, C_N2PI:C_N2PI + 1],
                                           in1=red[:], op0=ALU.mult, op1=ALU.add)
                    ctx.inc(v.engine_nop(), s_h, 1)
                    ctx.wge(v, s_h + "s", side * 2 + trig + 1)
                    ti = side * 2 + trig
                    ins = v.tensor_copy(tabs[32:64, ti, :], tabs[0:32, ti, :])
                    ctx.inc(ins, s_h, 1)
        @blk.tensor
        def _(t):
            ctx.wait_sb(t)
            ctx.wge(t, s_h, 1)
            for side in range(2):
                if side == 1:
                    ctx.wge(t, s_h, 2)
                rhs = pf1[0:1, 1024 + side * 1024:2048 + side * 1024]
                for nh in range(2):
                    ins = t.matmul(pp1[0:32, nh, :], st1[0:1, 0:32], rhs[0:1, nh * 512:(nh + 1) * 512],
                                   start=True, stop=True)
                ctx.inc(ins, s_mm, 2)
        @blk.scalar
        def _(s):
            ctx.wait_sb(s)
            for side in range(2):
                for trig in range(2):
                    k = side * 2 + trig
                    ctx.wge(s, s_h, 1 + k + 1 + k)
                    ins = s.activation(out=tabs[0:32, k, :], in_=red[:], func=AF.Sin)
                    ctx.inc(ins, s_h + "s", 1)
    ctx.stage_done()


def emit_heavy2(ctx, side):
    nc = ctx.nc
    pf1 = ctx.sb["pf1"]
    pb1 = ctx.sb["pb1"][:].rearrange("p a b -> p (a b)")
    bcsf = ctx.sb["bcs"][:].rearrange("p a b -> p (a b)")
    st0 = ctx.sb["st0"]
    cst = ctx.sb["cst"]
    ppt = ctx.sb["ppt"]
    idt = ctx.sb["idt"]
    s_w, s_ld, s_n, s_tp, s_te = [f"h2{side}{x}" for x in "wlntu"]
    selname = "selq" if side == 0 else "selkv"
    with nc.Block(f"heavy2{side}") as blk:
        @blk.gpsimd
        def _(g):
            ctx.wait_sb(g)
            ctx.inc(g.dma_start(out=ctx.sb["sti2"][:, side * 8:side * 8 + 8],
                                in_=ctx.aps[selname].rearrange("(gg p) one -> p (gg one)", p=128),
                                allow_slow_non_contiguous=True), s_w, 16)
            ctx.wge(g, s_w, 16)
            for gg in range(8):
                if gg >= 1:
                    ctx.wge(g, s_n, gg * 2)
                ins = g.indirect_dma_start(
                    out=pf1[:, 0:1024], out_offset=None, in_=ctx.aps["xn1tm"],
                    in_offset=IndirectOffsetOnAxis(ctx.sb["sti2"][:, side * 8 + gg:side * 8 + gg + 1], 0))
                ctx.inc(ins, s_ld, 16)
        @blk.scalar
        def _(s):
            ctx.wait_sb(s)
            for gg in range(8):
                ctx.wge(s, s_ld, (gg + 1) * 16)
                s.activation(out=ctx.sb["ysc"][:, 1, 0:1024], in_=pf1[:, 0:1024],
                             func=AF.Square, accum_out=st0[:, 0:1])
                s.drain()
                s.activation(out=st0[:, 1:2], in_=st0[:, 0:1], func=AF.Ln)
                s.drain()
                ins = s.activation(out=st0[:, 2:3], in_=st0[:, 1:2], func=AF.Exp, scale=cst[:, C_HALF:C_HALF + 1])
                ctx.inc(ins, s_n, 1)
        @blk.vector
        def _(v):
            ctx.wait_sb(v)
            for gg in range(8):
                ctx.wge(v, s_n, gg * 2 + 1)
                v.tensor_scalar(out=st0[:, 3:4], in0=st0[:, 2:3], scalar1=cst[:, C_E12:C_E12 + 1],
                                scalar2=None, op0=ALU.max)
                v.drain()
                v.reciprocal(st0[:, 4:5], st0[:, 3:4])
                v.drain()
                v.tensor_scalar(out=st0[:, 5:6], in0=st0[:, 4:5], scalar1=cst[:, C_SQD:C_SQD + 1],
                                scalar2=None, op0=ALU.mult)
                v.drain()
                ins = v.tensor_scalar(out=pb1[:, gg * 1024:(gg + 1) * 1024], in0=pf1[:, 0:1024],
                                      scalar1=st0[:, 5:6], scalar2=None, op0=ALU.mult)
                ctx.inc(ins, s_n, 1)
                for c in range(NDC):
                    kt = gg * 8 + c
                    ctx.wge(v, s_tp, kt + 1)
                    ins = v.tensor_copy(bcsf[:, c * 1024 + gg * 128:c * 1024 + (gg + 1) * 128],
                                        ppt[:, kt % 2, 0:128])
                    ctx.inc(ins, s_te, 1)
        @blk.tensor
        def _(t):
            ctx.wait_sb(t)
            k = 0
            for gg in range(8):
                ctx.wge(t, s_n, gg * 2 + 2)
                for c in range(NDC):
                    if k >= 2:
                        ctx.wge(t, s_te, k - 1)
                    ins = t.transpose(ppt[:, k % 2, 0:128],
                                      pb1[:, gg * 1024 + c * 128:gg * 1024 + (c + 1) * 128], idt[:])
                    ctx.inc(ins, s_tp, 1)
                    k += 1
        @blk.sync
        def _(sy):
            ctx.wait_sb(sy)
            ctx.wge(sy, s_te, 64)
            ctx.bar_inc(sy)
    ctx.stage_done()


def emit_heavy3(ctx):
    nc = ctx.nc
    pb0f = ctx.sb["pb0"][:].rearrange("p a b -> p (a b)")
    q_sb = pb0f[0:64, 0:8192].rearrange("p (h n) -> p h n", h=8)
    bcsf = ctx.sb["bcs"][:].rearrange("p a b -> p (a b)")
    xqf = bcsf.rearrange("p (c n) -> p c n", c=8)
    pb1f32 = ctx.sb["pb1"][:].rearrange("p a b -> p (a b)").bitcast(F32)
    tabs = ctx.sb["pf0"][0:64, 0, 0:4096].rearrange("p (k n) -> p k n", k=4)
    st0 = ctx.sb["st0"]
    cst = ctx.sb["cst"]
    qwg = ctx.carve(0, [128, NDC, 512], BF16)
    pp = ctx.sb["pp0"]
    s_w, s_g, s_mm, s_qe, s_rp = [f"h3{x}" for x in "wgmer"]
    with nc.Block("heavy3") as blk:
        @blk.gpsimd
        def _(g):
            ctx.wait_sb(g)
            ctx.inc(g.dma_start(out=qwg[:], in_=ctx.aps["L1_attn_heavy_q_w"].rearrange("(c p) j -> p c j", p=128)), s_w, 16)
            ctx.inc(g.dma_start(out=st0[:, 32:40], in_=ctx.aps["L1_attn_heavy_gamma"].rearrange("(c p) -> p c", p=128),
                                allow_slow_non_contiguous=True), s_w, 16)
        @blk.vector
        def _(v):
            ctx.wait_sb(v)
            ctx.wge(v, s_w, 32)
            for kc in range(NDC):
                ins = v.tensor_scalar(out=qwg[:, kc, :], in0=qwg[:, kc, :], scalar1=st0[:, 32 + kc:33 + kc],
                                      scalar2=None, op0=ALU.mult)
            ctx.inc(ins, s_g, 1)
            for h in range(8):
                ctx.wge(v, s_qe, (h // 2) * 4 + 4)
                q = q_sb[:, h, :]
                t1 = pb1f32[0:64, 0:1024]
                rq = pb1f32[0:64, 1024:2048]
                v.tensor_tensor(out=t1, in0=q, in1=tabs[:, 0, :], op=ALU.mult)
                v.tensor_scalar(out=rq[0:32, :], in0=q_sb[32:64, h, :], scalar1=cst[0:32, C_NEG1:C_NEG1 + 1],
                                scalar2=None, op0=ALU.mult)
                v.tensor_copy(rq[32:64, :], q_sb[0:32, h, :])
                v.tensor_tensor(out=rq, in0=rq, in1=tabs[:, 1, :], op=ALU.mult)
                ins = v.tensor_tensor(out=q, in0=t1, in1=rq, op=ALU.add)
                ctx.inc(ins, s_rp, 1)
        @blk.tensor
        def _(t):
            ctx.wait_sb(t)
            ctx.wge(t, s_g, 1)
            for m in range(4):
                for nh in range(2):
                    k = m * 2 + nh
                    if k >= 2:
                        ctx.wge(t, s_qe, (k - 2) * 2 + 2)
                    for kc in range(NDC):
                        ins = t.matmul(pp[:, k % 2, :], qwg[:, kc, m * 128:(m + 1) * 128],
                                       xqf[:, kc, nh * 512:(nh + 1) * 512], start=(kc == 0), stop=(kc == NDC - 1))
                    ctx.inc(ins, s_mm, 1)
        @blk.scalar
        def _(s):
            ctx.wait_sb(s)
            for m in range(4):
                for nh in range(2):
                    k = m * 2 + nh
                    ctx.wge(s, s_mm, k + 1)
                    ins = s.activation(out=q_sb[0:64, 2 * m, nh * 512:(nh + 1) * 512],
                                       in_=pp[0:64, k % 2, :], func=AF.Copy)
                    ctx.inc(ins, s_qe, 1)
                    ins = s.activation(out=q_sb[0:64, 2 * m + 1, nh * 512:(nh + 1) * 512],
                                       in_=pp[64:128, k % 2, :], func=AF.Copy)
                    ctx.inc(ins, s_qe, 1)
        @blk.sync
        def _(sy):
            ctx.wait_sb(sy)
            ctx.wge(sy, s_rp, 8)
            ctx.bar_inc(sy)
    ctx.stage_done()


def emit_heavy4(ctx):
    nc = ctx.nc
    pb0f = ctx.sb["pb0"][:].rearrange("p a b -> p (a b)")
    kfull = pb0f[0:64, 8192:16392].rearrange("p (h n) -> p h n", h=8)
    vT = pb0f[:, 24576:28672]
    yscb = ctx.sb["ysc"][:].rearrange("p a b -> p (a b)").bitcast(BF16)
    kvT = yscb[:, 0:8192].rearrange("p (c h) -> p c h", c=8).rearrange("p c (h j) -> p c h j", h=8) if False else None
    kvTf = yscb[:, 0:8192]
    bcsf = ctx.sb["bcs"][:].rearrange("p a b -> p (a b)")
    xcf = bcsf.rearrange("p (c n) -> p c n", c=8)
    pb1 = ctx.sb["pb1"][:].rearrange("p a b -> p (a b)")
    tabs = ctx.sb["pf0"][0:64, 0, 0:4096].rearrange("p (k n) -> p k n", k=4)
    st0 = ctx.sb["st0"]
    cst = ctx.sb["cst"]
    pp = ctx.sb["pp0"]
    ppt = ctx.sb["ppt"]
    idt = ctx.sb["idt"]
    s_w, s_ld, s_tp, s_te, s_g, s_mm, s_ke, s_ve, s_vt, s_vte, s_rp, s_nc = [f"h4{x}" for x in
        ("w", "l", "t", "u", "g", "m", "k", "v", "T", "U", "r", "n")]
    with nc.Block("heavy4") as blk:
        @blk.gpsimd
        def _(g):
            ctx.wait_sb(g)
            ctx.inc(g.dma_start(out=ctx.sb["nkvrow"][0:1, :],
                                in_=ctx.aps["L1_attn_heavy_null_kv"].rearrange("a h d -> a (h d)").rearrange("(one a) hd -> one (a hd)", one=1)), s_w, 16)
            for h in range(8):
                if h >= 1:
                    ctx.wge(g, s_tp, h * 8)
                ctx.inc(g.dma_start(out=pb1[:, 0:1024], in_=ctx.aps["L1_attn_heavy_kv_w"][h]), s_ld, 16)
        @blk.tensor
        def _(t):
            ctx.wait_sb(t)
            kt = 0
            for h in range(8):
                ctx.wge(t, s_ld, (h + 1) * 16)
                for c in range(NDC):
                    if kt >= 2:
                        ctx.wge(t, s_te, kt - 1)
                    ins = t.transpose(ppt[:, kt % 2, 0:128], pb1[:, c * 128:(c + 1) * 128], idt[:])
                    ctx.inc(ins, s_tp, 1)
                    kt += 1
            ctx.wge(t, s_w, 16)
            for h in range(8):
                ctx.wge(t, s_te, 64 + h)
                ins = t.transpose(ppt[0:64, kt % 2, 0:1], ctx.sb["nkvrow"][0:1, h * 64:(h + 1) * 64], idt[0:1, 0:1])
                ctx.inc(ins, s_tp, 1)
                kt += 1
            ctx.wge(t, s_g, 1)
            kt2 = 0
            for h in range(8):
                for nh in range(2):
                    k = h * 2 + nh
                    if k >= 2:
                        ctx.wge(t, s_ke, k - 1)
                        ctx.wge(t, s_ve, k - 1)
                    for kc in range(NDC):
                        ins = t.matmul(pp[:, k % 2, :], kvTf[:, kc * 1024 + h * 128:kc * 1024 + (h + 1) * 128],
                                       xcf[:, kc, nh * 512:(nh + 1) * 512], start=(kc == 0), stop=(kc == NDC - 1))
                    ctx.inc(ins, s_mm, 1)
                ctx.wge(t, s_ve, (h + 1) * 2)
                for c in range(NDC):
                    if kt2 >= 2:
                        ctx.wge(t, s_vte, kt2 - 1)
                    ins = t.transpose(ppt[:, kt2 % 2, 0:64], pb1[0:64, 2048 + c * 128:2048 + (c + 1) * 128], idt[0:64, 0:64])
                    ctx.inc(ins, s_vt, 1)
                    kt2 += 1
        @blk.vector
        def _(v):
            ctx.wait_sb(v)
            kt = 0
            for h in range(8):
                for c in range(NDC):
                    ctx.wge(v, s_tp, kt + 1)
                    ins = v.tensor_copy(kvTf[:, c * 1024 + h * 128:c * 1024 + (h + 1) * 128], ppt[:, kt % 2, 0:128])
                    ctx.inc(ins, s_te, 1)
                    kt += 1
            for c in range(NDC):
                ins = v.tensor_scalar(out=kvTf[:, c * 1024:(c + 1) * 1024], in0=kvTf[:, c * 1024:(c + 1) * 1024],
                                      scalar1=st0[:, 32 + c:33 + c], scalar2=None, op0=ALU.mult)
            ctx.inc(ins, s_g, 1)
            for h in range(8):
                ctx.wge(v, s_tp, 64 + h + 1)
                ins = v.tensor_copy(kfull[:, h, 0:1], ppt[0:64, (64 + h) % 2, 0:1])
                ctx.inc(ins, s_te, 1)
            def vte_evacs(hh):
                for c in range(NDC):
                    kt2 = hh * NDC + c
                    ctx.wge(v, s_vt, kt2 + 1)
                    ins = v.tensor_copy(vT[:, hh * 512:(hh + 1) * 512].rearrange("p (c8 j) -> p c8 j", c8=8)[:, c, :],
                                        ppt[0:128, kt2 % 2, 0:64])
                    ctx.inc(ins, s_vte, 1)
            for h in range(8):
                if h >= 1:
                    vte_evacs(h - 1)
                ctx.wge(v, s_ke, h * 2 + 2)
                kk = kfull[:, h, 1:1025]
                t1 = ctx.sb["pf1"][0:64, 0:1024]
                rq = ctx.sb["pf1"][0:64, 1024:2048]
                v.tensor_tensor(out=t1, in0=kk, in1=tabs[:, 2, :], op=ALU.mult)
                v.tensor_scalar(out=rq[0:32, :], in0=kfull[32:64, h, 1:1025], scalar1=cst[0:32, C_NEG1:C_NEG1 + 1],
                                scalar2=None, op0=ALU.mult)
                v.tensor_copy(rq[32:64, :], kfull[0:32, h, 1:1025])
                v.tensor_tensor(out=rq, in0=rq, in1=tabs[:, 3, :], op=ALU.mult)
                ins = v.tensor_tensor(out=kk, in0=t1, in1=rq, op=ALU.add)
                ctx.inc(ins, s_rp, 1)
            vte_evacs(7)
        @blk.scalar
        def _(s):
            ctx.wait_sb(s)
            for h in range(8):
                for nh in range(2):
                    k = h * 2 + nh
                    ctx.wge(s, s_mm, k + 1)
                    if h >= 1:
                        ctx.wge(s, s_vt, h * 8)
                    ins = s.activation(out=kfull[0:64, h, 1 + nh * 512:1 + (nh + 1) * 512],
                                       in_=pp[0:64, k % 2, :], func=AF.Copy)
                    ctx.inc(ins, s_ke, 1)
                    ins = s.activation(out=pb1[0:64, 2048 + nh * 512:2048 + (nh + 1) * 512].bitcast(BF16)
                                       if False else pb1[0:64, 2048 + nh * 512:2048 + (nh + 1) * 512],
                                       in_=pp[64:128, k % 2, :], func=AF.Copy)
                    ctx.inc(ins, s_ve, 1)
        @blk.sync
        def _(sy):
            ctx.wait_sb(sy)
            ctx.wge(sy, s_rp, 8)
            ctx.wge(sy, s_vte, 64)
            ctx.bar_inc(sy)
    ctx.stage_done()


def emit_heavy5(ctx):
    nc = ctx.nc
    pb0f = ctx.sb["pb0"][:].rearrange("p a b -> p (a b)")
    q_sb = pb0f[0:64, 0:8192].rearrange("p (h n) -> p h n", h=8)
    kfull = pb0f[0:64, 8192:16392].rearrange("p (h n) -> p h n", h=8)
    vT = pb0f[:, 24576:28672]
    yscb = ctx.sb["ysc"][:].rearrange("p a b -> p (a b)").bitcast(BF16)
    o2 = yscb[:, 8192:16384].rearrange("p (c n) -> p c n", c=4)
    bcsf = ctx.sb["bcs"][:].rearrange("p a b -> p (a b)")
    at_sb = bcsf[:, 0:1024].rearrange("p (c j) -> p c j", c=8)
    a0 = bcsf[0:1, 1024:1152]
    ab = ctx.sb["pb1"][:].rearrange("p a b -> p (a b)")
    simf = ctx.sb["pf0"][:, 1, 0:1032]
    nkvrow = ctx.sb["nkvrow"]
    st0 = ctx.sb["st0"]
    cst = ctx.sb["cst"]
    pp = ctx.sb["pp0"]
    pp1 = ctx.sb["pp1"]
    ppt = ctx.sb["ppt"]
    idt = ctx.sb["idt"]
    s_mm, s_sf, s_ex, s_ab, s_tp, s_te, s_av, s_oe = [f"h5{x}" for x in "msxbtuao"]
    with nc.Block("heavy5") as blk:
        @blk.tensor
        def _(t):
            ctx.wait_sb(t)
            ktp = 0
            for qt in range(8):
                for h in range(8):
                    k = qt * 8 + h
                    if k >= 1:
                        ctx.wge(t, s_sf, k)
                    for c in range(2):
                        ins = t.matmul(pp[:, c, :], q_sb[:, h, qt * 128:(qt + 1) * 128],
                                       kfull[:, h, 1 + c * 512:1 + (c + 1) * 512], start=True, stop=True)
                    ins = t.matmul(pp[:, 2, 0:1], q_sb[:, h, qt * 128:(qt + 1) * 128],
                                   kfull[:, h, 0:1], start=True, stop=True)
                    ctx.inc(ins, s_mm, 1)
                    ctx.wge(t, s_ab, k + 1)
                    for c in range(9):
                        if ktp >= 2:
                            ctx.wge(t, s_te, ktp - 1)
                        if c < 8:
                            ins = t.transpose(ppt[:, ktp % 2, 0:128],
                                              ab[:, (k % 2) * 1032 + c * 128:(k % 2) * 1032 + (c + 1) * 128], idt[:])
                        else:
                            ins = t.transpose(ppt[0:1, ktp % 2, 0:128],
                                              ab[:, (k % 2) * 1032 + 1024:(k % 2) * 1032 + 1025], idt[:])
                        ctx.inc(ins, s_tp, 1)
                        ktp += 1
                    ctx.wge(t, s_te, ktp)
                    if k >= 2:
                        ctx.wge(t, s_oe, k - 1)
                    for c in range(8):
                        t.matmul(pp1[0:64, k % 2, 0:128],
                                 vT[:, h * 512:(h + 1) * 512].rearrange("p (c8 j) -> p c8 j", c8=8)[:, c, :],
                                 at_sb[:, c, :], start=(c == 0), stop=False)
                    ins = t.matmul(pp1[0:64, k % 2, 0:128], nkvrow[0:1, 512 + h * 64:512 + (h + 1) * 64],
                                   a0[:], start=False, stop=True)
                    ctx.inc(ins, s_av, 1)
        @blk.vector
        def _(v):
            ctx.wait_sb(v)
            for qt in range(8):
                for h in range(8):
                    k = qt * 8 + h
                    ctx.wge(v, s_mm, k + 1)
                    v.tensor_scalar(out=simf[:, 0:1], in0=pp[:, 2, 0:1], scalar1=cst[:, C_QSC:C_QSC + 1],
                                    scalar2=None, op0=ALU.mult)
                    v.tensor_scalar(out=simf[:, 1:513], in0=pp[:, 0, :], scalar1=cst[:, C_QSC:C_QSC + 1],
                                    scalar2=None, op0=ALU.mult)
                    ins = v.tensor_scalar(out=simf[:, 513:1025], in0=pp[:, 1, :], scalar1=cst[:, C_QSC:C_QSC + 1],
                                          scalar2=None, op0=ALU.mult)
                    ctx.inc(ins, s_sf, 1)
                    v.reduce_max(st0[:, 0:1], simf[:, 0:1025], axis=AX.X, negate=True)
                    ctx.wge(v, s_ex, k + 1)
                    v.drain()
                    v.reciprocal(st0[:, 2:3], st0[:, 1:2])
                    v.drain()
                    ins = v.tensor_scalar(out=ab[:, (k % 2) * 1032:(k % 2) * 1032 + 1025],
                                          in0=simf[:, 1032:2057] if False else ctx.sb["pf0"][:, 0, 0:1025],
                                          scalar1=st0[:, 2:3], scalar2=None, op0=ALU.mult)
                    ctx.inc(ins, s_ab, 1)
                    for c in range(9):
                        kt9 = k * 9 + c
                        ctx.wge(v, s_tp, kt9 + 1)
                        if c < 8:
                            ins = v.tensor_copy(at_sb[:, c, :], ppt[:, kt9 % 2, 0:128])
                        else:
                            ins = v.tensor_copy(a0[:], ppt[0:1, kt9 % 2, 0:128])
                        ctx.inc(ins, s_te, 1)
        @blk.scalar
        def _(s):
            ctx.wait_sb(s)
            for qt in range(8):
                for h in range(8):
                    k = qt * 8 + h
                    ctx.wge(s, s_sf, k + 1)
                    ins = s.activation(out=ctx.sb["pf0"][:, 0, 0:1025], in_=simf[:, 0:1025], func=AF.Exp,
                                       bias=st0[:, 0:1], scale=1.0, accum_out=st0[:, 1:2])
                    ctx.inc(ins, s_ex, 1)
                    ctx.wge(s, s_av, k + 1)
                    if k >= 2:
                        ctx.wge(s, s_oe, k - 1)
                    ins = s.activation(out=o2[64 * (h % 2):64 * (h % 2) + 64, h // 2, qt * 128:(qt + 1) * 128],
                                       in_=pp1[0:64, k % 2, 0:128], func=AF.Copy)
                    ctx.inc(ins, s_oe, 1)
        @blk.sync
        def _(sy):
            ctx.wait_sb(sy)
            ctx.wge(sy, s_oe, 64)
            ctx.bar_inc(sy)
    ctx.stage_done()


def emit_heavy6(ctx):
    nc = ctx.nc
    yscb = ctx.sb["ysc"][:].rearrange("p a b -> p (a b)").bitcast(BF16)
    o2 = yscb[:, 8192:16384].rearrange("p (c n) -> p c n", c=4)
    how = ctx.carve(8192, [128, 4, 1024], BF16)
    pf1 = ctx.sb["pf1"]
    pp1 = ctx.sb["pp1"]
    s_w, s_mm, s_ev, s_pn, s_ps, s_sc = [f"h6{x}" for x in "wmenpc"]
    with nc.Block("heavy6") as blk:
        @blk.gpsimd
        def _(g):
            ctx.wait_sb(g)
            ctx.inc(g.dma_start(out=how[:], in_=ctx.aps["L1_attn_heavy_out_w"].rearrange("(c p) e -> p c e", p=128)), s_w, 16)
            ctx.inc(g.dma_start(out=ctx.sb["nqrow"][0:1, :],
                                in_=ctx.aps["L1_attn_null_q_token"].rearrange("(one n) -> one n", one=1)), s_w, 16)
            ctx.inc(g.dma_start(out=ctx.sb["sti2"][:, 0:8],
                                in_=ctx.aps["selq"].rearrange("(gg p) one -> p (gg one)", p=128),
                                allow_slow_non_contiguous=True), s_w, 16)
            ctx.wge(g, s_pn, 1)
            for ttp in range(32):
                ins = g.dma_start(out=ctx.aps["hscat"][ttp * 128:(ttp + 1) * 128, :], in_=pf1[:, 2048:3072])
                ctx.inc(ins, s_ps, 16)
            ctx.wge(g, s_ps, 32 * 16)
            for qt in range(8):
                ctx.wge(g, s_ev, qt + 1)
                ins = g.indirect_dma_start(
                    out=ctx.aps["hscat"], out_offset=IndirectOffsetOnAxis(ctx.sb["sti2"][:, qt:qt + 1], 0),
                    in_=pf1[:, (qt % 2) * 1024:(qt % 2) * 1024 + 1024], in_offset=None, bounds_check=4095)
                ctx.inc(ins, s_sc, 16)
            ctx.wge(g, s_sc, 8 * 16)
            ctx.bar_inc(g)
        @blk.tensor
        def _(t):
            ctx.wait_sb(t)
            ctx.wge(t, s_w, 32)
            for nh in range(2):
                ins = t.matmul(pp1[:, nh, :], ctx.sb["ones1f"][0:1, 0:128],
                               ctx.sb["nqrow"][0:1, nh * 512:(nh + 1) * 512], start=True, stop=True)
            ctx.inc(ins, s_mm, 1)
            for qt in range(8):
                if qt == 0:
                    ctx.wge(t, s_pn, 1)
                else:
                    ctx.wge(t, s_ev, qt)
                for nh in range(2):
                    for c in range(4):
                        ins = t.matmul(pp1[:, nh, :], o2[:, c, qt * 128:(qt + 1) * 128],
                                       how[:, c, nh * 512:(nh + 1) * 512], start=(c == 0), stop=(c == 3))
                ctx.inc(ins, s_mm, 1)
        @blk.vector
        def _(v):
            ctx.wait_sb(v)
            ctx.wge(v, s_mm, 1)
            ins = v.tensor_copy(pf1[:, 2048:3072], pp1[:, 0:2, :].rearrange("p a b -> p (a b)"))
            ctx.inc(ins, s_pn, 1)
            for qt in range(8):
                ctx.wge(v, s_mm, qt + 2)
                if qt >= 2:
                    ctx.wge(v, s_sc, (qt - 1) * 16)
                ins = v.tensor_copy(pf1[:, (qt % 2) * 1024:(qt % 2) * 1024 + 1024],
                                    pp1[:, 0:2, :].rearrange("p a b -> p (a b)"))
                ctx.inc(ins, s_ev, 1)
    ctx.stage_done()


def emit_moe(ctx):
    nc = ctx.nc
    xch = ctx.sb["pf0"]
    rwt = ctx.carve(0, [128, NDC, 8], F32)
    eb = ctx.carve(256, [1, 8], F32)
    act = ctx.sb["pf1"]
    st0 = ctx.sb["st0"]
    cst = ctx.sb["cst"]
    stb = ctx.sb["stb"]
    cfst = ctx.sb["bcs"]
    pp = ctx.sb["pp0"]
    ppt = ctx.sb["ppt"]
    idt = ctx.sb["idt"]
    s_w, s_ld, s_mm, s_ac, s_cf, s_tp, s_te, s_st = [f"mo{x}" for x in "wlmactuq"]
    with nc.Block("moe") as blk:
        @blk.gpsimd
        def _(g):
            ctx.wait_sb(g)
            ctx.inc(g.dma_start(out=rwt[:], in_=ctx.aps["rWTf"]), s_w, 16)
            ctx.inc(g.dma_start(out=eb[:], in_=ctx.aps["L1_ffn_expert_bias"].rearrange("(one e) -> one e", one=1)), s_w, 16)
            for ch in range(10):
                if 2 <= ch < 10 and ch < 8:
                    ctx.wge(g, s_mm, (ch - 1) * 4)
                if ch < 8:
                    ctx.inc(g.dma_start(out=xch[:, ch % 2, :].rearrange("p (c t) -> p c t", c=NDC),
                                        in_=ctx.aps["xn1f32"][:, :, ch * 512:(ch + 1) * 512]), s_ld, 16)
                if ch >= 2:
                    for tt in range((ch - 2) * 4, (ch - 1) * 4):
                        ctx.wge(g, s_te, tt + 1)
                        ins = g.dma_start(out=ctx.aps["cfrow"][:, tt * 128:(tt + 1) * 128],
                                          in_=cfst[0:8, tt % 2, 0:128])
                        ctx.inc(ins, s_st, 16)
            ctx.wge(g, s_st, 32 * 16)
            ctx.bar_inc(g)
        @blk.tensor
        def _(t):
            ctx.wait_sb(t)
            ctx.wge(t, s_w, 32)
            for ch in range(8):
                ctx.wge(t, s_ld, (ch + 1) * 16)
                for sub in range(4):
                    tt = ch * 4 + sub
                    if tt >= 4:
                        ctx.wge(t, s_ac, tt - 3)
                    for kc in range(NDC):
                        t.matmul(pp[:, tt % 4, 0:8],
                                 xch[:, ch % 2, :].rearrange("p (c t) -> p c t", c=NDC)[:, kc, sub * 128:(sub + 1) * 128],
                                 rwt[:, kc, :], start=(kc == 0), stop=False)
                    ins = t.matmul(pp[:, tt % 4, 0:8], ctx.sb["ones1f"][0:1, 0:128], eb[:], start=False, stop=True)
                    ctx.inc(ins, s_mm, 1)
                    ctx.wge(t, s_cf, tt + 1)
                    if tt >= 2:
                        ctx.wge(t, s_te, tt - 1)
                    ins = t.transpose(ppt[0:8, tt % 2, 0:128], stb[:, 48 + (tt % 2) * 8:48 + (tt % 2) * 8 + 8], idt[:])
                    ctx.inc(ins, s_tp, 1)
        @blk.scalar
        def _(s):
            ctx.wait_sb(s)
            for tt in range(32):
                ctx.wge(s, s_mm, tt + 1)
                if tt >= 4:
                    ctx.wge(s, s_cf, tt - 3)
                a = act[:, (tt % 4) * 8:(tt % 4) * 8 + 8]
                s.activation(out=a, in_=pp[:, tt % 4, 0:8], func=AF.Exp)
                s.drain()
                s.activation(out=a, in_=a, func=AF.Ln, bias=1.0)
                s.drain()
                ins = s.activation(out=a, in_=a, func=AF.Sqrt)
                ctx.inc(ins, s_ac, 1)
        @blk.vector
        def _(v):
            ctx.wait_sb(v)
            for tt in range(32):
                ctx.wge(v, s_ac, tt + 1)
                if tt >= 2:
                    ctx.wge(v, s_tp, tt - 1)
                a = act[:, (tt % 4) * 8:(tt % 4) * 8 + 8]
                for e in range(8):
                    v.tensor_scalar(out=act[:, 48 + 8 * e:56 + 8 * e], in0=a, scalar1=a[:, e:e + 1],
                                    scalar2=None, op0=ALU.is_gt)
                for e in range(8):
                    v.reduce_sum(st0[:, 8 + e:9 + e], act[:, 48 + 8 * e:56 + 8 * e], axis=AX.X)
                for e in range(1, 8):
                    v.tensor_scalar(out=act[:, 112 + 8 * e:112 + 8 * e + e], in0=a[:, 0:e], scalar1=a[:, e:e + 1],
                                    scalar2=None, op0=ALU.is_equal)
                for e in range(1, 8):
                    v.reduce_sum(st0[:, 16 + e:17 + e], act[:, 112 + 8 * e:112 + 8 * e + e], axis=AX.X)
                v.memset(st0[:, 16:17], 0.0)
                for e in range(8):
                    v.tensor_tensor(out=st0[:, 8 + e:9 + e], in0=st0[:, 8 + e:9 + e], in1=st0[:, 16 + e:17 + e], op=ALU.add)
                for e in range(8):
                    v.tensor_single_scalar(out=st0[:, 24 + e:25 + e], in_=st0[:, 8 + e:9 + e], scalar=2.0, op=ALU.is_lt)
                for e in range(8):
                    ins = v.tensor_tensor(out=stb[:, 48 + (tt % 2) * 8 + e:48 + (tt % 2) * 8 + e + 1],
                                          in0=a[:, e:e + 1], in1=st0[:, 24 + e:25 + e], op=ALU.mult)
                ctx.inc(ins, s_cf, 1)
                ctx.wge(v, s_tp, tt + 1)
                if tt >= 2:
                    ctx.wge(v, s_st, (tt - 1) * 16)
                ins = v.tensor_copy(cfst[0:8, tt % 2, 0:128], ppt[0:8, tt % 2, 0:128])
                ctx.inc(ins, s_te, 1)
    ctx.stage_done()


def emit_moeh(ctx, eidx):
    nc = ctx.nc
    tag = f"mh{eidx + 1}"
    w1 = ctx.carve(0, [128, NDC, 512], BF16)
    b1 = ctx.carve(8192, [128, 4], F32)
    xch = ctx.sb["pb1"]
    hb = ctx.sb["bcs"][:].rearrange("p a b -> p (a b)")
    pp = ctx.sb["pp0"]
    pp1 = ctx.sb["pp1"]
    base = 0 if eidx < 0 else 4 + 4 * eidx
    s_w, s_ld, s_mm, s_cb, s_hv, s_sc, s_st = [f"{tag}{x}" for x in "wlmbhcq"]
    w1name = "L1_ffn_sW1" if eidx < 0 else "L1_ffn_rW1"
    b1name = "L1_ffn_sb1" if eidx < 0 else "L1_ffn_rb1"
    with nc.Block("moeh" + tag) as blk:
        @blk.gpsimd
        def _(g):
            ctx.wait_sb(g)
            wsrc = ctx.aps[w1name] if eidx < 0 else ctx.aps[w1name][eidx]
            ctx.inc(g.dma_start(out=w1[:], in_=wsrc.rearrange("(c p) h -> p c h", p=128)), s_w, 16)
            bsrc = ctx.aps[b1name] if eidx < 0 else ctx.aps[b1name][eidx]
            ctx.inc(g.dma_start(out=b1[:], in_=bsrc.rearrange("(c p) -> p c", p=128),
                                allow_slow_non_contiguous=True), s_w, 16)
            for ch in range(10):
                if ch < 8:
                    if ch >= 2:
                        ctx.wge(g, s_mm, (ch - 1) * 4)
                    ctx.inc(g.dma_start(out=xch[:, ch % 2, :].rearrange("p (c t) -> p c t", c=NDC),
                                        in_=ctx.aps["xn1f"][:, :, ch * 512:(ch + 1) * 512]), s_ld, 16)
                    if eidx >= 0:
                        ctx.inc(g.dma_start(out=ctx.sb["nkvrow"][0:1, (ch % 2) * 512:(ch % 2) * 512 + 512],
                                            in_=ctx.aps["cfrow"][eidx:eidx + 1, ch * 512:(ch + 1) * 512]), s_ld, 16)
                if ch >= 2:
                    for k in range((ch - 2) * 4, (ch - 1) * 4):
                        ch2, ht = divmod(k, 4)
                        ctx.wge(g, s_hv, k + 1)
                        ins = g.dma_start(out=ctx.aps["shsc"][:, base + ht, ch2 * 512:(ch2 + 1) * 512],
                                          in_=hb[:, (k % 4) * 512:(k % 4 + 1) * 512])
                        ctx.inc(ins, s_st, 16)
            ctx.wge(g, s_st, 32 * 16)
            ctx.bar_inc(g)
        @blk.tensor
        def _(t):
            ctx.wait_sb(t)
            ctx.wge(t, s_w, 32)
            nld = 32 if eidx >= 0 else 16
            for ch in range(8):
                ctx.wge(t, s_ld, (ch + 1) * nld)
                if eidx >= 0:
                    if ch >= 2:
                        ctx.wge(t, s_cb, ch - 1)
                    ins = t.matmul(pp1[:, ch % 2, :], ctx.sb["ones1"][0:1, 0:128],
                                   ctx.sb["nkvrow"][0:1, (ch % 2) * 512:(ch % 2) * 512 + 512], start=True, stop=True)
                    ctx.inc(ins, s_cb + "m", 1)
                for ht in range(4):
                    k = ch * 4 + ht
                    if k >= 4:
                        ctx.wge(t, s_hv, k - 3)
                    for kc in range(NDC):
                        ins = t.matmul(pp[:, k % 4, :], w1[:, kc, ht * 128:(ht + 1) * 128],
                                       xch[:, ch % 2, :].rearrange("p (c t) -> p c t", c=NDC)[:, kc, :],
                                       start=(kc == 0), stop=(kc == NDC - 1))
                    ctx.inc(ins, s_mm, 1)
        @blk.scalar
        def _(s):
            ctx.wait_sb(s)
            for k in range(32):
                ch, ht = divmod(k, 4)
                ctx.wge(s, s_mm, k + 1)
                if k >= 4:
                    ctx.wge(s, s_hv, k - 3)
                    ctx.wge(s, s_st, (k - 3) * 16)
                ins = s.activation(out=hb[:, (k % 4) * 512:(k % 4 + 1) * 512], in_=pp[:, k % 4, :],
                                   func=AF.Silu, bias=b1[:, ht:ht + 1], scale=1.0)
                ctx.inc(ins, s_sc, 1)
        @blk.vector
        def _(v):
            ctx.wait_sb(v)
            for k in range(32):
                ch, ht = divmod(k, 4)
                ctx.wge(v, s_sc, k + 1)
                if k >= 4:
                    ctx.wge(v, s_st, (k - 3) * 16)
                if eidx >= 0:
                    ctx.wge(v, s_cb + "m", ch + 1)
                    ins = v.tensor_tensor(out=hb[:, (k % 4) * 512:(k % 4 + 1) * 512],
                                          in0=hb[:, (k % 4) * 512:(k % 4 + 1) * 512],
                                          in1=pp1[:, ch % 2, :], op=ALU.mult)
                else:
                    ins = v.tensor_copy(hb[:, (k % 4) * 512:(k % 4 + 1) * 512],
                                        hb[:, (k % 4) * 512:(k % 4 + 1) * 512])
                ctx.inc(ins, s_hv, 1)
                if eidx >= 0 and ht == 3:
                    ctx.inc(v.engine_nop(), s_cb, 1)
    ctx.stage_done()


def emit_final(ctx):
    nc = ctx.nc
    w2e = ctx.sb["pb0"][:].rearrange("p a b -> p (a b)").rearrange("p (c e) -> p c e", c=32)
    w2s = ctx.sb["bcs"][:].rearrange("p a b -> p (a b)").rearrange("p (c e) -> p c e", c=8)[:, 0:4, :]
    pb1f = ctx.sb["pb1"][:].rearrange("p a b -> p (a b)")
    yscb2 = ctx.sb["ysc"][:].rearrange("p a b -> p (a b)").bitcast(BF16)

    def r2slot(tt):
        base = pb1f if tt % 2 == 0 else yscb2
        return base[:, 0:4608].rearrange("p (c t) -> p c t", c=36)
    cfs = ctx.carve(0, [8, 4096], BF16)
    rb2 = ctx.carve(8192, [8, 1024], BF16)
    sb2r = ctx.carve(10240, [1, 1024], BF16)
    xres = ctx.sb["pf0"]
    ev = ctx.sb["pf1"]
    pp1 = ctx.sb["pp1"]
    s_w, s_ld, s_mm, s_ev, s_st = [f"fn{x}" for x in "wlmeq"]
    with nc.Block("final") as blk:
        @blk.gpsimd
        def _(g):
            ctx.wait_sb(g)
            ctx.inc(g.dma_start(out=w2e[:, 0:32, :].rearrange("p c e -> p (c e)").rearrange("p (e2 c4 d) -> p e2 c4 d", e2=8, c4=4),
                                in_=ctx.aps["L1_ffn_rW2"].rearrange("e (c p) d -> p e c d", p=128)), s_w, 16)
            ctx.inc(g.dma_start(out=w2s[:],
                                in_=ctx.aps["L1_ffn_sW2"].rearrange("(c p) d -> p c d", p=128)), s_w, 16)
            ctx.inc(g.dma_start(out=cfs[:], in_=ctx.aps["cfrow"]), s_w, 16)
            ctx.inc(g.dma_start(out=rb2[:], in_=ctx.aps["L1_ffn_rb2"]), s_w, 16)
            ctx.inc(g.dma_start(out=sb2r[:], in_=ctx.aps["L1_ffn_sb2"].rearrange("(one d) -> one d", one=1)), s_w, 16)
            for tt in range(32):
                t0 = tt * 128
                if tt >= 2:
                    ctx.wge(g, s_mm, tt - 1)
                ctx.inc(g.dma_start(out=r2slot(tt), in_=ctx.aps["shsc"][:, :, t0:t0 + 128]), s_ld, 16)
                if tt >= 2:
                    ctx.wge(g, s_ev, tt - 1)
                ctx.inc(g.dma_start(out=xres[:, tt % 2, 0:1024], in_=ctx.aps["x1"][t0:t0 + 128, :]), s_ld, 16)
                ctx.inc(g.dma_start(out=xres[:, tt % 2, 1024:2048], in_=ctx.aps["lighto"][t0:t0 + 128, :]), s_ld, 16)
                ctx.inc(g.dma_start(out=xres[:, tt % 2, 2048:3072], in_=ctx.aps["hscat"][t0:t0 + 128, :]), s_ld, 16)
        @blk.tensor
        def _(t):
            ctx.wait_sb(t)
            ctx.wge(t, s_w, 80)
            for tt in range(32):
                ctx.wge(t, s_ld, (tt + 1) * 64)
                if tt >= 1:
                    ctx.wge(t, s_ev, tt)
                for nh in range(2):
                    for c in range(36):
                        lhs = r2slot(tt)[:, c, :]
                        rhs = w2s[:, c, nh * 512:(nh + 1) * 512] if c < 4 else w2e[:, c - 4, nh * 512:(nh + 1) * 512]
                        t.matmul(pp1[:, nh, :], lhs, rhs, start=(c == 0), stop=False)
                    t.matmul(pp1[:, nh, :], cfs[:, tt * 128:(tt + 1) * 128],
                             rb2[:, nh * 512:(nh + 1) * 512], start=False, stop=False)
                    ins = t.matmul(pp1[:, nh, :], ctx.sb["ones1"][0:1, 0:128],
                                   sb2r[0:1, nh * 512:(nh + 1) * 512], start=False, stop=True)
                ctx.inc(ins, s_mm, 1)
        @blk.vector
        def _(v):
            ctx.wait_sb(v)
            for tt in range(32):
                ctx.wge(v, s_mm, tt + 1)
                if tt >= 2:
                    ctx.wge(v, s_st, (tt - 1) * 16)
                v.tensor_add(ev[:, (tt % 2) * 1024:(tt % 2) * 1024 + 1024],
                             pp1[:, 0:2, :].rearrange("p a b -> p (a b)"), xres[:, tt % 2, 0:1024])
                v.tensor_add(ev[:, (tt % 2) * 1024:(tt % 2) * 1024 + 1024],
                             ev[:, (tt % 2) * 1024:(tt % 2) * 1024 + 1024], xres[:, tt % 2, 1024:2048])
                ins = v.tensor_add(ev[:, (tt % 2) * 1024:(tt % 2) * 1024 + 1024],
                                   ev[:, (tt % 2) * 1024:(tt % 2) * 1024 + 1024], xres[:, tt % 2, 2048:3072])
                ctx.inc(ins, s_ev, 1)
        @blk.sync
        def _(sy):
            ctx.wait_sb(sy)
            for tt in range(32):
                ctx.wge(sy, s_ev, tt + 1)
                ins = sy.dma_start(out=ctx.aps["out"][tt * 128:(tt + 1) * 128, :],
                                   in_=ev[:, (tt % 2) * 1024:(tt % 2) * 1024 + 1024])
                ctx.inc(ins, s_st, 16)
            ctx.wge(sy, s_st, 32 * 16)
            ctx.bar_inc(sy)
    ctx.stage_done()


L1_STAGES = [
    ST("ln1", L(emit_ln, "l1", "x1", dst_f32="xn1f32", dst_tm="xn1tm"), ["x1"], ["xn1f32", "xn1tm"]),
    ST("cst1", L(emit_cast_f32_bf16, "c1", "xn1f32", "xn1f", NDC * T), ["xn1f32"], ["xn1f"]),
    ST("lnl", L(emit_ln, "lg", "xn1tm", dst_bf="xnlf"), ["xn1tm"], ["xnlf"]),
    ST("tprw", L(emit_transpose_mat, "rw", "L1_ffn_routing_W", 8, 1024, "rWTf", F32), ["L1_ffn_routing_W"], ["rWTf"]),
    ST("lqkv", L(emit_lqkv), ["L1_attn_light_qkv_w", "L1_attn_ln_gamma", "L1_attn_ln_beta", "xnlf"],
       ["qkvsc", "vtm", "bwrow"]),
    ST("latt", L(emit_lattn), ["qkvsc", "vtm", "lmask", "L1_attn_light_out_w"], ["lighto"]),
    ST("rout", L(emit_route), ["xn1f32", "L1_attn_q_route_tok", "L1_attn_kv_route_tok"],
       ["selq", "selkv", "scorerow", "rankrow"]),
    ST("hv1", L(emit_heavy1), ["selq", "selkv", "invf"], []),
    ST("hv2q", L(emit_heavy2, 0), ["selq", "xn1tm"], []),
    ST("hv3", L(emit_heavy3), ["L1_attn_heavy_q_w", "L1_attn_heavy_gamma"], []),
    ST("hv2k", L(emit_heavy2, 1), ["selkv", "xn1tm"], []),
    ST("hv4", L(emit_heavy4), ["L1_attn_heavy_kv_w", "L1_attn_heavy_null_kv", "L1_attn_heavy_gamma"], []),
    ST("hv5", L(emit_heavy5), [], []),
    ST("hv6", L(emit_heavy6), ["L1_attn_heavy_out_w", "L1_attn_null_q_token", "selq"], ["hscat"]),
    ST("moe", L(emit_moe), ["xn1f32", "rWTf", "L1_ffn_expert_bias"], ["cfrow"]),
    ST("mhs", L(emit_moeh, -1), ["xn1f", "L1_ffn_sW1", "L1_ffn_sb1"], ["shsc"]),
    *[ST(f"mh{e}", L(emit_moeh, e), ["xn1f", "L1_ffn_rW1", "L1_ffn_rb1", "cfrow"], ["shsc"]) for e in range(8)],
    ST("fin", L(emit_final), ["shsc", "cfrow", "L1_ffn_rW2", "L1_ffn_sW2", "L1_ffn_rb2", "L1_ffn_sb2",
                              "x1", "lighto", "hscat"], ["out"]),
]


L1A = L1_STAGES[0:6]
L1B = L1_STAGES[6:14]
L1C = L1_STAGES[14:]
GROUPS = [PREP_STAGES + L0_STAGES + L1_STAGES]


def _host_consts():
    masks = np.zeros((3, 128, 384), np.float32)
    for kind in range(3):
        for j in range(128):
            for blk in range(3):
                pass
    tq = np.arange(128)
    tk = np.arange(384) - 128
    for kind in range(3):
        m = np.zeros((128, 384), np.float32)
        for ci, base in enumerate((0,)):
            pass
        tt_q = tq[:, None]
        tt_k = tk[None, :]
        w = tq // 64
        lo = (w[:, None] - 1) * 64
        hi = (w[:, None] + 2) * 64
        valid = (tt_k >= lo) & (tt_k < hi) & (np.abs(tt_q - tt_k) <= 64)
        if kind == 0:
            valid &= (tt_k >= 0)
        if kind == 2:
            valid &= (tt_k < 128 + 256 - 256)
        m[:] = np.where(valid, 0.0, np.float32(-3.4028235e38))
        masks[kind] = m
    invf = (1.0 / 10000.0 ** (np.arange(0, 64, 2, dtype=np.float32) / 64)).reshape(1, 32).astype(np.float32)
    return {"lmask": masks, "invf": invf}
